# version 03
# baseline (speedup 1.0000x reference)
import numpy as np
from contextlib import ExitStack

import concourse.bass as bass
import concourse.mybir as mybir
from concourse.bass_utils import run_bass_kernel_spmd

F32 = mybir.dt.float32
F32R = mybir.dt.float32r
BF16 = mybir.dt.bfloat16
AF = mybir.ActivationFunctionType
OP = mybir.AluOpType
AX = mybir.AxisListType

L, D, D2, E4, NST, DTR, NE, NL = 4096, 1024, 2048, 4096, 16, 64, 8, 5
T1, T2, TF, NQ = 512, 256, 512, 4
NC1, NC2, LQ = L // T1, L // T2, L // NQ


class Ctr:
    def __init__(self, sem, name):
        self.sem, self.name, self.n = sem, name, 0


class Prog:
    ENGINES = ("sync", "scalar", "vector", "tensor", "gpsimd")

    def __init__(self):
        self.ops = []

    def emit(self, engine, fn, waits=(), inc=None, inc16=None):
        ws = [(c.sem, v) for (c, v) in waits if v > 0]
        incs = []
        if inc is not None:
            inc.n += 1
            incs.append((inc.sem, 1))
        if inc16 is not None:
            inc16.n += 16
            incs.append((inc16.sem, 16))
        self.ops.append((engine, fn, ws, incs))

    def build(self, block):
        by = {e: [] for e in self.ENGINES}
        for op in self.ops:
            by[op[0]].append(op)

        def mk(name):
            def body(eng):
                waited = {}
                for _, fn, ws, incs in by[name]:
                    for sem, val in ws:
                        k = id(sem)
                        if waited.get(k, -1) < val:
                            eng.wait_ge(sem, val)
                            waited[k] = val
                    instr = fn(eng)
                    for sem, kk in incs:
                        instr.then_inc(sem, kk)
            return body

        for name in self.ENGINES:
            if by[name]:
                getattr(block, name)(mk(name))


def build(depth=NL, do_ffn=True, dev_out=None, dbg_spill=False, skip_last_p2=False):
    nc = bass.Bass("TRN2", target_bir_lowering=False)
    x_in = nc.dram_tensor("x", [L, D], F32, kind="ExternalInput").ap()
    Wi_h = nc.dram_tensor("W_in", [NL, E4, D], F32, kind="ExternalInput").ap()
    cw_h = nc.dram_tensor("conv_w", [NL, D2, 4], F32, kind="ExternalInput").ap()
    cb_h = nc.dram_tensor("conv_b", [NL, D2], F32, kind="ExternalInput").ap()
    Wx_h = nc.dram_tensor("W_x", [NL, 96, D2], F32, kind="ExternalInput").ap()
    Wdt_h = nc.dram_tensor("W_dt", [NL, D2, DTR], F32, kind="ExternalInput").ap()
    bdt_h = nc.dram_tensor("b_dt", [NL, D2], F32, kind="ExternalInput").ap()
    Al_h = nc.dram_tensor("A_log", [NL, D2, NST], F32, kind="ExternalInput").ap()
    Dp_h = nc.dram_tensor("Dvec", [NL, D2], F32, kind="ExternalInput").ap()
    Wo_h = nc.dram_tensor("W_out", [NL, D, D2], F32, kind="ExternalInput").ap()
    rw_h = nc.dram_tensor("rw", [D, NE], F32, kind="ExternalInput").ap()
    eb_h = nc.dram_tensor("ebias", [1, NE], F32, kind="ExternalInput").ap()
    w1_h = nc.dram_tensor("w1", [NE, D, D], F32, kind="ExternalInput").ap()
    w2_h = nc.dram_tensor("w2", [NE, D, D], F32, kind="ExternalInput").ap()
    w3_h = nc.dram_tensor("w3", [NE, D, D], F32, kind="ExternalInput").ap()
    sw1_h = nc.dram_tensor("sw1", [D, D], F32, kind="ExternalInput").ap()
    sw2_h = nc.dram_tensor("sw2", [D, D], F32, kind="ExternalInput").ap()
    sw3_h = nc.dram_tensor("sw3", [D, D], F32, kind="ExternalInput").ap()
    id_h = nc.dram_tensor("ident", [128, 128], F32, kind="ExternalInput").ap()
    out_h = nc.dram_tensor("out", [L, D], F32, kind="ExternalOutput").ap()

    xk = "ExternalOutput" if dbg_spill else "Internal"
    xnT_h = nc.dram_tensor("xnT", [128, 8, L], F32, kind=xk).ap()
    xT0_h = nc.dram_tensor("xT0", [128, 8, L], F32, kind=xk).ap()
    xT1_h = nc.dram_tensor("xT1", [128, 8, L], F32, kind=xk).ap()
    spill_kind = "ExternalOutput" if dbg_spill else "Internal"
    xa_h = nc.dram_tensor("xa_s", [128, 16, L], BF16, kind=spill_kind).ap()
    zs_h = nc.dram_tensor("zs_s", [128, 16, L], F32, kind=spill_kind).ap()
    dbgd_h = (nc.dram_tensor("dbg_del", [128, 16, L], F32, kind="ExternalOutput").ap()
              if dbg_spill else None)
    dbgy_h = (nc.dram_tensor("dbg_yac", [128, 16, L], F32, kind="ExternalOutput").ap()
              if dbg_spill else None)
    dbgg_h = (nc.dram_tensor("dbg_yg", [128, 16, L], mybir.dt.bfloat16,
                             kind="ExternalOutput").ap()
              if dbg_spill else None)
    bcs_h = nc.dram_tensor("bcs", [32, L], F32).ap()
    weH_h = nc.dram_tensor("weH", [NE, L], F32, kind=spill_kind).ap()

    p = Prog()
    ctx = ExitStack()
    with ctx:
        def mkc(name):
            return Ctr(ctx.enter_context(nc.semaphore(name)), name)

        sP, sA, sV = mkc("sP"), mkc("sA"), mkc("sV")
        sG = mkc("sG")
        cId = mkc("cId")
        cLNx = [mkc("cLNx%d" % i) for i in range(2)]
        cLNs = [mkc("cLNs%d" % i) for i in range(2)]
        cW1 = [mkc("cW1_%d" % i) for i in range(2)]
        cC1 = mkc("cC1")
        cXC = [mkc("cXC%d" % i) for i in range(2)]
        cZS = [mkc("cZS%d" % i) for i in range(2)]
        cXA = [mkc("cXA%d" % i) for i in range(2)]
        cW2 = mkc("cW2")
        cC2 = mkc("cC2")
        cXAB = mkc("cXAB")
        cXR = [mkc("cXR%d" % i) for i in range(2)]
        cZL = mkc("cZL")
        cBW = mkc("cBW")
        cBC = [mkc("cBC%d" % i) for i in range(2)]
        cDS = [mkc("cDS%d" % i) for i in range(2)]
        cRW = mkc("cRW")
        cXQ = [mkc("cXQ%d" % i) for i in range(2)]
        cWE = mkc("cWE")
        cXN = mkc("cXN")
        cFW = [mkc("cFW%d" % i) for i in range(2)]
        cWB = [mkc("cWB%d" % i) for i in range(2)]
        cX5 = mkc("cX5")
        cX0 = [mkc("cX0_%d" % i) for i in range(2)]
        cOU = [mkc("cOU%d" % i) for i in range(2)]
        all_dma_ctrs = ([cId, cC1, cW2, cC2, cXAB, cZL, cBW, cRW, cWE, cXN, cX5]
                        + cLNx + cLNs + cW1 + cXC + cZS + cXA + cXR + cBC + cDS
                        + cXQ + cFW + cWB + cX0 + cOU)

        def sbt(c, name, shape, dt):
            return c.enter_context(nc.sbuf_tensor(name, shape, dt)).ap()

        def pst(c, name, shape, dt=F32):
            return c.enter_context(nc.psum_tensor(name, shape, dt)).ap()

        id32 = sbt(ctx, "id32", [128, 128], F32)
        id16 = sbt(ctx, "id16", [128, 128], BF16)
        idr = sbt(ctx, "idr", [128, 128], F32R)

        p.emit("sync", lambda e: e.dma_start(out=id32, in_=id_h), inc16=cId)
        p.emit("vector", lambda e: e.tensor_copy(id16, id32), waits=[(cId, 16)])
        p.emit("vector", lambda e: e.tensor_copy(idr, id32), inc=sV)

        fl = lambda ap: ap.rearrange("p j t -> p (j t)")

        with ExitStack() as lc:
            xt = [sbt(lc, "ln_x%d" % i, [128, D], F32) for i in range(2)]
            x2t = [sbt(lc, "ln_x2%d" % i, [128, D], F32) for i in range(2)]
            st = [sbt(lc, "ln_st%d" % i, [128, 8], F32) for i in range(2)]
            xnt = [sbt(lc, "ln_xn%d" % i, [128, D], F32R) for i in range(2)]
            stg = [sbt(lc, "ln_stg%d" % i, [128, D], F32) for i in range(2)]
            ps_ln = [pst(lc, "ln_ps%d" % i, [128, D], F32R) for i in range(2)]
            g_x = [0, 0]; g_x2 = [0, 0]; g_st = [0, 0]; g_xn = [0, 0]
            g_ps = [0, 0]; g_stg = [0, 0]
            for tt in range(32):
                pr = tt % 2
                p.emit("sync", lambda e, b=xt[pr], r=x_in[tt * 128:(tt + 1) * 128, :]:
                       e.dma_start(out=b, in_=r), waits=[(sV, g_x[pr])], inc16=cLNx[pr])
                ld = cLNx[pr].n
                p.emit("scalar", lambda e, b=xt[pr], o=x2t[pr]: e.activation(o, b, AF.Square),
                       waits=[(cLNx[pr], ld), (sV, g_x2[pr])], inc=sA)
                sq = sA.n
                s_ = st[pr]
                p.emit("vector", lambda e, b=xt[pr], s=s_: e.reduce_sum(s[:, 0:1], b, AX.X),
                       waits=[(cLNx[pr], ld), (sA, g_st[pr])], inc=sV)
                r1n = sV.n
                p.emit("vector", lambda e, o=x2t[pr], s=s_: e.reduce_sum(s[:, 1:2], o, AX.X),
                       waits=[(sA, sq)], inc=sV)
                g_x2[pr] = sV.n
                p.emit("vector", lambda e, s=s_: e.tensor_scalar(
                    s[:, 2:3], s[:, 0:1], 1.0 / D, None, OP.mult),
                    waits=[(sV, r1n)], inc=sV)
                p.emit("vector", lambda e, s=s_: e.tensor_tensor(
                    s[:, 3:4], s[:, 2:3], s[:, 2:3], OP.mult),
                    waits=[(sV, sV.n)], inc=sV)
                p.emit("vector", lambda e, s=s_: e.scalar_tensor_tensor(
                    s[:, 4:5], s[:, 1:2], 1.0 / D, s[:, 3:4], OP.mult, OP.subtract),
                    waits=[(sV, sV.n)], inc=sV)
                p.emit("vector", lambda e, s=s_: e.tensor_scalar(
                    s[:, 4:5], s[:, 4:5], 1e-5, None, OP.add),
                    waits=[(sV, sV.n)], inc=sV)
                var = sV.n
                p.emit("scalar", lambda e, s=s_: e.activation(
                    s[:, 5:6], s[:, 4:5], AF.Ln), waits=[(sV, var)], inc=sA)
                p.emit("vector", lambda e, s=s_: e.tensor_scalar(
                    s[:, 5:6], s[:, 5:6], -0.5, None, OP.mult), waits=[(sA, sA.n)], inc=sV)
                p.emit("scalar", lambda e, s=s_: e.activation(
                    s[:, 6:7], s[:, 5:6], AF.Exp), waits=[(sV, sV.n)], inc=sA)
                rst = sA.n
                p.emit("vector", lambda e, b=xt[pr], s=s_, o=xnt[pr]: e.tensor_scalar(
                    o, b, s[:, 2:3], s[:, 6:7], OP.subtract, OP.mult),
                    waits=[(sA, rst), (sP, g_xn[pr])], inc=sV)
                xn_done = sV.n
                g_x[pr] = sV.n
                g_st[pr] = sA.n

                def tr(e, o=ps_ln[pr], i_=xnt[pr]):
                    for dj in range(8):
                        ins = e.transpose(o[:, dj * 128:(dj + 1) * 128],
                                          i_[:, dj * 128:(dj + 1) * 128], idr)
                    return ins
                p.emit("tensor", tr, waits=[(sV, xn_done), (sV, g_ps[pr])], inc=sP)
                g_xn[pr] = sP.n
                p.emit("vector", lambda e, ps=ps_ln[pr], o=stg[pr]: e.tensor_copy(o, ps),
                       waits=[(sP, sP.n), (cLNs[pr], g_stg[pr])], inc=sV)
                g_ps[pr] = sV.n
                p.emit("sync", lambda e, o=xnT_h[:, :, tt * 128:(tt + 1) * 128], i_=stg[pr]:
                       e.dma_start(out=o, in_=i_.rearrange("p (j t) -> p j t", j=8)),
                       waits=[(sV, sV.n)], inc16=cLNs[pr])
                g_stg[pr] = cLNs[pr].n

        srcs = [xnT_h, xT0_h, xT1_h, xT0_h, xT1_h]
        dsts = [xT0_h, xT1_h, xT0_h, xT1_h, xT0_h]

        for li in range(depth):
            src, dst = srcs[li], dsts[li]
            barrier = (sP.n, sA.n, sV.n)
            bw = [(sP, barrier[0]), (sA, barrier[1]), (sV, barrier[2])]
            src_w = ([(c, c.n) for c in cLNs] if li == 0
                     else [(c, c.n) for c in cDS])
            spill_w = [(c, c.n) for c in cZS + cXA]

            with ExitStack() as lc:
                WiT = sbt(lc, "WiT_L%d" % li, [128, 8, E4], BF16)
                stg16 = [sbt(lc, "stg16_%d_L%d" % (i, li), [128, 2048], BF16) for i in range(2)]
                xcbf = [sbt(lc, "xcbf%d_L%d" % (i, li), [128, 8, T1], BF16) for i in range(2)]
                xim = sbt(lc, "xim_L%d" % li, [128, 16, T1 + 3], F32R)
                convb = sbt(lc, "convb_L%d" % li, [128, 16, T1], F32)
                xat = [sbt(lc, "xat%d_L%d" % (i, li), [128, T1], BF16) for i in range(2)]
                zst = [sbt(lc, "zst%d_L%d" % (i, li), [128, T1], F32R) for i in range(2)]
                cwsb = sbt(lc, "cwsb_L%d" % li, [128, 16, 4], F32)
                cbv = sbt(lc, "cbv_L%d" % li, [128, 16], F32)
                v16 = sbt(lc, "v16_L%d" % li, [16, 128], F32)
                ps_pr = [pst(lc, "p1pr%d_L%d" % (i, li), [128, 1024], BF16) for i in range(2)]
                ps_mm = [pst(lc, "p1mm%d_L%d" % (i, li), [128, T1]) for i in range(2)]
                ps_v = pst(lc, "p1v_L%d" % li, [128, 512])

                g_stg = [0, 0]; g_ps = [0, 0]
                for ebk in range(32):
                    pr = ebk % 2
                    p.emit("gpsimd", lambda e, o=stg16[pr],
                           i_=Wi_h[li, ebk * 128:(ebk + 1) * 128, :]:
                           e.dma_start(out=o[:, 0:1024], in_=i_),
                           waits=bw + src_w + [(sP, g_stg[pr]), (cBW, cBW.n)],
                           inc16=cW1[pr])
                    ld = cW1[pr].n

                    def trw(e, ps=ps_pr[pr], s=stg16[pr]):
                        for dj in range(8):
                            ins = e.transpose(ps[:, dj * 128:(dj + 1) * 128],
                                              s[:, dj * 128:(dj + 1) * 128], id16)
                        return ins
                    p.emit("tensor", trw, waits=bw + [(cW1[pr], ld), (sV, g_ps[pr])], inc=sP)
                    g_stg[pr] = sP.n
                    p.emit("vector", lambda e, o=WiT[:, :, ebk * 128:(ebk + 1) * 128],
                           ps=ps_pr[pr]:
                           e.tensor_copy(o, ps.rearrange("p (j t) -> p j t", j=8)),
                           waits=bw + [(sP, sP.n)], inc=sV)
                    g_ps[pr] = sV.n
                p.emit("sync", lambda e, o=cwsb,
                       i_=cw_h[li].rearrange("(j p) k -> p j k", p=128):
                       e.dma_start(out=o, in_=i_),
                       waits=bw + src_w + [(cBW, cBW.n)], inc16=cC1)
                p.emit("sync", lambda e, o=v16,
                       i_=cb_h[li].rearrange("(j p) -> j p", p=128):
                       e.dma_start(out=o, in_=i_),
                       waits=bw + src_w + [(cBW, cBW.n)], inc16=cC1)
                ldcb = cC1.n
                p.emit("tensor", lambda e: e.transpose(ps_v[:, 0:16], v16, id32[0:16, 0:16]),
                       waits=bw + [(cC1, ldcb)], inc=sP)
                p.emit("vector", lambda e: e.tensor_copy(cbv, ps_v[:, 0:16]),
                       waits=[(sP, sP.n)], inc=sV)
                for j in range(16):
                    p.emit("vector", lambda e, j=j: e.memset(xim[:, j, 0:3].bitcast(F32), 0.0),
                           inc=(sV if j == 15 else None))
                wit_done = sV.n
                ldcw = cC1.n

                g_xc = [0, 0]; g_ps1 = [0, 0]
                g_xat = [0, 0]; g_zst = [0, 0]
                g_cv = 0
                evac_last = 0
                for c in range(NC1):
                    pr = c % 2
                    p.emit("gpsimd", lambda e, o=xcbf[pr],
                           i_=src[:, :, c * T1:(c + 1) * T1]:
                           e.dma_start(out=o, in_=i_),
                           waits=bw + src_w + [(sP, g_xc[pr])], inc16=cXC[pr])
                    ldx = cXC[pr].n
                    if c > 0:
                        p.emit("vector", lambda e: e.tensor_copy(
                            xim[:, :, 0:3], xim[:, :, T1:T1 + 3]),
                            waits=[(sA, evac_last)], inc=sV)
                    ms = sV.n
                    evac0 = sA.n
                    for ebk in range(32):
                        pp = ebk % 2

                        def mm(e, pp=pp, ebk=ebk, pr=pr):
                            for dk in range(8):
                                ins = e.matmul(ps_mm[pp][:, 0:T1],
                                               WiT[:, dk, ebk * 128:(ebk + 1) * 128],
                                               xcbf[pr][:, dk, :],
                                               start=(dk == 0), stop=(dk == 7))
                            return ins
                        p.emit("tensor", mm,
                               waits=bw + [(cXC[pr], ldx), (sA, g_ps1[pp]),
                                           (sV, wit_done)], inc=sP)
                        mmn = sP.n
                        if ebk < 16:
                            p.emit("scalar", lambda e, o=xim[:, ebk, 3:3 + T1], ps=ps_mm[pp]:
                                   e.activation(o, ps[:, 0:T1], AF.Copy),
                                   waits=[(sP, mmn), (sV, ms)], inc=sA)
                        else:
                            zpr = ebk % 2
                            p.emit("scalar", lambda e, ps=ps_mm[pp], o=zst[zpr]:
                                   e.activation(o, ps[:, 0:T1], AF.Silu),
                                   waits=[(sP, mmn), (cZS[zpr], g_zst[zpr])], inc=sA)
                            p.emit("scalar", lambda e, o=zs_h[:, ebk - 16, c * T1:(c + 1) * T1],
                                   i_=zst[zpr].bitcast(F32): e.dma_start(out=o, in_=i_),
                                   waits=[(sA, sA.n)], inc16=cZS[zpr])
                            g_zst[zpr] = cZS[zpr].n
                        g_ps1[pp] = sA.n
                    evac_last = sA.n
                    g_xc[pr] = sP.n
                    for k in range(4):
                        for j in range(16):
                            evj = evac0 + j + 1
                            if k == 0:
                                p.emit("vector", lambda e, j=j: e.tensor_scalar(
                                    convb[:, j, :], xim[:, j, 0:T1],
                                    cwsb[:, j:j + 1, 0:1].squeeze(1),
                                    cbv[:, j:j + 1], OP.mult, OP.add),
                                    waits=[(sA, evj), (cC1, ldcw), (sV, wit_done),
                                           (sA, g_cv)])
                            else:
                                p.emit("vector", lambda e, j=j, k=k:
                                       e.scalar_tensor_tensor(
                                           convb[:, j, :], xim[:, j, k:k + T1],
                                           cwsb[:, j:j + 1, k:k + 1].squeeze(1),
                                           convb[:, j, :], OP.mult, OP.add),
                                       waits=[(sA, evj)],
                                       inc=(sV if (k == 3 and j == 15) else None))
                    cv_done = sV.n
                    for j in range(16):
                        apr = j % 2
                        p.emit("scalar", lambda e, j=j, o=xat[apr]:
                               e.activation(o, convb[:, j, :], AF.Silu),
                               waits=[(sV, cv_done), (cXA[apr], g_xat[apr])], inc=sA)
                        p.emit("scalar", lambda e, o=xa_h[:, j, c * T1:(c + 1) * T1],
                               i_=xat[apr]: e.dma_start(out=o, in_=i_),
                               waits=[(sA, sA.n)], inc16=cXA[apr])
                        g_xat[apr] = cXA[apr].n
                    g_cv = sA.n

            if skip_last_p2 and li == depth - 1:
                continue
            barrier = (sP.n, sA.n, sV.n)
            bw = [(sP, barrier[0]), (sA, barrier[1]), (sV, barrier[2])]
            spill_w = [(c, c.n) for c in cZS + cXA]
            with ExitStack() as lc:
                WoT = sbt(lc, "WoT_L%d" % li, [128, 16, D], BF16)
                WxT = sbt(lc, "WxT_L%d" % li, [128, 16, 96], BF16)
                WdtT = sbt(lc, "WdtT_L%d" % li, [64, D2], F32R)
                Asb = sbt(lc, "Asb_L%d" % li, [128, 16, NST], F32)
                Dpv = sbt(lc, "Dpv_L%d" % li, [128, 16], F32)
                bdtv = sbt(lc, "bdtv_L%d" % li, [128, 16], F32)
                v16b = sbt(lc, "v16b_L%d" % li, [16, 128], F32)
                state = sbt(lc, "state_L%d" % li, [128, 256], F32)

                with ExitStack() as pc:
                    stg16b = sbt(pc, "stg16b_L%d" % li, [128, 2048], BF16)
                    stgr = sbt(pc, "stgr_L%d" % li, [128, 2048], F32R)
                    ps_prb = pst(pc, "p2prb_L%d" % li, [128, 2048], BF16)
                    ps_prf = pst(pc, "p2prf_L%d" % li, [128, 512])
                    ps_prr = pst(pc, "p2prr_L%d" % li, [128, 512], F32R)
                    g_stg = 0
                    g_ps = 0
                    for ob in range(8):
                        p.emit("gpsimd", lambda e, o=stg16b,
                               i_=Wo_h[li, ob * 128:(ob + 1) * 128, :]:
                               e.dma_start(out=o, in_=i_),
                               waits=bw + spill_w + [(sP, g_stg)], inc16=cW2)
                        ld = cW2.n

                        def trw(e):
                            for dk in range(16):
                                ins = e.transpose(ps_prb[:, dk * 128:(dk + 1) * 128],
                                                  stg16b[:, dk * 128:(dk + 1) * 128], id16)
                            return ins
                        p.emit("tensor", trw, waits=bw + [(cW2, ld), (sV, g_ps)], inc=sP)
                        g_stg = sP.n
                        p.emit("vector", lambda e, o=WoT[:, :, ob * 128:(ob + 1) * 128]:
                               e.tensor_copy(o, ps_prb.rearrange("p (j t) -> p j t", j=16)),
                               waits=bw + [(sP, sP.n)], inc=sV)
                        g_ps = sV.n
                    p.emit("gpsimd", lambda e, o=stg16b[0:96, :], i_=Wx_h[li]:
                           e.dma_start(out=o, in_=i_), waits=[(sV, g_ps)], inc16=cW2)
                    ldwx = cW2.n
                    for dk in range(16):
                        p.emit("tensor", lambda e, dk=dk: e.transpose(
                            ps_prb[:, 0:96], stg16b[0:96, dk * 128:(dk + 1) * 128],
                            id16[0:96, 0:96]), waits=[(cW2, ldwx), (sV, sV.n)], inc=sP)
                        p.emit("vector", lambda e, dk=dk: e.tensor_copy(
                            WxT[:, dk, :], ps_prb[:, 0:96]), waits=[(sP, sP.n)], inc=sV)
                    p.emit("gpsimd", lambda e,
                           o=stgr[:, 0:1024].rearrange("p (dk r) -> p dk r", dk=16),
                           i_=Wdt_h[li].rearrange("(dk p) r -> p dk r", p=128):
                           e.dma_start(out=o, in_=i_), waits=[(sV, sV.n)], inc16=cW2)
                    ldwd = cW2.n
                    for dk in range(16):
                        p.emit("tensor", lambda e, dk=dk: e.transpose(
                            ps_prr[0:64, 0:128], stgr[:, dk * 64:(dk + 1) * 64], idr),
                            waits=[(cW2, ldwd), (sV, sV.n)], inc=sP)
                        p.emit("vector", lambda e, dk=dk: e.tensor_copy(
                            WdtT[:, dk * 128:(dk + 1) * 128], ps_prr[0:64, 0:128]),
                            waits=[(sP, sP.n)], inc=sV)
                    p.emit("sync", lambda e, o=Asb,
                           i_=Al_h[li].rearrange("(j p) n -> p j n", p=128):
                           e.dma_start(out=o, in_=i_),
                           waits=bw + spill_w, inc16=cC2)
                    ldA = cC2.n
                    p.emit("scalar", lambda e: e.activation(fl(Asb), fl(Asb), AF.Exp),
                           waits=bw + [(cC2, ldA)], inc=sA)
                    p.emit("vector", lambda e: e.tensor_scalar(
                        fl(Asb), fl(Asb), -1.0, None, OP.mult), waits=[(sA, sA.n)], inc=sV)
                    for vec_h, vdst in ((Dp_h[li], Dpv), (bdt_h[li], bdtv)):
                        p.emit("sync", lambda e, i_=vec_h.rearrange("(j p) -> j p", p=128):
                               e.dma_start(out=v16b, in_=i_),
                               waits=spill_w + [(sP, sP.n)], inc16=cC2)
                        ldv = cC2.n
                        p.emit("tensor", lambda e: e.transpose(
                            ps_prf[:, 0:16], v16b, id32[0:16, 0:16]),
                            waits=[(cC2, ldv), (sV, sV.n)], inc=sP)
                        p.emit("vector", lambda e, o=vdst: e.tensor_copy(o, ps_prf[:, 0:16]),
                               waits=[(sP, sP.n)], inc=sV)
                    p.emit("vector", lambda e: e.memset(state, 0.0), inc=sV)
                prep_done = sV.n

                xab = sbt(lc, "xab_L%d" % li, [128, 16, T2], BF16)
                delb = sbt(lc, "delb_L%d" % li, [128, 16, T2], F32)
                dub = sbt(lc, "dub_L%d" % li, [128, 16, T2], F32)
                dAb = [sbt(lc, "dAb%d_L%d" % (i, li), [128, 16, T2], F32) for i in range(2)]
                dBuA = [sbt(lc, "dBuA%d_L%d" % (i, li), [128, 16, T2], F32) for i in range(2)]
                dBuB = sbt(lc, "dBuB_L%d" % li, [128, 16, T2], F32)
                yac = sbt(lc, "yac_L%d" % li, [128, 16, T2], F32)
                ygbf = dBuA[1].bitcast(BF16)[:, :, 0:T2]
                xres = sbt(lc, "xres_L%d" % li, [128, 8, T2], F32)
                dtb = sbt(lc, "dtb_L%d" % li, [64, T2], F32R)
                bcb = sbt(lc, "bcb_L%d" % li, [32, T2], F32)
                bcc = [sbt(lc, "bcc%d_L%d" % (i, li), [128, 2, T2], F32) for i in range(2)]
                ps_xd = pst(lc, "p2xd_L%d" % li, [128, 512])
                ps_del = [pst(lc, "p2dl%d_L%d" % (i, li), [128, 512]) for i in range(2)]
                ps_wo = [pst(lc, "p2wo%d_L%d" % (i, li), [128, 512]) for i in range(2)]

                g_xab = 0
                g_xres = 0
                g_xd = 0
                g_del = [0, 0]
                g_wo = [0, 0]
                g_dA_g = [0, 0]
                g_dA_v = [0, 0]
                g_bcc = [0, 0]
                g_dbua = [0, 0]
                g_bcb = 0
                for c in range(NC2):
                    pr = c % 2
                    sl = slice(c * T2, (c + 1) * T2)
                    p.emit("sync", lambda e, i_=xa_h[:, :, sl]: e.dma_start(out=xab, in_=i_),
                           waits=bw + spill_w + [(sV, g_xab)], inc16=cXAB)
                    ldxa = cXAB.n
                    p.emit("scalar", lambda e, o=xres, i_=src[:, :, sl]: e.dma_start(
                        out=o, in_=i_),
                        waits=bw + src_w + [(cDS[0], g_xres)], inc16=cXR[0])
                    ldxr = cXR[0].n

                    def mmx(e):
                        for dk in range(16):
                            ins = e.matmul(ps_xd[0:96, 0:T2], WxT[:, dk, :], xab[:, dk, :],
                                           start=(dk == 0), stop=(dk == 15))
                        return ins
                    p.emit("tensor", mmx, waits=[(cXAB, ldxa), (sV, g_xd), (sV, prep_done)],
                           inc=sP)
                    mmxn = sP.n
                    p.emit("vector", lambda e: e.tensor_copy(dtb, ps_xd[0:64, 0:T2]),
                           waits=[(sP, mmxn), (cBW, g_bcb)])
                    p.emit("vector", lambda e: e.tensor_copy(bcb, ps_xd[64:96, 0:T2]), inc=sV)
                    g_xd = sV.n
                    bc_ev = sV.n
                    p.emit("sync", lambda e, o=bcs_h[:, sl]: e.dma_start(out=o, in_=bcb),
                           waits=[(sV, bc_ev)], inc16=cBW)
                    bcw = cBW.n
                    g_bcb = cBW.n
                    for dk in range(16):
                        pp = dk % 2
                        p.emit("tensor", lambda e, dk=dk, pp=pp: e.matmul(
                            ps_del[pp][:, 0:T2], WdtT[:, dk * 128:(dk + 1) * 128], dtb,
                            start=True, stop=True),
                            waits=[(sV, bc_ev), (sV, g_del[pp])], inc=sP)
                        p.emit("vector", lambda e, dk=dk, pp=pp: e.tensor_scalar(
                            delb[:, dk, :], ps_del[pp][:, 0:T2], bdtv[:, dk:dk + 1],
                            None, OP.add),
                            waits=[(sP, sP.n), (cC2, cC2.n)], inc=sV)
                        g_del[pp] = sV.n
                    del_done = sV.n
                    p.emit("vector", lambda e: e.tensor_scalar(
                        fl(dBuA[0]), fl(delb), -1.0, None, OP.mult),
                        waits=[(sV, del_done), (sV, g_dbua[0])], inc=sV)
                    p.emit("vector", lambda e: e.tensor_tensor(
                        fl(dBuA[0]), fl(delb), fl(dBuA[0]), OP.min),
                        waits=[(sV, sV.n)], inc=sV)
                    p.emit("scalar", lambda e: e.activation(fl(dAb[0]), fl(dBuA[0]), AF.Exp),
                           waits=[(sV, sV.n), (sG, g_dA_g[0]), (sV, g_dA_v[0]),
                                  (cC2, cC2.n)])
                    p.emit("scalar", lambda e: e.activation(fl(dBuA[0]), fl(dAb[0]), AF.Ln,
                                                            bias=1.0), inc=sA)
                    p.emit("vector", lambda e: e.scalar_tensor_tensor(
                        fl(delb), fl(delb), 0.0, fl(dBuA[0]), OP.max, OP.add),
                        waits=[(sA, sA.n)], inc=sV)
                    p.emit("vector", lambda e: e.tensor_tensor(fl(dub), fl(delb), fl(xab),
                                                               OP.mult),
                           waits=[(sV, sV.n)], inc=sV)
                    sp_done = sV.n
                    if dbgd_h is not None:
                        p.emit("sync", lambda e, o=dbgd_h[:, :, sl]: e.dma_start(
                            out=o, in_=delb), waits=[(sV, sp_done)], inc16=cC2)
                    yad_prev = yad_prev if c > 0 else 0
                    hc_q = []

                    def emit_hc_yadd(n, bpr, st_done):
                        p.emit("gpsimd", lambda e, bpr=bpr: e.tensor_tensor(
                            dBuB, dAb[bpr],
                            bcc[bpr][:, 1:2, :].broadcast_to((128, 16, T2)), OP.mult),
                            waits=[(sV, st_done), (sV, yad_prev_box[0])], inc=sG)
                        hcn = sG.n
                        g_bcc[bpr] = sG.n
                        g_dA_g[bpr] = sG.n
                        if n == 0 and first_box[0]:
                            p.emit("vector", lambda e: e.tensor_copy(fl(yac), fl(dBuB)),
                                   waits=[(sG, hcn)], inc=sV)
                        else:
                            p.emit("vector", lambda e: e.tensor_tensor(
                                fl(yac), fl(yac), fl(dBuB), OP.add),
                                waits=[(sG, hcn)], inc=sV)
                        yad_prev_box[0] = sV.n

                    yad_prev_box = [yad_prev]
                    first_box = [True]
                    for n in range(NST):
                        bpr = n % 2
                        row0 = bcs_h[n:n + 17:16, sl]
                        p.emit("sync", lambda e, o=bcc[bpr], r=row0: e.dma_start(
                            out=o, in_=r.unsqueeze(0).broadcast_to((128, 2, T2))),
                            waits=[(cBW, bcw), (sG, g_bcc[bpr])], inc16=cBC[bpr])
                        bcn = cBC[bpr].n

                        def dAe(e, n=n, apr=bpr):
                            for j in range(16):
                                ins = e.activation(dAb[apr][:, j, :], delb[:, j, :], AF.Exp,
                                                   scale=Asb[:, j:j + 1, n:n + 1].squeeze(1))
                            return ins
                        p.emit("scalar", dAe, waits=[(sV, sp_done), (sG, g_dA_g[bpr]),
                                                     (sV, g_dA_v[bpr])], inc=sA)
                        dan = sA.n
                        p.emit("gpsimd", lambda e, bpr=bpr: e.tensor_tensor(
                            dBuA[bpr], dub,
                            bcc[bpr][:, 0:1, :].broadcast_to((128, 16, T2)), OP.mult),
                            waits=[(cBC[bpr], bcn), (sV, sp_done), (sV, g_dbua[bpr])],
                            inc=sG)
                        dbun = sG.n

                        def scn(e, n=n, bpr=bpr):
                            for j in range(16):
                                ins = e.tensor_tensor_scan(
                                    dAb[bpr][:, j, :], dAb[bpr][:, j, :],
                                    dBuA[bpr][:, j, :],
                                    state[:, n * 16 + j:n * 16 + j + 1],
                                    OP.mult, OP.add)
                            return ins
                        p.emit("vector", scn, waits=[(sG, dbun), (sA, dan)], inc=sV)
                        scnn = sV.n
                        g_dbua[bpr] = sV.n
                        p.emit("vector", lambda e, n=n, bpr=bpr: e.tensor_copy(
                            state[:, n * 16:(n + 1) * 16],
                            dAb[bpr][:, :, T2 - 1:T2].squeeze(2)),
                            waits=[(sV, scnn)], inc=sV)
                        g_dA_v[bpr] = sV.n
                        hc_q.append((n, bpr, sV.n))
                        if len(hc_q) > 1:
                            emit_hc_yadd(*hc_q.pop(0))
                            first_box[0] = False
                    while hc_q:
                        emit_hc_yadd(*hc_q.pop(0))
                    yad_prev = yad_prev_box[0]
                    scan_done = sV.n
                    p.emit("scalar", lambda e, i_=zs_h[:, :, sl]: e.dma_start(
                        out=dBuA[0], in_=i_),
                        waits=[(sV, scan_done)] + spill_w, inc16=cZL)
                    ldz = cZL.n

                    def skp(e):
                        for j in range(16):
                            ins = e.scalar_tensor_tensor(dAb[0][:, j, :], xab[:, j, :],
                                                         Dpv[:, j:j + 1], yac[:, j, :],
                                                         OP.mult, OP.add)
                        return ins
                    p.emit("vector", skp, waits=[(sV, g_dA_v[0]), (sG, g_dA_g[0]),
                                                 (sV, scan_done)], inc=sV)
                    if dbgy_h is not None:
                        p.emit("sync", lambda e, o=dbgy_h[:, :, sl]: e.dma_start(
                            out=o, in_=dAb[0]), waits=[(sV, sV.n)], inc16=cC2)
                    p.emit("vector", lambda e: e.tensor_tensor(ygbf, dAb[0], dBuA[0],
                                                               OP.mult),
                           waits=[(cZL, ldz), (sV, sV.n), (cC2, cC2.n)], inc=sV)
                    g_dA_v[0] = sV.n
                    g_xab = sV.n
                    yg_done = sV.n
                    if dbgg_h is not None:
                        p.emit("sync", lambda e, o=dbgg_h[:, :, sl]: e.dma_start(
                            out=o, in_=ygbf), waits=[(sV, yg_done)], inc16=cC2)
                    for ob in range(8):
                        pp = ob % 2

                        def mwo(e, ob=ob, pp=pp):
                            for dk in range(16):
                                ins = e.matmul(ps_wo[pp][:, 0:T2],
                                               WoT[:, dk, ob * 128:(ob + 1) * 128],
                                               ygbf[:, dk, :], start=(dk == 0),
                                               stop=(dk == 15))
                            return ins
                        p.emit("tensor", mwo, waits=[(sV, yg_done), (sV, g_wo[pp])], inc=sP)
                        won = sP.n
                        p.emit("vector", lambda e, ob=ob, pp=pp: e.tensor_tensor(
                            xres[:, ob, :], xres[:, ob, :], ps_wo[pp][:, 0:T2], OP.add),
                            waits=[(sP, won), (cXR[0], ldxr)], inc=sV)
                        g_wo[pp] = sV.n
                    g_dbua[0] = sV.n
                    g_dbua[1] = sV.n
                    p.emit("scalar", lambda e, o=dst[:, :, sl], i_=xres: e.dma_start(
                        out=o, in_=i_), waits=[(sV, sV.n)], inc16=cDS[0])
                    g_xres = cDS[0].n

        mamba_done = (sP.n, sA.n, sV.n)
        mbw = [(sP, mamba_done[0]), (sA, mamba_done[1]), (sV, mamba_done[2])]
        dst_w = [(c, c.n) for c in (cDS if depth > 0 else cLNs)]
        ln_w = [(c, c.n) for c in cLNs]

        def writeback(src_hbm):
            with ExitStack() as lc:
                ps_f = [pst(lc, "wb%d" % i, [128, D]) for i in range(2)]
                ft = [sbt(lc, "wb_f%d" % i, [128, 8, 128], F32) for i in range(2)]
                ot = [sbt(lc, "wb_o%d" % i, [128, D], F32) for i in range(2)]
                g_ft = [0, 0]; g_ps = [0, 0]; g_ot = [0, 0]
                for tt in range(32):
                    pr = tt % 2
                    p.emit("scalar", lambda e, o=ft[pr],
                           i_=src_hbm[:, :, tt * 128:(tt + 1) * 128]:
                           e.dma_start(out=o, in_=i_),
                           waits=mbw + dst_w + [(sP, g_ft[pr])], inc16=cWB[pr])
                    ld = cWB[pr].n

                    def tr(e, pr=pr):
                        for dj in range(8):
                            ins = e.transpose(ps_f[pr][:, dj * 128:(dj + 1) * 128],
                                              ft[pr][:, dj, :], id32)
                        return ins
                    p.emit("tensor", tr, waits=mbw + [(cWB[pr], ld), (sV, g_ps[pr])], inc=sP)
                    g_ft[pr] = sP.n
                    p.emit("vector", lambda e, pr=pr: e.tensor_copy(ot[pr], ps_f[pr]),
                           waits=mbw + [(sP, sP.n), (cOU[pr], g_ot[pr])], inc=sV)
                    g_ps[pr] = sV.n
                    p.emit("sync", lambda e, pr=pr, o=out_h[tt * 128:(tt + 1) * 128, :]:
                           e.dma_start(out=o, in_=ot[pr]), waits=[(sV, sV.n)], inc16=cOU[pr])
                    g_ot[pr] = cOU[pr].n

        if dev_out == "xn":
            writeback(xnT_h)
        elif dev_out == "mamba" or not do_ffn:
            wb_src = dsts[depth - 1] if depth > 0 else xnT_h
            if skip_last_p2 and depth > 1:
                wb_src = dsts[depth - 2]
            writeback(wb_src)
        else:
            with ExitStack() as lc:
                rwsb = sbt(lc, "rwsb", [128, 8, 8], F32R)
                ebb = sbt(lc, "ebb", [128, 8], F32)
                wsc = sbt(lc, "wsc", [128, 8], F32)
                wval = sbt(lc, "wval", [128, 8], F32)
                wmsk = sbt(lc, "wmsk", [128, 8], F32)
                wtmp = sbt(lc, "wtmp", [128, 8], F32)
                m12 = sbt(lc, "m12", [128, 4], F32)
                weT = sbt(lc, "weT", [8, L], F32)
                xnq = [sbt(lc, "xnq%d" % i, [128, 8, TF], F32R) for i in range(2)]
                ps_rt = pst(lc, "rt", [128, 512])
                ps_wt = pst(lc, "wt", [128, 512])
                p.emit("gpsimd", lambda e: e.dma_start(
                    out=rwsb, in_=rw_h.rearrange("(dk p) e -> p dk e", p=128)),
                    waits=mbw + dst_w + [(cBW, cBW.n)], inc16=cRW)
                p.emit("sync", lambda e: e.dma_start(out=ebb,
                                                     in_=eb_h.broadcast_to((128, NE))),
                       waits=mbw + dst_w + [(cBW, cBW.n)], inc16=cRW)
                ld0 = cRW.n
                g_xnq = [0, 0]
                g_rt = 0
                g_wt = 0
                g_weT = 0
                for ch in range(8):
                    pr = ch % 2
                    p.emit("gpsimd", lambda e, o=xnq[pr],
                           i_=xnT_h[:, :, ch * TF:(ch + 1) * TF]: e.dma_start(out=o, in_=i_),
                           waits=mbw + ln_w + dst_w + [(sP, g_xnq[pr]), (cBW, cBW.n)],
                           inc16=cXQ[pr])
                    ldq = cXQ[pr].n
                    for ts_ in range(4):
                        tt = ch * 4 + ts_

                        def mmr(e, pr=pr, ts_=ts_):
                            for dk in range(8):
                                ins = e.matmul(ps_rt[:, 0:8],
                                               xnq[pr][:, dk, ts_ * 128:(ts_ + 1) * 128],
                                               rwsb[:, dk, :], start=(dk == 0),
                                               stop=(dk == 7))
                            return ins
                        p.emit("tensor", mmr,
                               waits=mbw + [(cXQ[pr], ldq), (cRW, ld0), (sV, g_rt)],
                               inc=sP)
                        rtn = sP.n

                        p.emit("vector", lambda e: e.reduce_max(
                            m12[:, 0:1], ps_rt[:, 0:8], AX.X),
                            waits=[(sP, rtn)], inc=sV)
                        p.emit("vector", lambda e: e.tensor_scalar(
                            wsc, ps_rt[:, 0:8], m12[:, 0:1], None, OP.subtract),
                            waits=[(sV, sV.n)], inc=sV)
                        g_rt = sV.n
                        p.emit("scalar", lambda e: e.activation(wsc, wsc, AF.Exp),
                               waits=[(sV, sV.n)], inc=sA)

                        for fn_ in (
                            lambda e: e.reduce_sum(m12[:, 1:2], wsc, AX.X),
                            lambda e: e.reciprocal(m12[:, 2:3], m12[:, 1:2]),
                            lambda e: e.tensor_scalar(wsc, wsc, m12[:, 2:3], None, OP.mult),
                            lambda e: e.tensor_tensor(wval, wsc, ebb, OP.add),
                            lambda e: e.reduce_max(m12[:, 0:1], wval, AX.X),
                            lambda e: e.tensor_scalar(wmsk, wval, m12[:, 0:1], None,
                                                      OP.is_ge),
                            lambda e: e.scalar_tensor_tensor(wtmp, wmsk, -1e30, wval,
                                                             OP.mult, OP.add),
                            lambda e: e.reduce_max(m12[:, 1:2], wtmp, AX.X),
                            lambda e: e.tensor_scalar(wtmp, wtmp, m12[:, 1:2], None,
                                                      OP.is_ge),
                            lambda e: e.tensor_tensor(wmsk, wmsk, wtmp, OP.add),
                            lambda e: e.tensor_tensor(wval, wsc, wmsk, OP.mult),
                        ):
                            p.emit("vector", fn_,
                                   waits=[(sA, sA.n), (sP, g_wt), (sV, sV.n)], inc=sV)
                        wdone = sV.n
                        p.emit("tensor", lambda e: e.transpose(ps_wt[0:8, 0:128], wval, id32),
                               waits=[(sV, wdone), (sV, g_weT)], inc=sP)
                        g_wt = sP.n
                        p.emit("vector", lambda e, o=weT[0:8, tt * 128:(tt + 1) * 128]:
                               e.tensor_copy(o, ps_wt[0:8, 0:128]),
                               waits=[(sP, sP.n)], inc=sV)
                        g_weT = sV.n
                    g_xnq[pr] = sP.n
                p.emit("gpsimd", lambda e: e.dma_start(out=weH_h, in_=weT),
                       waits=[(sV, g_weT)], inc16=cWE)
            we_done = cWE.n
            rbw = [(sP, sP.n), (sA, sA.n), (sV, sV.n)]

            with ExitStack() as fc:
                xnq16 = sbt(fc, "xnq16", [128, 8, LQ], BF16)
                acc = sbt(fc, "acc", [128, 8, LQ], F32)
                s1t = [sbt(fc, "s1t%d" % i, [128, TF], F32) for i in range(2)]
                t1t = [sbt(fc, "t1t%d" % i, [128, TF], F32) for i in range(2)]
                gbf = sbt(fc, "gbf", [128, 8, TF], BF16)
                web = [sbt(fc, "web%d" % i, [128, TF], F32) for i in range(2)]
                for q in range(NQ):
                    qsl = slice(q * LQ, (q + 1) * LQ)
                    with ExitStack() as wc:
                        ps_h1 = [pst(wc, "fh1%d_q%d" % (i, q), [128, 512]) for i in range(2)]
                        ps_h3 = [pst(wc, "fh3%d_q%d" % (i, q), [128, 512]) for i in range(2)]
                        ps_o = [pst(wc, "fo%d_q%d" % (i, q), [128, 512]) for i in range(2)]
                        w1sb = [sbt(wc, "w1sb%d_q%d" % (i, q), [128, 8, D], BF16) for i in range(2)]
                        w3sb = [sbt(wc, "w3sb%d_q%d" % (i, q), [128, 8, D], BF16) for i in range(2)]
                        w2sb = [sbt(wc, "w2sb%d_q%d" % (i, q), [128, 8, D], BF16) for i in range(2)]
                        p.emit("gpsimd", lambda e, o=xnq16, i_=xnT_h[:, :, qsl]:
                               e.dma_start(out=o, in_=i_),
                               waits=rbw + ln_w + [(sP, sP.n), (cWE, we_done)],
                               inc16=cXN)
                        ldxn = cXN.n
                        p.emit("vector", lambda e: e.memset(fl(acc), 0.0),
                               waits=rbw + [(sV, sV.n), (sP, sP.n), (cWE, we_done)],
                               inc=sV)
                        g_w = [0, 0]
                        g_web = [0, 0]
                        g_h1 = [0, 0]
                        g_h3 = [0, 0]
                        g_po = [0, 0]
                        g_s1 = [0, 0]
                        g_gbf = 0
                        ldw = [0, 0]
                        for e_ in range(9):
                            wpr = e_ % 2
                            if e_ < 8:
                                srcs_w = (w1_h[e_], w3_h[e_], w2_h[e_])
                            else:
                                srcs_w = (sw1_h, sw3_h, sw2_h)
                            for wsrc, wdst in zip(srcs_w, (w1sb[wpr], w3sb[wpr], w2sb[wpr])):
                                p.emit("gpsimd", lambda e, o=wdst,
                                       i_=wsrc.rearrange("(dk p) h -> p dk h", p=128):
                                       e.dma_start(out=o, in_=i_),
                                       waits=rbw + [(sP, g_w[wpr]), (cWE, we_done),
                                                    (cOU[0], cOU[0].n),
                                                    (cOU[1], cOU[1].n),
                                                    (cX5, cX5.n)],
                                       inc16=cFW[wpr])
                            ldw[wpr] = cFW[wpr].n
                            for c2 in range(2):
                                csl = slice(c2 * TF, (c2 + 1) * TF)
                                tok0 = q * LQ + c2 * TF
                                if e_ < 8:
                                    p.emit("gpsimd", lambda e, o=web[c2], e_=e_,
                                           r=weH_h[e_:e_ + 1, tok0:tok0 + TF]:
                                           e.dma_start(out=o, in_=r.broadcast_to((128, TF))),
                                           waits=[(cWE, we_done), (sV, g_web[c2])],
                                           inc16=cWB[c2])
                                ldwe = cWB[c2].n
                                for hj in range(8):
                                    hp = hj % 2

                                    def mmh(e, wpr=wpr, hj=hj, hp=hp, cs=csl):
                                        for dk in range(8):
                                            e.matmul(ps_h1[hp][:, 0:TF],
                                                     w1sb[wpr][:, dk, hj * 128:(hj + 1) * 128],
                                                     xnq16[:, dk, cs],
                                                     start=(dk == 0), stop=(dk == 7))
                                        for dk in range(8):
                                            ins = e.matmul(
                                                ps_h3[hp][:, 0:TF],
                                                w3sb[wpr][:, dk, hj * 128:(hj + 1) * 128],
                                                xnq16[:, dk, cs],
                                                start=(dk == 0), stop=(dk == 7))
                                        return ins
                                    p.emit("tensor", mmh,
                                           waits=[(cFW[wpr], ldw[wpr]), (cXN, ldxn),
                                                  (sA, g_h1[hp]), (sV, g_h3[hp])], inc=sP)
                                    hn = sP.n
                                    p.emit("scalar", lambda e, hp=hp, sp=s1t[hj % 2]:
                                           e.activation(sp, ps_h1[hp][:, 0:TF], AF.Silu),
                                           waits=[(sP, hn), (sV, g_s1[hj % 2])], inc=sA)
                                    g_h1[hp] = sA.n

                                    p.emit("vector", lambda e, hj=hj, hp=hp:
                                           e.tensor_tensor(t1t[hj % 2], s1t[hj % 2],
                                                           ps_h3[hp][:, 0:TF], OP.mult),
                                           waits=[(sA, sA.n), (sP, g_gbf)], inc=sV)

                                    def gat(e, hj=hj, e_=e_, c2=c2):
                                        tp = t1t[hj % 2]
                                        if e_ < 8:
                                            ins = e.tensor_tensor(gbf[:, hj, :], tp,
                                                                  web[c2], OP.mult)
                                        else:
                                            ins = e.tensor_copy(gbf[:, hj, :], tp)
                                        return ins
                                    p.emit("vector", gat,
                                           waits=[(cWB[c2], ldwe), (sV, sV.n)], inc=sV)
                                    g_h3[hp] = sV.n
                                    g_s1[hj % 2] = sV.n
                                g_done = sV.n
                                g_web[c2] = sV.n
                                for ob in range(8):
                                    po = ob % 2

                                    def mmo(e, wpr=wpr, ob=ob, po=po):
                                        for hk in range(8):
                                            ins = e.matmul(
                                                ps_o[po][:, 0:TF],
                                                w2sb[wpr][:, hk, ob * 128:(ob + 1) * 128],
                                                gbf[:, hk, :],
                                                start=(hk == 0), stop=(hk == 7))
                                        return ins
                                    p.emit("tensor", mmo,
                                           waits=[(sV, g_done), (sV, g_po[po])], inc=sP)
                                    on = sP.n
                                    p.emit("vector", lambda e, ob=ob, po=po, cs=csl:
                                           e.tensor_tensor(acc[:, ob, cs], acc[:, ob, cs],
                                                           ps_o[po][:, 0:TF], OP.add),
                                           waits=[(sP, on)], inc=sV)
                                    g_po[po] = sV.n
                                g_gbf = sP.n
                            g_w[wpr] = sP.n
                        experts_done = (sP.n, sV.n)
                    with ExitStack() as gc2:
                        gq = sbt(gc2, "gq_q%d" % q, [128, 8, LQ], F32R)
                        x5q = sbt(gc2, "x5q_q%d" % q, [128, 8, LQ], F32)
                        x0t = [sbt(gc2, "x0t%d_q%d" % (i, q), [128, D], F32) for i in range(2)]
                        ott = [sbt(gc2, "ott%d_q%d" % (i, q), [128, D], F32) for i in range(2)]
                        ps_f = [pst(gc2, "pf%d_q%d" % (i, q), [128, D]) for i in range(2)]
                        p.emit("scalar", lambda e, o=x5q, i_=xT0_h[:, :, qsl]: e.dma_start(
                            out=o, in_=i_),
                            waits=dst_w + [(sP, experts_done[0])], inc16=cX5)
                        ldx5 = cX5.n
                        p.emit("scalar", lambda e: e.activation(fl(gq), fl(acc), AF.Silu),
                               waits=[(sV, experts_done[1]), (sP, experts_done[0])], inc=sA)
                        p.emit("vector", lambda e: e.tensor_tensor(fl(acc), fl(x5q), fl(gq),
                                                                   OP.mult),
                               waits=[(sA, sA.n), (cX5, ldx5)], inc=sV)
                        sg_done = sV.n
                        g_ft = [0, 0]; g_psf = [0, 0]; g_ot = [0, 0]
                        for tt in range(8):
                            pr = tt % 2
                            row = q * LQ + tt * 128

                            def trf(e, pr=pr, tt=tt):
                                for dj in range(8):
                                    ins = e.transpose(
                                        ps_f[pr][:, dj * 128:(dj + 1) * 128],
                                        acc[:, dj, tt * 128:(tt + 1) * 128], id32)
                                return ins
                            p.emit("tensor", trf, waits=[(sV, sg_done), (sV, g_psf[pr])],
                                   inc=sP)
                            trn = sP.n
                            p.emit("scalar", lambda e, o=x0t[pr],
                                   i_=x_in[row:row + 128, :]: e.dma_start(out=o, in_=i_),
                                   waits=[(sV, g_ft[pr])], inc16=cX0[pr])
                            ldx0 = cX0[pr].n
                            p.emit("vector", lambda e, pr=pr: e.tensor_tensor(
                                ott[pr], x0t[pr], ps_f[pr], OP.add),
                                waits=[(sP, trn), (cX0[pr], ldx0), (cOU[pr], g_ot[pr])],
                                inc=sV)
                            g_psf[pr] = sV.n
                            g_ft[pr] = sV.n
                            p.emit("sync", lambda e, pr=pr, o=out_h[row:row + 128, :]:
                                   e.dma_start(out=o, in_=ott[pr]), waits=[(sV, sV.n)],
                                   inc16=cOU[pr])
                            g_ot[pr] = cOU[pr].n

        p.emit("sync", lambda e: e.nop(),
               waits=[(c, c.n) for c in all_dma_ctrs])

        block = ctx.enter_context(nc.Block())
        p.build(block)
    return nc


_NC_CACHE = {}


def _get_nc(key):
    if key not in _NC_CACHE:
        _NC_CACHE[key] = build(*key)
    return _NC_CACHE[key]


def kernel(L0_attn_x, L0_attn_W_in, L0_attn_conv_w, L0_attn_conv_b, L0_attn_W_x,
           L0_attn_W_dt, L0_attn_b_dt, L0_attn_A_log, L0_attn_D, L0_attn_W_out,
           L0_ffn_router_w, L0_ffn_expert_bias, L0_ffn_w1, L0_ffn_w2, L0_ffn_w3,
           L0_ffn_sw1, L0_ffn_sw2, L0_ffn_sw3,
           _depth=NL, _do_ffn=True, _dev_out=None, _cores=8, _dbg_spill=False,
           _ret=None):
    nc = _get_nc((_depth, _do_ffn, _dev_out, _dbg_spill))
    common = {
        "W_in": np.ascontiguousarray(L0_attn_W_in),
        "conv_w": np.ascontiguousarray(L0_attn_conv_w),
        "conv_b": np.ascontiguousarray(L0_attn_conv_b),
        "W_x": np.ascontiguousarray(L0_attn_W_x),
        "W_dt": np.ascontiguousarray(L0_attn_W_dt),
        "b_dt": np.ascontiguousarray(L0_attn_b_dt),
        "A_log": np.ascontiguousarray(L0_attn_A_log),
        "Dvec": np.ascontiguousarray(L0_attn_D),
        "W_out": np.ascontiguousarray(L0_attn_W_out),
        "rw": np.ascontiguousarray(L0_ffn_router_w),
        "ebias": np.ascontiguousarray(L0_ffn_expert_bias.reshape(1, NE)),
        "w1": np.ascontiguousarray(L0_ffn_w1),
        "w2": np.ascontiguousarray(L0_ffn_w2),
        "w3": np.ascontiguousarray(L0_ffn_w3),
        "sw1": np.ascontiguousarray(L0_ffn_sw1),
        "sw2": np.ascontiguousarray(L0_ffn_sw2),
        "sw3": np.ascontiguousarray(L0_ffn_sw3),
        "ident": np.eye(128, dtype=np.float32),
    }
    in_maps = []
    for b in range(_cores):
        m = dict(common)
        m["x"] = np.ascontiguousarray(L0_attn_x[b])
        in_maps.append(m)
    res = run_bass_kernel_spmd(nc, in_maps, core_ids=list(range(_cores)))
    if _ret is not None:
        return res
    return np.stack([res.results[b]["out"] for b in range(_cores)], axis=0)


# version 04
# speedup vs baseline: 1.2137x; 1.2137x over previous
import numpy as np
from contextlib import ExitStack

import concourse.bass as bass
import concourse.mybir as mybir
from concourse.bass_utils import run_bass_kernel_spmd

F32 = mybir.dt.float32
F32R = mybir.dt.float32r
BF16 = mybir.dt.bfloat16
AF = mybir.ActivationFunctionType
OP = mybir.AluOpType
AX = mybir.AxisListType

L, D, D2, E4, NST, DTR, NE, NL = 4096, 1024, 2048, 4096, 16, 64, 8, 5
T1, T2, TF, NQ = 512, 256, 512, 4
NC1, NC2, LQ = L // T1, L // T2, L // NQ


class Ctr:
    def __init__(self, sem, name):
        self.sem, self.name, self.n = sem, name, 0


class Prog:
    ENGINES = ("sync", "scalar", "vector", "tensor", "gpsimd")

    def __init__(self):
        self.ops = []

    def emit(self, engine, fn, waits=(), inc=None, inc16=None):
        ws = [(c.sem, v) for (c, v) in waits if v > 0]
        incs = []
        if inc is not None:
            inc.n += 1
            incs.append((inc.sem, 1))
        if inc16 is not None:
            inc16.n += 16
            incs.append((inc16.sem, 16))
        self.ops.append((engine, fn, ws, incs))

    def build(self, block):
        by = {e: [] for e in self.ENGINES}
        for op in self.ops:
            by[op[0]].append(op)

        def mk(name):
            def body(eng):
                waited = {}
                for _, fn, ws, incs in by[name]:
                    for sem, val in ws:
                        k = id(sem)
                        if waited.get(k, -1) < val:
                            eng.wait_ge(sem, val)
                            waited[k] = val
                    instr = fn(eng)
                    for sem, kk in incs:
                        instr.then_inc(sem, kk)
            return body

        for name in self.ENGINES:
            if by[name]:
                getattr(block, name)(mk(name))


def build(depth=NL, do_ffn=True, dev_out=None, dbg_spill=False, skip_last_p2=False):
    nc = bass.Bass("TRN2", target_bir_lowering=False)
    x_in = nc.dram_tensor("x", [L, D], F32, kind="ExternalInput").ap()
    Wi_h = nc.dram_tensor("W_in", [NL, E4, D], F32, kind="ExternalInput").ap()
    cw_h = nc.dram_tensor("conv_w", [NL, D2, 4], F32, kind="ExternalInput").ap()
    cb_h = nc.dram_tensor("conv_b", [NL, D2], F32, kind="ExternalInput").ap()
    Wx_h = nc.dram_tensor("W_x", [NL, 96, D2], F32, kind="ExternalInput").ap()
    Wdt_h = nc.dram_tensor("W_dt", [NL, D2, DTR], F32, kind="ExternalInput").ap()
    bdt_h = nc.dram_tensor("b_dt", [NL, D2], F32, kind="ExternalInput").ap()
    Al_h = nc.dram_tensor("A_log", [NL, D2, NST], F32, kind="ExternalInput").ap()
    Dp_h = nc.dram_tensor("Dvec", [NL, D2], F32, kind="ExternalInput").ap()
    Wo_h = nc.dram_tensor("W_out", [NL, D, D2], F32, kind="ExternalInput").ap()
    rw_h = nc.dram_tensor("rw", [D, NE], F32, kind="ExternalInput").ap()
    eb_h = nc.dram_tensor("ebias", [1, NE], F32, kind="ExternalInput").ap()
    w1_h = nc.dram_tensor("w1", [NE, D, D], F32, kind="ExternalInput").ap()
    w2_h = nc.dram_tensor("w2", [NE, D, D], F32, kind="ExternalInput").ap()
    w3_h = nc.dram_tensor("w3", [NE, D, D], F32, kind="ExternalInput").ap()
    sw1_h = nc.dram_tensor("sw1", [D, D], F32, kind="ExternalInput").ap()
    sw2_h = nc.dram_tensor("sw2", [D, D], F32, kind="ExternalInput").ap()
    sw3_h = nc.dram_tensor("sw3", [D, D], F32, kind="ExternalInput").ap()
    id_h = nc.dram_tensor("ident", [128, 128], F32, kind="ExternalInput").ap()
    out_h = nc.dram_tensor("out", [L, D], F32, kind="ExternalOutput").ap()

    xk = "ExternalOutput" if dbg_spill else "Internal"
    xnT_h = nc.dram_tensor("xnT", [128, 8, L], F32, kind=xk).ap()
    xT0_h = nc.dram_tensor("xT0", [128, 8, L], F32, kind=xk).ap()
    xT1_h = nc.dram_tensor("xT1", [128, 8, L], F32, kind=xk).ap()
    spill_kind = "ExternalOutput" if dbg_spill else "Internal"
    xa_h = nc.dram_tensor("xa_s", [128, 16, L], BF16, kind=spill_kind).ap()
    zs_h = nc.dram_tensor("zs_s", [128, 16, L], F32, kind=spill_kind).ap()
    dbgd_h = (nc.dram_tensor("dbg_del", [128, 16, L], F32, kind="ExternalOutput").ap()
              if dbg_spill else None)
    dbgy_h = (nc.dram_tensor("dbg_yac", [128, 16, L], F32, kind="ExternalOutput").ap()
              if dbg_spill else None)
    dbgg_h = (nc.dram_tensor("dbg_yg", [128, 16, L], mybir.dt.bfloat16,
                             kind="ExternalOutput").ap()
              if dbg_spill else None)
    bcs_h = nc.dram_tensor("bcs", [32, L], F32).ap()
    weH_h = nc.dram_tensor("weH", [NE, L], F32, kind=spill_kind).ap()

    p = Prog()
    ctx = ExitStack()
    with ctx:
        def mkc(name):
            return Ctr(ctx.enter_context(nc.semaphore(name)), name)

        sP, sA, sV = mkc("sP"), mkc("sA"), mkc("sV")
        sG = mkc("sG")
        cId = mkc("cId")
        cLNx = [mkc("cLNx%d" % i) for i in range(2)]
        cLNs = [mkc("cLNs%d" % i) for i in range(2)]
        cW1 = [mkc("cW1_%d" % i) for i in range(2)]
        cC1 = mkc("cC1")
        cXC = [mkc("cXC%d" % i) for i in range(2)]
        cZS = [mkc("cZS%d" % i) for i in range(2)]
        cXA = [mkc("cXA%d" % i) for i in range(2)]
        cW2 = mkc("cW2")
        cC2 = mkc("cC2")
        cXAB = mkc("cXAB")
        cXR = [mkc("cXR%d" % i) for i in range(2)]
        cZL = mkc("cZL")
        cBW = mkc("cBW")
        cBC = [mkc("cBC%d" % i) for i in range(4)]
        cDS = [mkc("cDS%d" % i) for i in range(2)]
        cRW = mkc("cRW")
        cXQ = [mkc("cXQ%d" % i) for i in range(2)]
        cWE = mkc("cWE")
        cXN = mkc("cXN")
        cFW = [mkc("cFW%d" % i) for i in range(2)]
        cWB = [mkc("cWB%d" % i) for i in range(2)]
        cX5 = mkc("cX5")
        cX0 = [mkc("cX0_%d" % i) for i in range(2)]
        cOU = [mkc("cOU%d" % i) for i in range(2)]
        all_dma_ctrs = ([cId, cC1, cW2, cC2, cXAB, cZL, cBW, cRW, cWE, cXN, cX5]
                        + cLNx + cLNs + cW1 + cXC + cZS + cXA + cXR + cBC + cDS
                        + cXQ + cFW + cWB + cX0 + cOU)

        def sbt(c, name, shape, dt):
            return c.enter_context(nc.sbuf_tensor(name, shape, dt)).ap()

        def pst(c, name, shape, dt=F32):
            return c.enter_context(nc.psum_tensor(name, shape, dt)).ap()

        id32 = sbt(ctx, "id32", [128, 128], F32)
        id16 = sbt(ctx, "id16", [128, 128], BF16)
        idr = sbt(ctx, "idr", [128, 128], F32R)

        p.emit("sync", lambda e: e.dma_start(out=id32, in_=id_h), inc16=cId)
        p.emit("vector", lambda e: e.tensor_copy(id16, id32), waits=[(cId, 16)])
        p.emit("vector", lambda e: e.tensor_copy(idr, id32), inc=sV)

        fl = lambda ap: ap.rearrange("p j t -> p (j t)")

        with ExitStack() as lc:
            xt = [sbt(lc, "ln_x%d" % i, [128, D], F32) for i in range(2)]
            x2t = [sbt(lc, "ln_x2%d" % i, [128, D], F32) for i in range(2)]
            st = [sbt(lc, "ln_st%d" % i, [128, 8], F32) for i in range(2)]
            xnt = [sbt(lc, "ln_xn%d" % i, [128, D], F32R) for i in range(2)]
            stg = [sbt(lc, "ln_stg%d" % i, [128, D], F32) for i in range(2)]
            ps_ln = [pst(lc, "ln_ps%d" % i, [128, D], F32R) for i in range(2)]
            g_x = [0, 0]; g_x2 = [0, 0]; g_st = [0, 0]; g_xn = [0, 0]
            g_ps = [0, 0]; g_stg = [0, 0]
            for tt in range(32):
                pr = tt % 2
                p.emit("sync", lambda e, b=xt[pr], r=x_in[tt * 128:(tt + 1) * 128, :]:
                       e.dma_start(out=b, in_=r), waits=[(sV, g_x[pr])], inc16=cLNx[pr])
                ld = cLNx[pr].n
                p.emit("scalar", lambda e, b=xt[pr], o=x2t[pr]: e.activation(o, b, AF.Square),
                       waits=[(cLNx[pr], ld), (sV, g_x2[pr])], inc=sA)
                sq = sA.n
                s_ = st[pr]
                p.emit("vector", lambda e, b=xt[pr], s=s_: e.reduce_sum(s[:, 0:1], b, AX.X),
                       waits=[(cLNx[pr], ld), (sA, g_st[pr])], inc=sV)
                r1n = sV.n
                p.emit("vector", lambda e, o=x2t[pr], s=s_: e.reduce_sum(s[:, 1:2], o, AX.X),
                       waits=[(sA, sq)], inc=sV)
                g_x2[pr] = sV.n
                p.emit("vector", lambda e, s=s_: e.tensor_scalar(
                    s[:, 2:3], s[:, 0:1], 1.0 / D, None, OP.mult),
                    waits=[(sV, r1n)], inc=sV)
                p.emit("vector", lambda e, s=s_: e.tensor_tensor(
                    s[:, 3:4], s[:, 2:3], s[:, 2:3], OP.mult),
                    waits=[(sV, sV.n)], inc=sV)
                p.emit("vector", lambda e, s=s_: e.scalar_tensor_tensor(
                    s[:, 4:5], s[:, 1:2], 1.0 / D, s[:, 3:4], OP.mult, OP.subtract),
                    waits=[(sV, sV.n)], inc=sV)
                p.emit("vector", lambda e, s=s_: e.tensor_scalar(
                    s[:, 4:5], s[:, 4:5], 1e-5, None, OP.add),
                    waits=[(sV, sV.n)], inc=sV)
                var = sV.n
                p.emit("scalar", lambda e, s=s_: e.activation(
                    s[:, 5:6], s[:, 4:5], AF.Ln), waits=[(sV, var)], inc=sA)
                p.emit("vector", lambda e, s=s_: e.tensor_scalar(
                    s[:, 5:6], s[:, 5:6], -0.5, None, OP.mult), waits=[(sA, sA.n)], inc=sV)
                p.emit("scalar", lambda e, s=s_: e.activation(
                    s[:, 6:7], s[:, 5:6], AF.Exp), waits=[(sV, sV.n)], inc=sA)
                rst = sA.n
                p.emit("vector", lambda e, b=xt[pr], s=s_, o=xnt[pr]: e.tensor_scalar(
                    o, b, s[:, 2:3], s[:, 6:7], OP.subtract, OP.mult),
                    waits=[(sA, rst), (sP, g_xn[pr])], inc=sV)
                xn_done = sV.n
                g_x[pr] = sV.n
                g_st[pr] = sA.n

                def tr(e, o=ps_ln[pr], i_=xnt[pr]):
                    for dj in range(8):
                        ins = e.transpose(o[:, dj * 128:(dj + 1) * 128],
                                          i_[:, dj * 128:(dj + 1) * 128], idr)
                    return ins
                p.emit("tensor", tr, waits=[(sV, xn_done), (sV, g_ps[pr])], inc=sP)
                g_xn[pr] = sP.n
                p.emit("vector", lambda e, ps=ps_ln[pr], o=stg[pr]: e.tensor_copy(o, ps),
                       waits=[(sP, sP.n), (cLNs[pr], g_stg[pr])], inc=sV)
                g_ps[pr] = sV.n
                p.emit("sync", lambda e, o=xnT_h[:, :, tt * 128:(tt + 1) * 128], i_=stg[pr]:
                       e.dma_start(out=o, in_=i_.rearrange("p (j t) -> p j t", j=8)),
                       waits=[(sV, sV.n)], inc16=cLNs[pr])
                g_stg[pr] = cLNs[pr].n

        srcs = [xnT_h, xT0_h, xT1_h, xT0_h, xT1_h]
        dsts = [xT0_h, xT1_h, xT0_h, xT1_h, xT0_h]

        for li in range(depth):
            src, dst = srcs[li], dsts[li]
            barrier = (sP.n, sA.n, sV.n)
            bw = [(sP, barrier[0]), (sA, barrier[1]), (sV, barrier[2])]
            src_w = ([(c, c.n) for c in cLNs] if li == 0
                     else [(c, c.n) for c in cDS])
            spill_w = [(c, c.n) for c in cZS + cXA]

            with ExitStack() as lc:
                WiT = sbt(lc, "WiT_L%d" % li, [128, 8, E4], BF16)
                stg16 = [sbt(lc, "stg16_%d_L%d" % (i, li), [128, 2048], BF16) for i in range(2)]
                xcbf = [sbt(lc, "xcbf%d_L%d" % (i, li), [128, 8, T1], BF16) for i in range(2)]
                xim = sbt(lc, "xim_L%d" % li, [128, 16, T1 + 3], F32R)
                convb = sbt(lc, "convb_L%d" % li, [128, 16, T1], F32)
                xat = [sbt(lc, "xat%d_L%d" % (i, li), [128, T1], BF16) for i in range(2)]
                zst = [sbt(lc, "zst%d_L%d" % (i, li), [128, T1], F32R) for i in range(2)]
                cwsb = sbt(lc, "cwsb_L%d" % li, [128, 16, 4], F32)
                cbv = sbt(lc, "cbv_L%d" % li, [128, 16], F32)
                v16 = sbt(lc, "v16_L%d" % li, [16, 128], F32)
                ps_pr = [pst(lc, "p1pr%d_L%d" % (i, li), [128, 1024], BF16) for i in range(2)]
                ps_mm = [pst(lc, "p1mm%d_L%d" % (i, li), [128, T1]) for i in range(2)]
                ps_v = pst(lc, "p1v_L%d" % li, [128, 512])

                g_stg = [0, 0]; g_ps = [0, 0]
                for ebk in range(32):
                    pr = ebk % 2
                    p.emit("gpsimd", lambda e, o=stg16[pr],
                           i_=Wi_h[li, ebk * 128:(ebk + 1) * 128, :]:
                           e.dma_start(out=o[:, 0:1024], in_=i_),
                           waits=bw + src_w + [(sP, g_stg[pr]), (cBW, cBW.n)],
                           inc16=cW1[pr])
                    ld = cW1[pr].n

                    def trw(e, ps=ps_pr[pr], s=stg16[pr]):
                        for dj in range(8):
                            ins = e.transpose(ps[:, dj * 128:(dj + 1) * 128],
                                              s[:, dj * 128:(dj + 1) * 128], id16)
                        return ins
                    p.emit("tensor", trw, waits=bw + [(cW1[pr], ld), (sV, g_ps[pr])], inc=sP)
                    g_stg[pr] = sP.n
                    p.emit("vector", lambda e, o=WiT[:, :, ebk * 128:(ebk + 1) * 128],
                           ps=ps_pr[pr]:
                           e.tensor_copy(o, ps.rearrange("p (j t) -> p j t", j=8)),
                           waits=bw + [(sP, sP.n)], inc=sV)
                    g_ps[pr] = sV.n
                p.emit("sync", lambda e, o=cwsb,
                       i_=cw_h[li].rearrange("(j p) k -> p j k", p=128):
                       e.dma_start(out=o, in_=i_),
                       waits=bw + src_w + [(cBW, cBW.n)], inc16=cC1)
                p.emit("sync", lambda e, o=v16,
                       i_=cb_h[li].rearrange("(j p) -> j p", p=128):
                       e.dma_start(out=o, in_=i_),
                       waits=bw + src_w + [(cBW, cBW.n)], inc16=cC1)
                ldcb = cC1.n
                p.emit("tensor", lambda e: e.transpose(ps_v[:, 0:16], v16, id32[0:16, 0:16]),
                       waits=bw + [(cC1, ldcb)], inc=sP)
                p.emit("vector", lambda e: e.tensor_copy(cbv, ps_v[:, 0:16]),
                       waits=[(sP, sP.n)], inc=sV)
                for j in range(16):
                    p.emit("vector", lambda e, j=j: e.memset(xim[:, j, 0:3].bitcast(F32), 0.0),
                           inc=(sV if j == 15 else None))
                wit_done = sV.n
                ldcw = cC1.n

                g_xc = [0, 0]; g_ps1 = [0, 0]
                g_xat = [0, 0]; g_zst = [0, 0]
                g_cv = 0
                evac_last = 0
                for c in range(NC1):
                    pr = c % 2
                    p.emit("gpsimd", lambda e, o=xcbf[pr],
                           i_=src[:, :, c * T1:(c + 1) * T1]:
                           e.dma_start(out=o, in_=i_),
                           waits=bw + src_w + [(sP, g_xc[pr])], inc16=cXC[pr])
                    ldx = cXC[pr].n
                    if c > 0:
                        p.emit("vector", lambda e: e.tensor_copy(
                            xim[:, :, 0:3], xim[:, :, T1:T1 + 3]),
                            waits=[(sA, evac_last)], inc=sV)
                    ms = sV.n
                    evac0 = sA.n
                    for ebk in range(32):
                        pp = ebk % 2

                        def mm(e, pp=pp, ebk=ebk, pr=pr):
                            for dk in range(8):
                                ins = e.matmul(ps_mm[pp][:, 0:T1],
                                               WiT[:, dk, ebk * 128:(ebk + 1) * 128],
                                               xcbf[pr][:, dk, :],
                                               start=(dk == 0), stop=(dk == 7))
                            return ins
                        p.emit("tensor", mm,
                               waits=bw + [(cXC[pr], ldx), (sA, g_ps1[pp]),
                                           (sV, wit_done)], inc=sP)
                        mmn = sP.n
                        if ebk < 16:
                            p.emit("scalar", lambda e, o=xim[:, ebk, 3:3 + T1], ps=ps_mm[pp]:
                                   e.activation(o, ps[:, 0:T1], AF.Copy),
                                   waits=[(sP, mmn), (sV, ms)], inc=sA)
                        else:
                            zpr = ebk % 2
                            p.emit("scalar", lambda e, ps=ps_mm[pp], o=zst[zpr]:
                                   e.activation(o, ps[:, 0:T1], AF.Silu),
                                   waits=[(sP, mmn), (cZS[zpr], g_zst[zpr])], inc=sA)
                            p.emit("scalar", lambda e, o=zs_h[:, ebk - 16, c * T1:(c + 1) * T1],
                                   i_=zst[zpr].bitcast(F32): e.dma_start(out=o, in_=i_),
                                   waits=[(sA, sA.n)], inc16=cZS[zpr])
                            g_zst[zpr] = cZS[zpr].n
                        g_ps1[pp] = sA.n
                    evac_last = sA.n
                    g_xc[pr] = sP.n
                    for k in range(4):
                        for j in range(16):
                            evj = evac0 + j + 1
                            if k == 0:
                                p.emit("vector", lambda e, j=j: e.tensor_scalar(
                                    convb[:, j, :], xim[:, j, 0:T1],
                                    cwsb[:, j:j + 1, 0:1].squeeze(1),
                                    cbv[:, j:j + 1], OP.mult, OP.add),
                                    waits=[(sA, evj), (cC1, ldcw), (sV, wit_done),
                                           (sA, g_cv)])
                            else:
                                p.emit("vector", lambda e, j=j, k=k:
                                       e.scalar_tensor_tensor(
                                           convb[:, j, :], xim[:, j, k:k + T1],
                                           cwsb[:, j:j + 1, k:k + 1].squeeze(1),
                                           convb[:, j, :], OP.mult, OP.add),
                                       waits=[(sA, evj)],
                                       inc=(sV if (k == 3 and j == 15) else None))
                    cv_done = sV.n
                    for j in range(16):
                        apr = j % 2
                        p.emit("scalar", lambda e, j=j, o=xat[apr]:
                               e.activation(o, convb[:, j, :], AF.Silu),
                               waits=[(sV, cv_done), (cXA[apr], g_xat[apr])], inc=sA)
                        p.emit("scalar", lambda e, o=xa_h[:, j, c * T1:(c + 1) * T1],
                               i_=xat[apr]: e.dma_start(out=o, in_=i_),
                               waits=[(sA, sA.n)], inc16=cXA[apr])
                        g_xat[apr] = cXA[apr].n
                    g_cv = sA.n

            if skip_last_p2 and li == depth - 1:
                continue
            barrier = (sP.n, sA.n, sV.n)
            bw = [(sP, barrier[0]), (sA, barrier[1]), (sV, barrier[2])]
            spill_w = [(c, c.n) for c in cZS + cXA]
            with ExitStack() as lc:
                WoT = sbt(lc, "WoT_L%d" % li, [128, 16, D], BF16)
                WxT = sbt(lc, "WxT_L%d" % li, [128, 16, 96], BF16)
                WdtT = sbt(lc, "WdtT_L%d" % li, [64, D2], F32R)
                Asb = sbt(lc, "Asb_L%d" % li, [128, 16, NST], F32)
                Dpv = sbt(lc, "Dpv_L%d" % li, [128, 16], F32)
                bdtv = sbt(lc, "bdtv_L%d" % li, [128, 16], F32)
                v16b = sbt(lc, "v16b_L%d" % li, [16, 128], F32)
                state = sbt(lc, "state_L%d" % li, [128, 256], F32)

                with ExitStack() as pc:
                    stg16b = sbt(pc, "stg16b_L%d" % li, [128, 2048], BF16)
                    stgr = sbt(pc, "stgr_L%d" % li, [128, 2048], F32R)
                    ps_prb = pst(pc, "p2prb_L%d" % li, [128, 2048], BF16)
                    ps_prf = pst(pc, "p2prf_L%d" % li, [128, 512])
                    ps_prr = pst(pc, "p2prr_L%d" % li, [128, 512], F32R)
                    g_stg = 0
                    g_ps = 0
                    for ob in range(8):
                        p.emit("gpsimd", lambda e, o=stg16b,
                               i_=Wo_h[li, ob * 128:(ob + 1) * 128, :]:
                               e.dma_start(out=o, in_=i_),
                               waits=bw + spill_w + [(sP, g_stg)], inc16=cW2)
                        ld = cW2.n

                        def trw(e):
                            for dk in range(16):
                                ins = e.transpose(ps_prb[:, dk * 128:(dk + 1) * 128],
                                                  stg16b[:, dk * 128:(dk + 1) * 128], id16)
                            return ins
                        p.emit("tensor", trw, waits=bw + [(cW2, ld), (sV, g_ps)], inc=sP)
                        g_stg = sP.n
                        p.emit("vector", lambda e, o=WoT[:, :, ob * 128:(ob + 1) * 128]:
                               e.tensor_copy(o, ps_prb.rearrange("p (j t) -> p j t", j=16)),
                               waits=bw + [(sP, sP.n)], inc=sV)
                        g_ps = sV.n
                    p.emit("gpsimd", lambda e, o=stg16b[0:96, :], i_=Wx_h[li]:
                           e.dma_start(out=o, in_=i_), waits=[(sV, g_ps)], inc16=cW2)
                    ldwx = cW2.n
                    for dk in range(16):
                        p.emit("tensor", lambda e, dk=dk: e.transpose(
                            ps_prb[:, 0:96], stg16b[0:96, dk * 128:(dk + 1) * 128],
                            id16[0:96, 0:96]), waits=[(cW2, ldwx), (sV, sV.n)], inc=sP)
                        p.emit("vector", lambda e, dk=dk: e.tensor_copy(
                            WxT[:, dk, :], ps_prb[:, 0:96]), waits=[(sP, sP.n)], inc=sV)
                    p.emit("gpsimd", lambda e,
                           o=stgr[:, 0:1024].rearrange("p (dk r) -> p dk r", dk=16),
                           i_=Wdt_h[li].rearrange("(dk p) r -> p dk r", p=128):
                           e.dma_start(out=o, in_=i_), waits=[(sV, sV.n)], inc16=cW2)
                    ldwd = cW2.n
                    for dk in range(16):
                        p.emit("tensor", lambda e, dk=dk: e.transpose(
                            ps_prr[0:64, 0:128], stgr[:, dk * 64:(dk + 1) * 64], idr),
                            waits=[(cW2, ldwd), (sV, sV.n)], inc=sP)
                        p.emit("vector", lambda e, dk=dk: e.tensor_copy(
                            WdtT[:, dk * 128:(dk + 1) * 128], ps_prr[0:64, 0:128]),
                            waits=[(sP, sP.n)], inc=sV)
                    p.emit("sync", lambda e, o=Asb,
                           i_=Al_h[li].rearrange("(j p) n -> p j n", p=128):
                           e.dma_start(out=o, in_=i_),
                           waits=bw + spill_w, inc16=cC2)
                    ldA = cC2.n
                    p.emit("scalar", lambda e: e.activation(fl(Asb), fl(Asb), AF.Exp),
                           waits=bw + [(cC2, ldA)], inc=sA)
                    p.emit("vector", lambda e: e.tensor_scalar(
                        fl(Asb), fl(Asb), -1.0, None, OP.mult), waits=[(sA, sA.n)], inc=sV)
                    for vec_h, vdst in ((Dp_h[li], Dpv), (bdt_h[li], bdtv)):
                        p.emit("sync", lambda e, i_=vec_h.rearrange("(j p) -> j p", p=128):
                               e.dma_start(out=v16b, in_=i_),
                               waits=spill_w + [(sP, sP.n)], inc16=cC2)
                        ldv = cC2.n
                        p.emit("tensor", lambda e: e.transpose(
                            ps_prf[:, 0:16], v16b, id32[0:16, 0:16]),
                            waits=[(cC2, ldv), (sV, sV.n)], inc=sP)
                        p.emit("vector", lambda e, o=vdst: e.tensor_copy(o, ps_prf[:, 0:16]),
                               waits=[(sP, sP.n)], inc=sV)
                    p.emit("vector", lambda e: e.memset(state, 0.0), inc=sV)
                prep_done = sV.n

                xab = sbt(lc, "xab_L%d" % li, [128, 16, T2], BF16)
                delb = sbt(lc, "delb_L%d" % li, [128, 16, T2], F32)
                dub = sbt(lc, "dub_L%d" % li, [128, 16, T2], F32)
                dAb = [sbt(lc, "dAb%d_L%d" % (i, li), [128, 16, T2], F32) for i in range(2)]
                dBuA = [sbt(lc, "dBuA%d_L%d" % (i, li), [128, 16, T2], F32) for i in range(2)]
                dBuB = sbt(lc, "dBuB_L%d" % li, [128, 16, T2], F32)
                yac = sbt(lc, "yac_L%d" % li, [128, 16, T2], F32)
                ygbf = dBuA[1].bitcast(BF16)[:, :, 0:T2]
                xres = sbt(lc, "xres_L%d" % li, [128, 8, T2], F32)
                dtb = sbt(lc, "dtb_L%d" % li, [64, T2], F32R)
                bcb = sbt(lc, "bcb_L%d" % li, [32, T2], F32)
                bcc = [sbt(lc, "bcc%d_L%d" % (i, li), [128, 2, T2], F32) for i in range(4)]
                ps_xd = pst(lc, "p2xd_L%d" % li, [128, 512])
                ps_del = [pst(lc, "p2dl%d_L%d" % (i, li), [128, 512]) for i in range(2)]
                ps_wo = [pst(lc, "p2wo%d_L%d" % (i, li), [128, 512]) for i in range(2)]

                JD = 10
                g_xab = 0
                g_xres = 0
                g_xd = 0
                g_del = [0, 0]
                g_wo = [0, 0]
                g_bcb = 0
                g_scanA = [0, 0]
                g_m1 = [0, 0]
                g_exp = [0, 0]
                g_hV = [0, 0]
                g_hG = [0, 0]
                g_warA = [[], []]
                g_bccV = [0, 0, 0, 0]
                g_bccG = [0, 0, 0, 0]
                bcc_ld = [0, 0, 0, 0]
                g_yaddD = 0
                g_yaddP = 0
                g_memset = 0
                g_skp = 0
                g_yg = 0
                for c in range(NC2):
                    sl = slice(c * T2, (c + 1) * T2)
                    p.emit("sync", lambda e, i_=xa_h[:, :, sl]: e.dma_start(out=xab, in_=i_),
                           waits=bw + spill_w + [(sV, g_xab)], inc16=cXAB)
                    ldxa = cXAB.n
                    p.emit("scalar", lambda e, o=xres, i_=src[:, :, sl]: e.dma_start(
                        out=o, in_=i_),
                        waits=bw + src_w + [(cDS[0], g_xres)], inc16=cXR[0])
                    ldxr = cXR[0].n

                    def mmx(e):
                        for dk in range(16):
                            ins = e.matmul(ps_xd[0:96, 0:T2], WxT[:, dk, :], xab[:, dk, :],
                                           start=(dk == 0), stop=(dk == 15))
                        return ins
                    p.emit("tensor", mmx, waits=[(cXAB, ldxa), (sV, g_xd), (sV, prep_done)],
                           inc=sP)
                    mmxn = sP.n
                    p.emit("vector", lambda e: e.tensor_copy(dtb, ps_xd[0:64, 0:T2]),
                           waits=[(sP, mmxn), (cBW, g_bcb)])
                    p.emit("vector", lambda e: e.tensor_copy(bcb, ps_xd[64:96, 0:T2]), inc=sV)
                    g_xd = sV.n
                    bc_ev = sV.n
                    p.emit("sync", lambda e, o=bcs_h[:, sl]: e.dma_start(out=o, in_=bcb),
                           waits=[(sV, bc_ev)], inc16=cBW)
                    bcw = cBW.n
                    g_bcb = cBW.n
                    for dk in range(16):
                        pp = dk % 2
                        p.emit("tensor", lambda e, dk=dk, pp=pp: e.matmul(
                            ps_del[pp][:, 0:T2], WdtT[:, dk * 128:(dk + 1) * 128], dtb,
                            start=True, stop=True),
                            waits=[(sV, bc_ev), (sV, g_del[pp])], inc=sP)
                        p.emit("vector", lambda e, dk=dk, pp=pp: e.tensor_scalar(
                            delb[:, dk, :], ps_del[pp][:, 0:T2], bdtv[:, dk:dk + 1],
                            None, OP.add),
                            waits=[(sP, sP.n), (sV, prep_done),
                                   (sA, g_exp[0]), (sA, g_exp[1])], inc=sV)
                        g_del[pp] = sV.n
                    del_done = sV.n
                    p.emit("vector", lambda e: e.tensor_scalar(
                        fl(dBuA[0]), fl(delb), -1.0, None, OP.mult),
                        waits=[(sV, del_done), (sV, g_yg), (cZL, cZL.n)] + g_warA[0],
                        inc=sV)
                    p.emit("vector", lambda e: e.tensor_tensor(
                        fl(dBuA[0]), fl(delb), fl(dBuA[0]), OP.min),
                        waits=[(sV, sV.n)], inc=sV)
                    p.emit("scalar", lambda e: e.activation(fl(dAb[0]), fl(dBuA[0]), AF.Exp),
                           waits=[(sV, sV.n), (sV, g_yg), (sG, g_hG[0])])
                    p.emit("scalar", lambda e: e.activation(fl(dBuA[0]), fl(dAb[0]), AF.Ln,
                                                            bias=1.0), inc=sA)
                    p.emit("vector", lambda e: e.scalar_tensor_tensor(
                        fl(delb), fl(delb), 0.0, fl(dBuA[0]), OP.max, OP.add),
                        waits=[(sA, sA.n)], inc=sV)
                    p.emit("vector", lambda e: e.tensor_tensor(fl(dub), fl(delb), fl(xab),
                                                               OP.mult),
                           waits=[(sV, sV.n), (sG, g_m1[0]), (sG, g_m1[1])], inc=sV)
                    sp_done = sV.n
                    p.emit("vector", lambda e: e.memset(fl(yac), 0.0),
                           waits=[(sV, g_skp)], inc=sV)
                    g_memset = sV.n
                    for q in range(3):
                        row0 = bcs_h[q:q + 17:16, sl]
                        p.emit("sync", lambda e, o=bcc[q], r=row0: e.dma_start(
                            out=o, in_=r.unsqueeze(0).broadcast_to((128, 2, T2))),
                            waits=[(cBW, bcw), (sV, g_bccV[q]), (sG, g_bccG[q])],
                            inc16=cBC[q])
                        bcc_ld[q] = cBC[q].n
                    p.emit("gpsimd", lambda e: e.tensor_tensor(
                        dBuA[0], dub,
                        bcc[0][:, 0:1, :].broadcast_to((128, 16, T2)), OP.mult),
                        waits=[(sV, sp_done), (cBC[0], bcc_ld[0]),
                               (sV, g_scanA[0])] + g_warA[0], inc=sG)
                    g_m1[0] = sG.n
                    g_warA[0] = []
                    g_bccG[0] = sG.n

                    for n in range(NST):
                        bp = n % 2
                        q = n % 4

                        def dAe(e, n=n, bp=bp):
                            for j in range(16):
                                ins = e.activation(dAb[bp][:, j, :], delb[:, j, :], AF.Exp,
                                                   scale=Asb[:, j:j + 1, n:n + 1].squeeze(1))
                            return ins
                        p.emit("scalar", dAe, waits=[(sV, sp_done), (sV, prep_done),
                                                     (sV, g_hV[bp]), (sG, g_hG[bp])],
                               inc=sA)
                        g_exp[bp] = sA.n
                        if n < NST - 1:
                            bp1 = (n + 1) % 2
                            q1 = (n + 1) % 4
                            p.emit("gpsimd", lambda e, bp1=bp1, q1=q1: e.tensor_tensor(
                                dBuA[bp1], dub,
                                bcc[q1][:, 0:1, :].broadcast_to((128, 16, T2)), OP.mult),
                                waits=[(sV, sp_done), (cBC[q1], bcc_ld[q1]),
                                       (sV, g_scanA[bp1])] + g_warA[bp1], inc=sG)
                            g_m1[bp1] = sG.n
                            g_warA[bp1] = []
                            g_bccG[q1] = sG.n

                        def scn(e, n=n, bp=bp):
                            for j in range(16):
                                ins = e.tensor_tensor_scan(
                                    dAb[bp][:, j, :], dAb[bp][:, j, :],
                                    dBuA[bp][:, j, :],
                                    state[:, n * 16 + j:n * 16 + j + 1],
                                    OP.mult, OP.add)
                            return ins
                        p.emit("vector", scn, waits=[(sA, g_exp[bp]), (sG, g_m1[bp])],
                               inc=sV)
                        scn_n = sV.n
                        g_scanA[bp] = sV.n
                        p.emit("vector", lambda e, n=n, bp=bp: e.tensor_copy(
                            state[:, n * 16:(n + 1) * 16],
                            dAb[bp][:, :, T2 - 1:T2].squeeze(2)),
                            waits=[(sV, scn_n)], inc=sV)
                        p.emit("vector", lambda e, bp=bp, q=q: e.tensor_tensor(
                            dBuB[:, 0:JD, :], dAb[bp][:, 0:JD, :],
                            bcc[q][:, 1:2, :].broadcast_to((128, JD, T2)), OP.mult),
                            waits=[(sV, scn_n), (cBC[q], bcc_ld[q]), (sV, g_yaddD)],
                            inc=sV)
                        mD = sV.n
                        p.emit("vector", lambda e: e.tensor_tensor(
                            yac[:, 0:JD, :], yac[:, 0:JD, :], dBuB[:, 0:JD, :], OP.add),
                            waits=[(sV, mD), (sV, g_memset)], inc=sV)
                        g_yaddD = sV.n
                        g_hV[bp] = mD
                        g_bccV[q] = mD
                        p.emit("gpsimd", lambda e, bp=bp, q=q: e.tensor_tensor(
                            dBuB[:, JD:16, :], dAb[bp][:, JD:16, :],
                            bcc[q][:, 1:2, :].broadcast_to((128, 16 - JD, T2)), OP.mult),
                            waits=[(sV, scn_n), (cBC[q], bcc_ld[q]), (sG, g_yaddP)],
                            inc=sG)
                        mP = sG.n
                        p.emit("gpsimd", lambda e: e.tensor_tensor(
                            yac[:, JD:16, :], yac[:, JD:16, :], dBuB[:, JD:16, :], OP.add),
                            waits=[(sG, mP), (sV, g_memset)], inc=sG)
                        g_yaddP = sG.n
                        g_hG[bp] = mP
                        g_bccG[q] = max(g_bccG[q], mP)
                        if n + 3 < NST:
                            q3 = (n + 3) % 4
                            row3 = bcs_h[n + 3:n + 3 + 17:16, sl]
                            p.emit("sync", lambda e, o=bcc[q3], r=row3: e.dma_start(
                                out=o, in_=r.unsqueeze(0).broadcast_to((128, 2, T2))),
                                waits=[(cBW, bcw), (sV, g_bccV[q3]), (sG, g_bccG[q3])],
                                inc16=cBC[q3])
                            bcc_ld[q3] = cBC[q3].n

                    p.emit("scalar", lambda e, i_=zs_h[:, :, sl]: e.dma_start(
                        out=dBuA[0], in_=i_),
                        waits=[(sV, g_scanA[0])] + spill_w, inc16=cZL)
                    ldz = cZL.n

                    def skp(e):
                        for j in range(16):
                            ins = e.scalar_tensor_tensor(dAb[0][:, j, :], xab[:, j, :],
                                                         Dpv[:, j:j + 1], yac[:, j, :],
                                                         OP.mult, OP.add)
                        return ins
                    p.emit("vector", skp, waits=[(sV, g_yaddD), (sG, g_yaddP),
                                                 (sV, g_hV[0]), (sG, g_hG[0])], inc=sV)
                    g_skp = sV.n
                    g_xab = sV.n
                    p.emit("vector", lambda e: e.tensor_tensor(ygbf, dAb[0], dBuA[0],
                                                               OP.mult),
                           waits=[(cZL, ldz), (sV, g_skp), (sV, g_scanA[1])], inc=sV)
                    g_yg = sV.n
                    g_hV[0] = g_yg
                    g_warA[0] = [(sV, g_yg)]
                    for ob in range(8):
                        pp = ob % 2

                        def mwo(e, ob=ob, pp=pp):
                            for dk in range(16):
                                ins = e.matmul(ps_wo[pp][:, 0:T2],
                                               WoT[:, dk, ob * 128:(ob + 1) * 128],
                                               ygbf[:, dk, :], start=(dk == 0),
                                               stop=(dk == 15))
                            return ins
                        p.emit("tensor", mwo, waits=[(sV, g_yg), (sV, g_wo[pp])], inc=sP)
                        won = sP.n
                        p.emit("vector", lambda e, ob=ob, pp=pp: e.tensor_tensor(
                            xres[:, ob, :], xres[:, ob, :], ps_wo[pp][:, 0:T2], OP.add),
                            waits=[(sP, won), (cXR[0], ldxr)], inc=sV)
                        g_wo[pp] = sV.n
                    g_warA[1] = [(sP, sP.n)]
                    p.emit("scalar", lambda e, o=dst[:, :, sl], i_=xres: e.dma_start(
                        out=o, in_=i_), waits=[(sV, sV.n)], inc16=cDS[0])
                    g_xres = cDS[0].n

        mamba_done = (sP.n, sA.n, sV.n)
        mbw = [(sP, mamba_done[0]), (sA, mamba_done[1]), (sV, mamba_done[2])]
        dst_w = [(c, c.n) for c in (cDS if depth > 0 else cLNs)]
        ln_w = [(c, c.n) for c in cLNs]

        def writeback(src_hbm):
            with ExitStack() as lc:
                ps_f = [pst(lc, "wb%d" % i, [128, D]) for i in range(2)]
                ft = [sbt(lc, "wb_f%d" % i, [128, 8, 128], F32) for i in range(2)]
                ot = [sbt(lc, "wb_o%d" % i, [128, D], F32) for i in range(2)]
                g_ft = [0, 0]; g_ps = [0, 0]; g_ot = [0, 0]
                for tt in range(32):
                    pr = tt % 2
                    p.emit("scalar", lambda e, o=ft[pr],
                           i_=src_hbm[:, :, tt * 128:(tt + 1) * 128]:
                           e.dma_start(out=o, in_=i_),
                           waits=mbw + dst_w + [(sP, g_ft[pr])], inc16=cWB[pr])
                    ld = cWB[pr].n

                    def tr(e, pr=pr):
                        for dj in range(8):
                            ins = e.transpose(ps_f[pr][:, dj * 128:(dj + 1) * 128],
                                              ft[pr][:, dj, :], id32)
                        return ins
                    p.emit("tensor", tr, waits=mbw + [(cWB[pr], ld), (sV, g_ps[pr])], inc=sP)
                    g_ft[pr] = sP.n
                    p.emit("vector", lambda e, pr=pr: e.tensor_copy(ot[pr], ps_f[pr]),
                           waits=mbw + [(sP, sP.n), (cOU[pr], g_ot[pr])], inc=sV)
                    g_ps[pr] = sV.n
                    p.emit("sync", lambda e, pr=pr, o=out_h[tt * 128:(tt + 1) * 128, :]:
                           e.dma_start(out=o, in_=ot[pr]), waits=[(sV, sV.n)], inc16=cOU[pr])
                    g_ot[pr] = cOU[pr].n

        if dev_out == "xn":
            writeback(xnT_h)
        elif dev_out == "mamba" or not do_ffn:
            wb_src = dsts[depth - 1] if depth > 0 else xnT_h
            if skip_last_p2 and depth > 1:
                wb_src = dsts[depth - 2]
            writeback(wb_src)
        else:
            with ExitStack() as lc:
                rwsb = sbt(lc, "rwsb", [128, 8, 8], F32R)
                ebb = sbt(lc, "ebb", [128, 8], F32)
                wsc = sbt(lc, "wsc", [128, 8], F32)
                wval = sbt(lc, "wval", [128, 8], F32)
                wmsk = sbt(lc, "wmsk", [128, 8], F32)
                wtmp = sbt(lc, "wtmp", [128, 8], F32)
                m12 = sbt(lc, "m12", [128, 4], F32)
                weT = sbt(lc, "weT", [8, L], F32)
                xnq = [sbt(lc, "xnq%d" % i, [128, 8, TF], F32R) for i in range(2)]
                ps_rt = pst(lc, "rt", [128, 512])
                ps_wt = pst(lc, "wt", [128, 512])
                p.emit("gpsimd", lambda e: e.dma_start(
                    out=rwsb, in_=rw_h.rearrange("(dk p) e -> p dk e", p=128)),
                    waits=mbw + dst_w + [(cBW, cBW.n)], inc16=cRW)
                p.emit("sync", lambda e: e.dma_start(out=ebb,
                                                     in_=eb_h.broadcast_to((128, NE))),
                       waits=mbw + dst_w + [(cBW, cBW.n)], inc16=cRW)
                ld0 = cRW.n
                g_xnq = [0, 0]
                g_rt = 0
                g_wt = 0
                g_weT = 0
                for ch in range(8):
                    pr = ch % 2
                    p.emit("gpsimd", lambda e, o=xnq[pr],
                           i_=xnT_h[:, :, ch * TF:(ch + 1) * TF]: e.dma_start(out=o, in_=i_),
                           waits=mbw + ln_w + dst_w + [(sP, g_xnq[pr]), (cBW, cBW.n)],
                           inc16=cXQ[pr])
                    ldq = cXQ[pr].n
                    for ts_ in range(4):
                        tt = ch * 4 + ts_

                        def mmr(e, pr=pr, ts_=ts_):
                            for dk in range(8):
                                ins = e.matmul(ps_rt[:, 0:8],
                                               xnq[pr][:, dk, ts_ * 128:(ts_ + 1) * 128],
                                               rwsb[:, dk, :], start=(dk == 0),
                                               stop=(dk == 7))
                            return ins
                        p.emit("tensor", mmr,
                               waits=mbw + [(cXQ[pr], ldq), (cRW, ld0), (sV, g_rt)],
                               inc=sP)
                        rtn = sP.n

                        p.emit("vector", lambda e: e.reduce_max(
                            m12[:, 0:1], ps_rt[:, 0:8], AX.X),
                            waits=[(sP, rtn)], inc=sV)
                        p.emit("vector", lambda e: e.tensor_scalar(
                            wsc, ps_rt[:, 0:8], m12[:, 0:1], None, OP.subtract),
                            waits=[(sV, sV.n)], inc=sV)
                        g_rt = sV.n
                        p.emit("scalar", lambda e: e.activation(wsc, wsc, AF.Exp),
                               waits=[(sV, sV.n)], inc=sA)

                        for fn_ in (
                            lambda e: e.reduce_sum(m12[:, 1:2], wsc, AX.X),
                            lambda e: e.reciprocal(m12[:, 2:3], m12[:, 1:2]),
                            lambda e: e.tensor_scalar(wsc, wsc, m12[:, 2:3], None, OP.mult),
                            lambda e: e.tensor_tensor(wval, wsc, ebb, OP.add),
                            lambda e: e.reduce_max(m12[:, 0:1], wval, AX.X),
                            lambda e: e.tensor_scalar(wmsk, wval, m12[:, 0:1], None,
                                                      OP.is_ge),
                            lambda e: e.scalar_tensor_tensor(wtmp, wmsk, -1e30, wval,
                                                             OP.mult, OP.add),
                            lambda e: e.reduce_max(m12[:, 1:2], wtmp, AX.X),
                            lambda e: e.tensor_scalar(wtmp, wtmp, m12[:, 1:2], None,
                                                      OP.is_ge),
                            lambda e: e.tensor_tensor(wmsk, wmsk, wtmp, OP.add),
                            lambda e: e.tensor_tensor(wval, wsc, wmsk, OP.mult),
                        ):
                            p.emit("vector", fn_,
                                   waits=[(sA, sA.n), (sP, g_wt), (sV, sV.n)], inc=sV)
                        wdone = sV.n
                        p.emit("tensor", lambda e: e.transpose(ps_wt[0:8, 0:128], wval, id32),
                               waits=[(sV, wdone), (sV, g_weT)], inc=sP)
                        g_wt = sP.n
                        p.emit("vector", lambda e, o=weT[0:8, tt * 128:(tt + 1) * 128]:
                               e.tensor_copy(o, ps_wt[0:8, 0:128]),
                               waits=[(sP, sP.n)], inc=sV)
                        g_weT = sV.n
                    g_xnq[pr] = sP.n
                p.emit("gpsimd", lambda e: e.dma_start(out=weH_h, in_=weT),
                       waits=[(sV, g_weT)], inc16=cWE)
            we_done = cWE.n
            rbw = [(sP, sP.n), (sA, sA.n), (sV, sV.n)]

            with ExitStack() as fc:
                xnq16 = sbt(fc, "xnq16", [128, 8, LQ], BF16)
                acc = sbt(fc, "acc", [128, 8, LQ], F32)
                s1t = [sbt(fc, "s1t%d" % i, [128, TF], F32) for i in range(2)]
                t1t = [sbt(fc, "t1t%d" % i, [128, TF], F32) for i in range(2)]
                gbf = sbt(fc, "gbf", [128, 8, TF], BF16)
                web = [sbt(fc, "web%d" % i, [128, TF], F32) for i in range(2)]
                for q in range(NQ):
                    qsl = slice(q * LQ, (q + 1) * LQ)
                    with ExitStack() as wc:
                        ps_h1 = [pst(wc, "fh1%d_q%d" % (i, q), [128, 512]) for i in range(2)]
                        ps_h3 = [pst(wc, "fh3%d_q%d" % (i, q), [128, 512]) for i in range(2)]
                        ps_o = [pst(wc, "fo%d_q%d" % (i, q), [128, 512]) for i in range(2)]
                        w1sb = [sbt(wc, "w1sb%d_q%d" % (i, q), [128, 8, D], BF16) for i in range(2)]
                        w3sb = [sbt(wc, "w3sb%d_q%d" % (i, q), [128, 8, D], BF16) for i in range(2)]
                        w2sb = [sbt(wc, "w2sb%d_q%d" % (i, q), [128, 8, D], BF16) for i in range(2)]
                        p.emit("gpsimd", lambda e, o=xnq16, i_=xnT_h[:, :, qsl]:
                               e.dma_start(out=o, in_=i_),
                               waits=rbw + ln_w + [(sP, sP.n), (cWE, we_done)],
                               inc16=cXN)
                        ldxn = cXN.n
                        p.emit("vector", lambda e: e.memset(fl(acc), 0.0),
                               waits=rbw + [(sV, sV.n), (sP, sP.n), (cWE, we_done)],
                               inc=sV)
                        g_w = [0, 0]
                        g_web = [0, 0]
                        g_h1 = [0, 0]
                        g_h3 = [0, 0]
                        g_po = [0, 0]
                        g_s1 = [0, 0]
                        g_gbf = 0
                        ldw = [0, 0]
                        for e_ in range(9):
                            wpr = e_ % 2
                            if e_ < 8:
                                srcs_w = (w1_h[e_], w3_h[e_], w2_h[e_])
                            else:
                                srcs_w = (sw1_h, sw3_h, sw2_h)
                            for wsrc, wdst in zip(srcs_w, (w1sb[wpr], w3sb[wpr], w2sb[wpr])):
                                p.emit("gpsimd", lambda e, o=wdst,
                                       i_=wsrc.rearrange("(dk p) h -> p dk h", p=128):
                                       e.dma_start(out=o, in_=i_),
                                       waits=rbw + [(sP, g_w[wpr]), (cWE, we_done),
                                                    (cOU[0], cOU[0].n),
                                                    (cOU[1], cOU[1].n),
                                                    (cX5, cX5.n)],
                                       inc16=cFW[wpr])
                            ldw[wpr] = cFW[wpr].n
                            for c2 in range(2):
                                csl = slice(c2 * TF, (c2 + 1) * TF)
                                tok0 = q * LQ + c2 * TF
                                if e_ < 8:
                                    p.emit("gpsimd", lambda e, o=web[c2], e_=e_,
                                           r=weH_h[e_:e_ + 1, tok0:tok0 + TF]:
                                           e.dma_start(out=o, in_=r.broadcast_to((128, TF))),
                                           waits=[(cWE, we_done), (sV, g_web[c2])],
                                           inc16=cWB[c2])
                                ldwe = cWB[c2].n
                                for hj in range(8):
                                    hp = hj % 2

                                    def mmh(e, wpr=wpr, hj=hj, hp=hp, cs=csl):
                                        for dk in range(8):
                                            e.matmul(ps_h1[hp][:, 0:TF],
                                                     w1sb[wpr][:, dk, hj * 128:(hj + 1) * 128],
                                                     xnq16[:, dk, cs],
                                                     start=(dk == 0), stop=(dk == 7))
                                        for dk in range(8):
                                            ins = e.matmul(
                                                ps_h3[hp][:, 0:TF],
                                                w3sb[wpr][:, dk, hj * 128:(hj + 1) * 128],
                                                xnq16[:, dk, cs],
                                                start=(dk == 0), stop=(dk == 7))
                                        return ins
                                    p.emit("tensor", mmh,
                                           waits=[(cFW[wpr], ldw[wpr]), (cXN, ldxn),
                                                  (sA, g_h1[hp]), (sV, g_h3[hp])], inc=sP)
                                    hn = sP.n
                                    p.emit("scalar", lambda e, hp=hp, sp=s1t[hj % 2]:
                                           e.activation(sp, ps_h1[hp][:, 0:TF], AF.Silu),
                                           waits=[(sP, hn), (sV, g_s1[hj % 2])], inc=sA)
                                    g_h1[hp] = sA.n

                                    p.emit("vector", lambda e, hj=hj, hp=hp:
                                           e.tensor_tensor(t1t[hj % 2], s1t[hj % 2],
                                                           ps_h3[hp][:, 0:TF], OP.mult),
                                           waits=[(sA, sA.n), (sP, g_gbf)], inc=sV)

                                    def gat(e, hj=hj, e_=e_, c2=c2):
                                        tp = t1t[hj % 2]
                                        if e_ < 8:
                                            ins = e.tensor_tensor(gbf[:, hj, :], tp,
                                                                  web[c2], OP.mult)
                                        else:
                                            ins = e.tensor_copy(gbf[:, hj, :], tp)
                                        return ins
                                    p.emit("vector", gat,
                                           waits=[(cWB[c2], ldwe), (sV, sV.n)], inc=sV)
                                    g_h3[hp] = sV.n
                                    g_s1[hj % 2] = sV.n
                                g_done = sV.n
                                g_web[c2] = sV.n
                                for ob in range(8):
                                    po = ob % 2

                                    def mmo(e, wpr=wpr, ob=ob, po=po):
                                        for hk in range(8):
                                            ins = e.matmul(
                                                ps_o[po][:, 0:TF],
                                                w2sb[wpr][:, hk, ob * 128:(ob + 1) * 128],
                                                gbf[:, hk, :],
                                                start=(hk == 0), stop=(hk == 7))
                                        return ins
                                    p.emit("tensor", mmo,
                                           waits=[(sV, g_done), (sV, g_po[po])], inc=sP)
                                    on = sP.n
                                    p.emit("vector", lambda e, ob=ob, po=po, cs=csl:
                                           e.tensor_tensor(acc[:, ob, cs], acc[:, ob, cs],
                                                           ps_o[po][:, 0:TF], OP.add),
                                           waits=[(sP, on)], inc=sV)
                                    g_po[po] = sV.n
                                g_gbf = sP.n
                            g_w[wpr] = sP.n
                        experts_done = (sP.n, sV.n)
                    with ExitStack() as gc2:
                        gq = sbt(gc2, "gq_q%d" % q, [128, 8, LQ], F32R)
                        x5q = sbt(gc2, "x5q_q%d" % q, [128, 8, LQ], F32)
                        x0t = [sbt(gc2, "x0t%d_q%d" % (i, q), [128, D], F32) for i in range(2)]
                        ott = [sbt(gc2, "ott%d_q%d" % (i, q), [128, D], F32) for i in range(2)]
                        ps_f = [pst(gc2, "pf%d_q%d" % (i, q), [128, D]) for i in range(2)]
                        p.emit("scalar", lambda e, o=x5q, i_=xT0_h[:, :, qsl]: e.dma_start(
                            out=o, in_=i_),
                            waits=dst_w + [(sP, experts_done[0])], inc16=cX5)
                        ldx5 = cX5.n
                        p.emit("scalar", lambda e: e.activation(fl(gq), fl(acc), AF.Silu),
                               waits=[(sV, experts_done[1]), (sP, experts_done[0])], inc=sA)
                        p.emit("vector", lambda e: e.tensor_tensor(fl(acc), fl(x5q), fl(gq),
                                                                   OP.mult),
                               waits=[(sA, sA.n), (cX5, ldx5)], inc=sV)
                        sg_done = sV.n
                        g_ft = [0, 0]; g_psf = [0, 0]; g_ot = [0, 0]
                        for tt in range(8):
                            pr = tt % 2
                            row = q * LQ + tt * 128

                            def trf(e, pr=pr, tt=tt):
                                for dj in range(8):
                                    ins = e.transpose(
                                        ps_f[pr][:, dj * 128:(dj + 1) * 128],
                                        acc[:, dj, tt * 128:(tt + 1) * 128], id32)
                                return ins
                            p.emit("tensor", trf, waits=[(sV, sg_done), (sV, g_psf[pr])],
                                   inc=sP)
                            trn = sP.n
                            p.emit("scalar", lambda e, o=x0t[pr],
                                   i_=x_in[row:row + 128, :]: e.dma_start(out=o, in_=i_),
                                   waits=[(sV, g_ft[pr])], inc16=cX0[pr])
                            ldx0 = cX0[pr].n
                            p.emit("vector", lambda e, pr=pr: e.tensor_tensor(
                                ott[pr], x0t[pr], ps_f[pr], OP.add),
                                waits=[(sP, trn), (cX0[pr], ldx0), (cOU[pr], g_ot[pr])],
                                inc=sV)
                            g_psf[pr] = sV.n
                            g_ft[pr] = sV.n
                            p.emit("sync", lambda e, pr=pr, o=out_h[row:row + 128, :]:
                                   e.dma_start(out=o, in_=ott[pr]), waits=[(sV, sV.n)],
                                   inc16=cOU[pr])
                            g_ot[pr] = cOU[pr].n

        p.emit("sync", lambda e: e.nop(),
               waits=[(c, c.n) for c in all_dma_ctrs])

        block = ctx.enter_context(nc.Block())
        p.build(block)
    return nc


_NC_CACHE = {}


def _get_nc(key):
    if key not in _NC_CACHE:
        _NC_CACHE[key] = build(*key)
    return _NC_CACHE[key]


def kernel(L0_attn_x, L0_attn_W_in, L0_attn_conv_w, L0_attn_conv_b, L0_attn_W_x,
           L0_attn_W_dt, L0_attn_b_dt, L0_attn_A_log, L0_attn_D, L0_attn_W_out,
           L0_ffn_router_w, L0_ffn_expert_bias, L0_ffn_w1, L0_ffn_w2, L0_ffn_w3,
           L0_ffn_sw1, L0_ffn_sw2, L0_ffn_sw3,
           _depth=NL, _do_ffn=True, _dev_out=None, _cores=8, _dbg_spill=False,
           _ret=None):
    nc = _get_nc((_depth, _do_ffn, _dev_out, _dbg_spill))
    common = {
        "W_in": np.ascontiguousarray(L0_attn_W_in),
        "conv_w": np.ascontiguousarray(L0_attn_conv_w),
        "conv_b": np.ascontiguousarray(L0_attn_conv_b),
        "W_x": np.ascontiguousarray(L0_attn_W_x),
        "W_dt": np.ascontiguousarray(L0_attn_W_dt),
        "b_dt": np.ascontiguousarray(L0_attn_b_dt),
        "A_log": np.ascontiguousarray(L0_attn_A_log),
        "Dvec": np.ascontiguousarray(L0_attn_D),
        "W_out": np.ascontiguousarray(L0_attn_W_out),
        "rw": np.ascontiguousarray(L0_ffn_router_w),
        "ebias": np.ascontiguousarray(L0_ffn_expert_bias.reshape(1, NE)),
        "w1": np.ascontiguousarray(L0_ffn_w1),
        "w2": np.ascontiguousarray(L0_ffn_w2),
        "w3": np.ascontiguousarray(L0_ffn_w3),
        "sw1": np.ascontiguousarray(L0_ffn_sw1),
        "sw2": np.ascontiguousarray(L0_ffn_sw2),
        "sw3": np.ascontiguousarray(L0_ffn_sw3),
        "ident": np.eye(128, dtype=np.float32),
    }
    in_maps = []
    for b in range(_cores):
        m = dict(common)
        m["x"] = np.ascontiguousarray(L0_attn_x[b])
        in_maps.append(m)
    res = run_bass_kernel_spmd(nc, in_maps, core_ids=list(range(_cores)))
    if _ret is not None:
        return res
    return np.stack([res.results[b]["out"] for b in range(_cores)], axis=0)


# version 07
# speedup vs baseline: 1.2248x; 1.0092x over previous
import numpy as np
from contextlib import ExitStack

import concourse.bass as bass
import concourse.mybir as mybir
from concourse.bass_utils import run_bass_kernel_spmd

F32 = mybir.dt.float32
F32R = mybir.dt.float32r
BF16 = mybir.dt.bfloat16
AF = mybir.ActivationFunctionType
OP = mybir.AluOpType
AX = mybir.AxisListType

L, D, D2, E4, NST, DTR, NE, NL = 4096, 1024, 2048, 4096, 16, 64, 8, 5
T1, T2, TF, NQ = 512, 256, 512, 4
NC1, NC2, LQ = L // T1, L // T2, L // NQ


class Ctr:
    def __init__(self, sem, name):
        self.sem, self.name, self.n = sem, name, 0


class Prog:
    ENGINES = ("sync", "scalar", "vector", "tensor", "gpsimd")

    def __init__(self):
        self.ops = []

    def emit(self, engine, fn, waits=(), inc=None, inc16=None):
        ws = [(c.sem, v) for (c, v) in waits if v > 0]
        incs = []
        if inc is not None:
            inc.n += 1
            incs.append((inc.sem, 1))
        if inc16 is not None:
            inc16.n += 16
            incs.append((inc16.sem, 16))
        self.ops.append((engine, fn, ws, incs))

    def build(self, block):
        by = {e: [] for e in self.ENGINES}
        for op in self.ops:
            by[op[0]].append(op)

        def mk(name):
            def body(eng):
                waited = {}
                for _, fn, ws, incs in by[name]:
                    for sem, val in ws:
                        k = id(sem)
                        if waited.get(k, -1) < val:
                            eng.wait_ge(sem, val)
                            waited[k] = val
                    instr = fn(eng)
                    for sem, kk in incs:
                        instr.then_inc(sem, kk)
            return body

        for name in self.ENGINES:
            if by[name]:
                getattr(block, name)(mk(name))


def build(depth=NL, do_ffn=True, dev_out=None, dbg_spill=False, skip_last_p2=False):
    nc = bass.Bass("TRN2", target_bir_lowering=False)
    x_in = nc.dram_tensor("x", [L, D], F32, kind="ExternalInput").ap()
    Wi_h = nc.dram_tensor("W_in", [NL, E4, D], F32, kind="ExternalInput").ap()
    cw_h = nc.dram_tensor("conv_w", [NL, D2, 4], F32, kind="ExternalInput").ap()
    cb_h = nc.dram_tensor("conv_b", [NL, D2], F32, kind="ExternalInput").ap()
    Wx_h = nc.dram_tensor("W_x", [NL, 96, D2], F32, kind="ExternalInput").ap()
    Wdt_h = nc.dram_tensor("W_dt", [NL, D2, DTR], F32, kind="ExternalInput").ap()
    bdt_h = nc.dram_tensor("b_dt", [NL, D2], F32, kind="ExternalInput").ap()
    Al_h = nc.dram_tensor("A_log", [NL, D2, NST], F32, kind="ExternalInput").ap()
    Dp_h = nc.dram_tensor("Dvec", [NL, D2], F32, kind="ExternalInput").ap()
    Wo_h = nc.dram_tensor("W_out", [NL, D, D2], F32, kind="ExternalInput").ap()
    rw_h = nc.dram_tensor("rw", [D, NE], F32, kind="ExternalInput").ap()
    eb_h = nc.dram_tensor("ebias", [1, NE], F32, kind="ExternalInput").ap()
    w1_h = nc.dram_tensor("w1", [NE, D, D], F32, kind="ExternalInput").ap()
    w2_h = nc.dram_tensor("w2", [NE, D, D], F32, kind="ExternalInput").ap()
    w3_h = nc.dram_tensor("w3", [NE, D, D], F32, kind="ExternalInput").ap()
    sw1_h = nc.dram_tensor("sw1", [D, D], F32, kind="ExternalInput").ap()
    sw2_h = nc.dram_tensor("sw2", [D, D], F32, kind="ExternalInput").ap()
    sw3_h = nc.dram_tensor("sw3", [D, D], F32, kind="ExternalInput").ap()
    id_h = nc.dram_tensor("ident", [128, 128], F32, kind="ExternalInput").ap()
    out_h = nc.dram_tensor("out", [L, D], F32, kind="ExternalOutput").ap()

    xk = "ExternalOutput" if dbg_spill else "Internal"
    xnT_h = nc.dram_tensor("xnT", [128, 8, L], F32, kind=xk).ap()
    xT0_h = nc.dram_tensor("xT0", [128, 8, L], F32, kind=xk).ap()
    xT1_h = nc.dram_tensor("xT1", [128, 8, L], F32, kind=xk).ap()
    spill_kind = "ExternalOutput" if dbg_spill else "Internal"
    xa_h = nc.dram_tensor("xa_s", [128, 16, L], BF16, kind=spill_kind).ap()
    zs_h = nc.dram_tensor("zs_s", [128, 16, L], F32, kind=spill_kind).ap()
    dbgd_h = (nc.dram_tensor("dbg_del", [128, 16, L], F32, kind="ExternalOutput").ap()
              if dbg_spill else None)
    dbgy_h = (nc.dram_tensor("dbg_yac", [128, 16, L], F32, kind="ExternalOutput").ap()
              if dbg_spill else None)
    dbgg_h = (nc.dram_tensor("dbg_yg", [128, 16, L], mybir.dt.bfloat16,
                             kind="ExternalOutput").ap()
              if dbg_spill else None)
    bcs_h = nc.dram_tensor("bcs", [32, L], F32).ap()
    weH_h = nc.dram_tensor("weH", [NE, L], F32, kind=spill_kind).ap()

    p = Prog()
    ctx = ExitStack()
    with ctx:
        def mkc(name):
            return Ctr(ctx.enter_context(nc.semaphore(name)), name)

        sP, sA, sV = mkc("sP"), mkc("sA"), mkc("sV")
        sG = mkc("sG")
        cId = mkc("cId")
        cLNx = [mkc("cLNx%d" % i) for i in range(2)]
        cLNs = [mkc("cLNs%d" % i) for i in range(2)]
        cW1 = [mkc("cW1_%d" % i) for i in range(2)]
        cC1 = mkc("cC1")
        cXC = [mkc("cXC%d" % i) for i in range(2)]
        cZS = [mkc("cZS%d" % i) for i in range(2)]
        cXA = [mkc("cXA%d" % i) for i in range(2)]
        cW2 = mkc("cW2")
        cC2 = mkc("cC2")
        cXAB = mkc("cXAB")
        cXR = [mkc("cXR%d" % i) for i in range(2)]
        cZL = mkc("cZL")
        cBW = mkc("cBW")
        cBC = [mkc("cBC%d" % i) for i in range(4)]
        cDS = [mkc("cDS%d" % i) for i in range(2)]
        cRW = mkc("cRW")
        cXQ = [mkc("cXQ%d" % i) for i in range(2)]
        cWE = mkc("cWE")
        cXN = mkc("cXN")
        cFW = [mkc("cFW%d" % i) for i in range(2)]
        cWB = [mkc("cWB%d" % i) for i in range(2)]
        cX5 = mkc("cX5")
        cX0 = [mkc("cX0_%d" % i) for i in range(2)]
        cOU = [mkc("cOU%d" % i) for i in range(2)]
        all_dma_ctrs = ([cId, cC1, cW2, cC2, cXAB, cZL, cBW, cRW, cWE, cXN, cX5]
                        + cLNx + cLNs + cW1 + cXC + cZS + cXA + cXR + cBC + cDS
                        + cXQ + cFW + cWB + cX0 + cOU)

        def sbt(c, name, shape, dt):
            return c.enter_context(nc.sbuf_tensor(name, shape, dt)).ap()

        def pst(c, name, shape, dt=F32):
            return c.enter_context(nc.psum_tensor(name, shape, dt)).ap()

        id32 = sbt(ctx, "id32", [128, 128], F32)
        id16 = sbt(ctx, "id16", [128, 128], BF16)
        idr = sbt(ctx, "idr", [128, 128], F32R)

        p.emit("sync", lambda e: e.dma_start(out=id32, in_=id_h), inc16=cId)
        p.emit("vector", lambda e: e.tensor_copy(id16, id32), waits=[(cId, 16)])
        p.emit("vector", lambda e: e.tensor_copy(idr, id32), inc=sV)

        fl = lambda ap: ap.rearrange("p j t -> p (j t)")

        with ExitStack() as lc:
            xt = [sbt(lc, "ln_x%d" % i, [128, D], F32) for i in range(2)]
            x2t = [sbt(lc, "ln_x2%d" % i, [128, D], F32) for i in range(2)]
            st = [sbt(lc, "ln_st%d" % i, [128, 8], F32) for i in range(2)]
            xnt = [sbt(lc, "ln_xn%d" % i, [128, D], F32R) for i in range(2)]
            stg = [sbt(lc, "ln_stg%d" % i, [128, D], F32) for i in range(2)]
            ps_ln = [pst(lc, "ln_ps%d" % i, [128, D], F32R) for i in range(2)]
            g_x = [0, 0]; g_x2 = [0, 0]; g_st = [0, 0]; g_xn = [0, 0]
            g_ps = [0, 0]; g_stg = [0, 0]
            for tt in range(32):
                pr = tt % 2
                p.emit("sync", lambda e, b=xt[pr], r=x_in[tt * 128:(tt + 1) * 128, :]:
                       e.dma_start(out=b, in_=r), waits=[(sV, g_x[pr])], inc16=cLNx[pr])
                ld = cLNx[pr].n
                p.emit("scalar", lambda e, b=xt[pr], o=x2t[pr]: e.activation(o, b, AF.Square),
                       waits=[(cLNx[pr], ld), (sV, g_x2[pr])], inc=sA)
                sq = sA.n
                s_ = st[pr]
                p.emit("vector", lambda e, b=xt[pr], s=s_: e.reduce_sum(s[:, 0:1], b, AX.X),
                       waits=[(cLNx[pr], ld), (sA, g_st[pr])], inc=sV)
                r1n = sV.n
                p.emit("vector", lambda e, o=x2t[pr], s=s_: e.reduce_sum(s[:, 1:2], o, AX.X),
                       waits=[(sA, sq)], inc=sV)
                g_x2[pr] = sV.n
                p.emit("vector", lambda e, s=s_: e.tensor_scalar(
                    s[:, 2:3], s[:, 0:1], 1.0 / D, None, OP.mult),
                    waits=[(sV, r1n)], inc=sV)
                p.emit("vector", lambda e, s=s_: e.tensor_tensor(
                    s[:, 3:4], s[:, 2:3], s[:, 2:3], OP.mult),
                    waits=[(sV, sV.n)], inc=sV)
                p.emit("vector", lambda e, s=s_: e.scalar_tensor_tensor(
                    s[:, 4:5], s[:, 1:2], 1.0 / D, s[:, 3:4], OP.mult, OP.subtract),
                    waits=[(sV, sV.n)], inc=sV)
                p.emit("vector", lambda e, s=s_: e.tensor_scalar(
                    s[:, 4:5], s[:, 4:5], 1e-5, None, OP.add),
                    waits=[(sV, sV.n)], inc=sV)
                var = sV.n
                p.emit("scalar", lambda e, s=s_: e.activation(
                    s[:, 5:6], s[:, 4:5], AF.Ln), waits=[(sV, var)], inc=sA)
                p.emit("vector", lambda e, s=s_: e.tensor_scalar(
                    s[:, 5:6], s[:, 5:6], -0.5, None, OP.mult), waits=[(sA, sA.n)], inc=sV)
                p.emit("scalar", lambda e, s=s_: e.activation(
                    s[:, 6:7], s[:, 5:6], AF.Exp), waits=[(sV, sV.n)], inc=sA)
                rst = sA.n
                p.emit("vector", lambda e, b=xt[pr], s=s_, o=xnt[pr]: e.tensor_scalar(
                    o, b, s[:, 2:3], s[:, 6:7], OP.subtract, OP.mult),
                    waits=[(sA, rst), (sP, g_xn[pr])], inc=sV)
                xn_done = sV.n
                g_x[pr] = sV.n
                g_st[pr] = sA.n

                def tr(e, o=ps_ln[pr], i_=xnt[pr]):
                    for dj in range(8):
                        ins = e.transpose(o[:, dj * 128:(dj + 1) * 128],
                                          i_[:, dj * 128:(dj + 1) * 128], idr)
                    return ins
                p.emit("tensor", tr, waits=[(sV, xn_done), (sV, g_ps[pr])], inc=sP)
                g_xn[pr] = sP.n
                p.emit("vector", lambda e, ps=ps_ln[pr], o=stg[pr]: e.tensor_copy(o, ps),
                       waits=[(sP, sP.n), (cLNs[pr], g_stg[pr])], inc=sV)
                g_ps[pr] = sV.n
                p.emit("sync", lambda e, o=xnT_h[:, :, tt * 128:(tt + 1) * 128], i_=stg[pr]:
                       e.dma_start(out=o, in_=i_.rearrange("p (j t) -> p j t", j=8)),
                       waits=[(sV, sV.n)], inc16=cLNs[pr])
                g_stg[pr] = cLNs[pr].n

        srcs = [xnT_h, xT0_h, xT1_h, xT0_h, xT1_h]
        dsts = [xT0_h, xT1_h, xT0_h, xT1_h, xT0_h]

        for li in range(depth):
            src, dst = srcs[li], dsts[li]
            barrier = (sP.n, sA.n, sV.n)
            bw = [(sP, barrier[0]), (sA, barrier[1]), (sV, barrier[2])]
            src_w = ([(c, c.n) for c in cLNs] if li == 0
                     else [(c, c.n) for c in cDS])
            spill_w = [(c, c.n) for c in cZS + cXA]

            with ExitStack() as lc:
                WiT = sbt(lc, "WiT_L%d" % li, [128, 8, E4], BF16)
                stg16 = [sbt(lc, "stg16_%d_L%d" % (i, li), [128, 2048], BF16) for i in range(2)]
                xcbf = [sbt(lc, "xcbf%d_L%d" % (i, li), [128, 8, T1], BF16) for i in range(2)]
                xim = sbt(lc, "xim_L%d" % li, [128, 16, T1 + 3], F32R)
                convb = sbt(lc, "convb_L%d" % li, [128, 16, T1], F32)
                xat = [sbt(lc, "xat%d_L%d" % (i, li), [128, T1], BF16) for i in range(2)]
                zst = [sbt(lc, "zst%d_L%d" % (i, li), [128, T1], F32R) for i in range(2)]
                cwsb = sbt(lc, "cwsb_L%d" % li, [128, 16, 4], F32)
                cbv = sbt(lc, "cbv_L%d" % li, [128, 16], F32)
                v16 = sbt(lc, "v16_L%d" % li, [16, 128], F32)
                ps_pr = [pst(lc, "p1pr%d_L%d" % (i, li), [128, 1024], BF16) for i in range(2)]
                ps_mm = [pst(lc, "p1mm%d_L%d" % (i, li), [128, T1]) for i in range(2)]
                ps_v = pst(lc, "p1v_L%d" % li, [128, 512])

                g_stg = [0, 0]; g_ps = [0, 0]
                for ebk in range(32):
                    pr = ebk % 2
                    p.emit("gpsimd", lambda e, o=stg16[pr],
                           i_=Wi_h[li, ebk * 128:(ebk + 1) * 128, :]:
                           e.dma_start(out=o[:, 0:1024], in_=i_),
                           waits=bw + src_w + [(sP, g_stg[pr]), (cBW, cBW.n)],
                           inc16=cW1[pr])
                    ld = cW1[pr].n

                    def trw(e, ps=ps_pr[pr], s=stg16[pr]):
                        for dj in range(8):
                            ins = e.transpose(ps[:, dj * 128:(dj + 1) * 128],
                                              s[:, dj * 128:(dj + 1) * 128], id16)
                        return ins
                    p.emit("tensor", trw, waits=bw + [(cW1[pr], ld), (sV, g_ps[pr])], inc=sP)
                    g_stg[pr] = sP.n
                    p.emit("vector", lambda e, o=WiT[:, :, ebk * 128:(ebk + 1) * 128],
                           ps=ps_pr[pr]:
                           e.tensor_copy(o, ps.rearrange("p (j t) -> p j t", j=8)),
                           waits=bw + [(sP, sP.n)], inc=sV)
                    g_ps[pr] = sV.n
                p.emit("sync", lambda e, o=cwsb,
                       i_=cw_h[li].rearrange("(j p) k -> p j k", p=128):
                       e.dma_start(out=o, in_=i_),
                       waits=bw + src_w + [(cBW, cBW.n)], inc16=cC1)
                p.emit("sync", lambda e, o=v16,
                       i_=cb_h[li].rearrange("(j p) -> j p", p=128):
                       e.dma_start(out=o, in_=i_),
                       waits=bw + src_w + [(cBW, cBW.n)], inc16=cC1)
                ldcb = cC1.n
                p.emit("tensor", lambda e: e.transpose(ps_v[:, 0:16], v16, id32[0:16, 0:16]),
                       waits=bw + [(cC1, ldcb)], inc=sP)
                p.emit("vector", lambda e: e.tensor_copy(cbv, ps_v[:, 0:16]),
                       waits=[(sP, sP.n)], inc=sV)
                for j in range(16):
                    p.emit("vector", lambda e, j=j: e.memset(xim[:, j, 0:3].bitcast(F32), 0.0),
                           inc=(sV if j == 15 else None))
                wit_done = sV.n
                ldcw = cC1.n

                g_xc = [0, 0]; g_ps1 = [0, 0]
                g_xat = [0, 0]; g_zst = [0, 0]
                g_cv = 0
                evac_last = 0
                for c in range(NC1):
                    pr = c % 2
                    p.emit("gpsimd", lambda e, o=xcbf[pr],
                           i_=src[:, :, c * T1:(c + 1) * T1]:
                           e.dma_start(out=o, in_=i_),
                           waits=bw + src_w + [(sP, g_xc[pr])], inc16=cXC[pr])
                    ldx = cXC[pr].n
                    if c > 0:
                        p.emit("vector", lambda e: e.tensor_copy(
                            xim[:, :, 0:3], xim[:, :, T1:T1 + 3]),
                            waits=[(sA, evac_last)], inc=sV)
                    ms = sV.n
                    evac0 = sA.n
                    for ebk in range(32):
                        pp = ebk % 2

                        def mm(e, pp=pp, ebk=ebk, pr=pr):
                            for dk in range(8):
                                ins = e.matmul(ps_mm[pp][:, 0:T1],
                                               WiT[:, dk, ebk * 128:(ebk + 1) * 128],
                                               xcbf[pr][:, dk, :],
                                               start=(dk == 0), stop=(dk == 7))
                            return ins
                        p.emit("tensor", mm,
                               waits=bw + [(cXC[pr], ldx), (sA, g_ps1[pp]),
                                           (sV, wit_done)], inc=sP)
                        mmn = sP.n
                        if ebk < 16:
                            p.emit("scalar", lambda e, o=xim[:, ebk, 3:3 + T1], ps=ps_mm[pp]:
                                   e.activation(o, ps[:, 0:T1], AF.Copy),
                                   waits=[(sP, mmn), (sV, ms)], inc=sA)
                        else:
                            zpr = ebk % 2
                            p.emit("scalar", lambda e, ps=ps_mm[pp], o=zst[zpr]:
                                   e.activation(o, ps[:, 0:T1], AF.Silu),
                                   waits=[(sP, mmn), (cZS[zpr], g_zst[zpr])], inc=sA)
                            p.emit("scalar", lambda e, o=zs_h[:, ebk - 16, c * T1:(c + 1) * T1],
                                   i_=zst[zpr].bitcast(F32): e.dma_start(out=o, in_=i_),
                                   waits=[(sA, sA.n)], inc16=cZS[zpr])
                            g_zst[zpr] = cZS[zpr].n
                        g_ps1[pp] = sA.n
                    evac_last = sA.n
                    g_xc[pr] = sP.n
                    for k in range(4):
                        for j in range(16):
                            evj = evac0 + j + 1
                            if k == 0:
                                p.emit("vector", lambda e, j=j: e.tensor_scalar(
                                    convb[:, j, :], xim[:, j, 0:T1],
                                    cwsb[:, j:j + 1, 0:1].squeeze(1),
                                    cbv[:, j:j + 1], OP.mult, OP.add),
                                    waits=[(sA, evj), (cC1, ldcw), (sV, wit_done),
                                           (sA, g_cv)])
                            else:
                                p.emit("vector", lambda e, j=j, k=k:
                                       e.scalar_tensor_tensor(
                                           convb[:, j, :], xim[:, j, k:k + T1],
                                           cwsb[:, j:j + 1, k:k + 1].squeeze(1),
                                           convb[:, j, :], OP.mult, OP.add),
                                       waits=[(sA, evj)],
                                       inc=(sV if (k == 3 and j == 15) else None))
                    cv_done = sV.n
                    for j in range(16):
                        apr = j % 2
                        p.emit("scalar", lambda e, j=j, o=xat[apr]:
                               e.activation(o, convb[:, j, :], AF.Silu),
                               waits=[(sV, cv_done), (cXA[apr], g_xat[apr])], inc=sA)
                        p.emit("scalar", lambda e, o=xa_h[:, j, c * T1:(c + 1) * T1],
                               i_=xat[apr]: e.dma_start(out=o, in_=i_),
                               waits=[(sA, sA.n)], inc16=cXA[apr])
                        g_xat[apr] = cXA[apr].n
                    g_cv = sA.n

            if skip_last_p2 and li == depth - 1:
                continue
            barrier = (sP.n, sA.n, sV.n)
            bw = [(sP, barrier[0]), (sA, barrier[1]), (sV, barrier[2])]
            spill_w = [(c, c.n) for c in cZS + cXA]
            with ExitStack() as lc:
                WoT = sbt(lc, "WoT_L%d" % li, [128, 16, D], BF16)
                WxT = sbt(lc, "WxT_L%d" % li, [128, 16, 96], BF16)
                WdtT = sbt(lc, "WdtT_L%d" % li, [64, D2], F32R)
                Asb = sbt(lc, "Asb_L%d" % li, [128, 16, NST], F32)
                Dpv = sbt(lc, "Dpv_L%d" % li, [128, 16], F32)
                bdtv = sbt(lc, "bdtv_L%d" % li, [128, 16], F32)
                v16b = sbt(lc, "v16b_L%d" % li, [16, 128], F32)
                state = sbt(lc, "state_L%d" % li, [128, 256], F32)

                with ExitStack() as pc:
                    stg16b = sbt(pc, "stg16b_L%d" % li, [128, 2048], BF16)
                    stgr = sbt(pc, "stgr_L%d" % li, [128, 2048], F32R)
                    ps_prb = pst(pc, "p2prb_L%d" % li, [128, 2048], BF16)
                    ps_prf = pst(pc, "p2prf_L%d" % li, [128, 512])
                    ps_prr = pst(pc, "p2prr_L%d" % li, [128, 512], F32R)
                    g_stg = 0
                    g_ps = 0
                    for ob in range(8):
                        p.emit("gpsimd", lambda e, o=stg16b,
                               i_=Wo_h[li, ob * 128:(ob + 1) * 128, :]:
                               e.dma_start(out=o, in_=i_),
                               waits=bw + spill_w + [(sP, g_stg)], inc16=cW2)
                        ld = cW2.n

                        def trw(e):
                            for dk in range(16):
                                ins = e.transpose(ps_prb[:, dk * 128:(dk + 1) * 128],
                                                  stg16b[:, dk * 128:(dk + 1) * 128], id16)
                            return ins
                        p.emit("tensor", trw, waits=bw + [(cW2, ld), (sV, g_ps)], inc=sP)
                        g_stg = sP.n
                        p.emit("vector", lambda e, o=WoT[:, :, ob * 128:(ob + 1) * 128]:
                               e.tensor_copy(o, ps_prb.rearrange("p (j t) -> p j t", j=16)),
                               waits=bw + [(sP, sP.n)], inc=sV)
                        g_ps = sV.n
                    p.emit("gpsimd", lambda e, o=stg16b[0:96, :], i_=Wx_h[li]:
                           e.dma_start(out=o, in_=i_), waits=[(sV, g_ps)], inc16=cW2)
                    ldwx = cW2.n
                    for dk in range(16):
                        p.emit("tensor", lambda e, dk=dk: e.transpose(
                            ps_prb[:, 0:96], stg16b[0:96, dk * 128:(dk + 1) * 128],
                            id16[0:96, 0:96]), waits=[(cW2, ldwx), (sV, sV.n)], inc=sP)
                        p.emit("vector", lambda e, dk=dk: e.tensor_copy(
                            WxT[:, dk, :], ps_prb[:, 0:96]), waits=[(sP, sP.n)], inc=sV)
                    p.emit("gpsimd", lambda e,
                           o=stgr[:, 0:1024].rearrange("p (dk r) -> p dk r", dk=16),
                           i_=Wdt_h[li].rearrange("(dk p) r -> p dk r", p=128):
                           e.dma_start(out=o, in_=i_), waits=[(sV, sV.n)], inc16=cW2)
                    ldwd = cW2.n
                    for dk in range(16):
                        p.emit("tensor", lambda e, dk=dk: e.transpose(
                            ps_prr[0:64, 0:128], stgr[:, dk * 64:(dk + 1) * 64], idr),
                            waits=[(cW2, ldwd), (sV, sV.n)], inc=sP)
                        p.emit("vector", lambda e, dk=dk: e.tensor_copy(
                            WdtT[:, dk * 128:(dk + 1) * 128], ps_prr[0:64, 0:128]),
                            waits=[(sP, sP.n)], inc=sV)
                    p.emit("sync", lambda e, o=Asb,
                           i_=Al_h[li].rearrange("(j p) n -> p j n", p=128):
                           e.dma_start(out=o, in_=i_),
                           waits=bw + spill_w, inc16=cC2)
                    ldA = cC2.n
                    p.emit("scalar", lambda e: e.activation(fl(Asb), fl(Asb), AF.Exp),
                           waits=bw + [(cC2, ldA)], inc=sA)
                    p.emit("vector", lambda e: e.tensor_scalar(
                        fl(Asb), fl(Asb), -1.0, None, OP.mult), waits=[(sA, sA.n)], inc=sV)
                    for vec_h, vdst in ((Dp_h[li], Dpv), (bdt_h[li], bdtv)):
                        p.emit("sync", lambda e, i_=vec_h.rearrange("(j p) -> j p", p=128):
                               e.dma_start(out=v16b, in_=i_),
                               waits=spill_w + [(sP, sP.n)], inc16=cC2)
                        ldv = cC2.n
                        p.emit("tensor", lambda e: e.transpose(
                            ps_prf[:, 0:16], v16b, id32[0:16, 0:16]),
                            waits=[(cC2, ldv), (sV, sV.n)], inc=sP)
                        p.emit("vector", lambda e, o=vdst: e.tensor_copy(o, ps_prf[:, 0:16]),
                               waits=[(sP, sP.n)], inc=sV)
                    p.emit("vector", lambda e: e.memset(state, 0.0), inc=sV)
                prep_done = sV.n

                xab = sbt(lc, "xab_L%d" % li, [128, 16, T2], BF16)
                delb = sbt(lc, "delb_L%d" % li, [128, 16, T2], F32)
                dub = sbt(lc, "dub_L%d" % li, [128, 16, T2], F32)
                dAb = [sbt(lc, "dAb%d_L%d" % (i, li), [128, 16, T2], F32) for i in range(2)]
                dBuA = [sbt(lc, "dBuA%d_L%d" % (i, li), [128, 16, T2], F32) for i in range(2)]
                dBuB = sbt(lc, "dBuB_L%d" % li, [128, 16, T2], F32)
                yac = sbt(lc, "yac_L%d" % li, [128, 16, T2], F32)
                ygbf = dBuA[1].bitcast(BF16)[:, :, 0:T2]
                xres = sbt(lc, "xres_L%d" % li, [128, 8, T2], F32)
                dtb = sbt(lc, "dtb_L%d" % li, [64, T2], F32R)
                bcb = sbt(lc, "bcb_L%d" % li, [32, T2], F32)
                bcc = [sbt(lc, "bcc%d_L%d" % (i, li), [128, 2, T2], F32) for i in range(4)]
                ps_xd = pst(lc, "p2xd_L%d" % li, [128, 512])
                ps_del = [pst(lc, "p2dl%d_L%d" % (i, li), [128, 512]) for i in range(2)]
                ps_wo = [pst(lc, "p2wo%d_L%d" % (i, li), [128, 512]) for i in range(2)]

                JD = 3
                g_xab = 0
                g_xres = 0
                g_xd = 0
                g_del = [0, 0]
                g_wo = [0, 0]
                g_bcb = 0
                g_scanA = [0, 0]
                g_m1 = [0, 0]
                g_exp = [0, 0]
                g_hV = [0, 0]
                g_hG = [0, 0]
                g_warA = [[], []]
                g_bccV = [0, 0, 0, 0]
                g_bccG = [0, 0, 0, 0]
                bcc_ld = [0, 0, 0, 0]
                g_yaddD = 0
                g_yaddP = 0
                g_memset = 0
                g_skp = 0
                g_yg = 0
                g_dub = 0
                g_m1last = 0
                for c in range(NC2):
                    sl = slice(c * T2, (c + 1) * T2)
                    p.emit("sync", lambda e, i_=xa_h[:, :, sl]: e.dma_start(out=xab, in_=i_),
                           waits=bw + spill_w + [(sV, g_xab)], inc16=cXAB)
                    ldxa = cXAB.n
                    p.emit("scalar", lambda e, o=xres, i_=src[:, :, sl]: e.dma_start(
                        out=o, in_=i_),
                        waits=bw + src_w + [(cDS[0], g_xres)], inc16=cXR[0])
                    ldxr = cXR[0].n

                    def mmx(e):
                        for dk in range(16):
                            ins = e.matmul(ps_xd[0:96, 0:T2], WxT[:, dk, :], xab[:, dk, :],
                                           start=(dk == 0), stop=(dk == 15))
                        return ins
                    p.emit("tensor", mmx, waits=[(cXAB, ldxa), (sV, g_xd), (sV, prep_done)],
                           inc=sP)
                    mmxn = sP.n
                    p.emit("vector", lambda e: e.tensor_copy(dtb, ps_xd[0:64, 0:T2]),
                           waits=[(sP, mmxn), (cBW, g_bcb)])
                    p.emit("vector", lambda e: e.tensor_copy(bcb, ps_xd[64:96, 0:T2]), inc=sV)
                    g_xd = sV.n
                    bc_ev = sV.n
                    p.emit("sync", lambda e, o=bcs_h[:, sl]: e.dma_start(out=o, in_=bcb),
                           waits=[(sV, bc_ev)], inc16=cBW)
                    bcw = cBW.n
                    g_bcb = cBW.n
                    for dk in range(16):
                        pp = dk % 2
                        p.emit("tensor", lambda e, dk=dk, pp=pp: e.matmul(
                            ps_del[pp][:, 0:T2], WdtT[:, dk * 128:(dk + 1) * 128], dtb,
                            start=True, stop=True),
                            waits=[(sV, bc_ev), (sV, g_del[pp])], inc=sP)
                        p.emit("vector", lambda e, dk=dk, pp=pp: e.tensor_scalar(
                            delb[:, dk, :], ps_del[pp][:, 0:T2], bdtv[:, dk:dk + 1],
                            None, OP.add),
                            waits=[(sP, sP.n), (sV, prep_done),
                                   (sA, g_exp[0]), (sA, g_exp[1])], inc=sV)
                        g_del[pp] = sV.n
                    del_done = sV.n
                    p.emit("vector", lambda e: e.tensor_scalar(
                        fl(dBuA[0]), fl(delb), -1.0, None, OP.mult),
                        waits=[(sV, del_done), (sV, g_yg), (cZL, cZL.n)] + g_warA[0],
                        inc=sV)
                    p.emit("vector", lambda e: e.tensor_tensor(
                        fl(dBuA[0]), fl(delb), fl(dBuA[0]), OP.min),
                        waits=[(sV, sV.n)], inc=sV)
                    min_n = sV.n
                    p.emit("scalar", lambda e: e.activation(fl(dAb[0]), fl(dBuA[0]), AF.Exp),
                           waits=[(sV, min_n), (sV, g_yg), (sG, g_hG[0])])
                    p.emit("scalar", lambda e: e.activation(fl(dBuA[0]), fl(dAb[0]), AF.Ln,
                                                            bias=1.0), inc=sA)
                    p.emit("vector", lambda e: e.scalar_tensor_tensor(
                        fl(delb), fl(delb), 0.0, fl(dBuA[0]), OP.max, OP.add),
                        waits=[(sA, sA.n)], inc=sV)
                    stt_n = sV.n
                    p.emit("gpsimd", lambda e: e.tensor_tensor(fl(dub), fl(delb), fl(xab),
                                                               OP.mult),
                           waits=[(sV, stt_n), (sV, g_m1last)], inc=sG)
                    g_dub = sG.n
                    sp_done = sV.n
                    p.emit("vector", lambda e: e.memset(fl(yac), 0.0),
                           waits=[(sV, g_skp)], inc=sV)
                    g_memset = sV.n
                    for q in range(3):
                        row0 = bcs_h[q:q + 17:16, sl]
                        p.emit("sync", lambda e, o=bcc[q], r=row0: e.dma_start(
                            out=o, in_=r.unsqueeze(0).broadcast_to((128, 2, T2))),
                            waits=[(cBW, bcw), (sV, g_bccV[q]), (sG, g_bccG[q])],
                            inc16=cBC[q])
                        bcc_ld[q] = cBC[q].n
                    for nn in range(2):
                        p.emit("vector", lambda e, nn=nn: e.tensor_tensor(
                            dBuA[nn], dub,
                            bcc[nn][:, 0:1, :].broadcast_to((128, 16, T2)), OP.mult),
                            waits=[(sG, g_dub), (cBC[nn], bcc_ld[nn]),
                                   (sV, g_scanA[nn])] + g_warA[nn], inc=sV)
                        g_m1[nn] = sV.n
                        g_warA[nn] = []
                        g_bccV[nn] = sV.n

                    for n in range(NST):
                        bp = n % 2
                        q = n % 4

                        def dAe(e, n=n, bp=bp):
                            for j in range(16):
                                ins = e.activation(dAb[bp][:, j, :], delb[:, j, :], AF.Exp,
                                                   scale=Asb[:, j:j + 1, n:n + 1].squeeze(1))
                            return ins
                        p.emit("scalar", dAe, waits=[(sV, sp_done), (sV, prep_done),
                                                     (sV, g_hV[bp]), (sG, g_hG[bp])],
                               inc=sA)
                        g_exp[bp] = sA.n

                        def scn(e, n=n, bp=bp):
                            for j in range(16):
                                ins = e.tensor_tensor_scan(
                                    dAb[bp][:, j, :], dAb[bp][:, j, :],
                                    dBuA[bp][:, j, :],
                                    state[:, n * 16 + j:n * 16 + j + 1],
                                    OP.mult, OP.add)
                            return ins
                        p.emit("vector", scn, waits=[(sA, g_exp[bp]), (sV, g_m1[bp])],
                               inc=sV)
                        scn_n = sV.n
                        g_scanA[bp] = sV.n
                        p.emit("vector", lambda e, n=n, bp=bp: e.tensor_copy(
                            state[:, n * 16:(n + 1) * 16],
                            dAb[bp][:, :, T2 - 1:T2].squeeze(2)),
                            waits=[(sV, scn_n)], inc=sV)
                        st_n = sV.n
                        if n + 2 < NST:
                            bp2 = n % 2
                            q2 = (n + 2) % 4
                            p.emit("vector", lambda e, bp2=bp2, q2=q2: e.tensor_tensor(
                                dBuA[bp2], dub,
                                bcc[q2][:, 0:1, :].broadcast_to((128, 16, T2)), OP.mult),
                                waits=[(sG, g_dub), (cBC[q2], bcc_ld[q2]),
                                       (sV, scn_n)], inc=sV)
                            g_m1[bp2] = sV.n
                            g_bccV[q2] = sV.n
                            if n + 2 == NST - 1:
                                g_m1last = sV.n
                        p.emit("vector", lambda e, bp=bp, q=q: e.tensor_tensor(
                            dAb[bp][:, 0:JD, :], dAb[bp][:, 0:JD, :],
                            bcc[q][:, 1:2, :].broadcast_to((128, JD, T2)), OP.mult),
                            waits=[(sV, st_n), (cBC[q], bcc_ld[q])], inc=sV)
                        mD = sV.n
                        p.emit("vector", lambda e, bp=bp: e.tensor_tensor(
                            yac[:, 0:JD, :], yac[:, 0:JD, :], dAb[bp][:, 0:JD, :],
                            OP.add),
                            waits=[(sV, mD), (sV, g_memset)], inc=sV)
                        g_yaddD = sV.n
                        g_hV[bp] = sV.n
                        g_bccV[q] = max(g_bccV[q], mD)
                        p.emit("gpsimd", lambda e, bp=bp, q=q: e.tensor_tensor(
                            dAb[bp][:, JD:16, :], dAb[bp][:, JD:16, :],
                            bcc[q][:, 1:2, :].broadcast_to((128, 16 - JD, T2)), OP.mult),
                            waits=[(sV, st_n), (cBC[q], bcc_ld[q]), (sG, g_yaddP)],
                            inc=sG)
                        mP = sG.n
                        p.emit("gpsimd", lambda e, bp=bp: e.tensor_tensor(
                            yac[:, JD:16, :], yac[:, JD:16, :], dAb[bp][:, JD:16, :],
                            OP.add),
                            waits=[(sG, mP), (sV, g_memset)], inc=sG)
                        g_yaddP = sG.n
                        g_hG[bp] = sG.n
                        g_bccG[q] = sG.n
                        if n + 3 < NST:
                            q3 = (n + 3) % 4
                            row3 = bcs_h[n + 3:n + 3 + 17:16, sl]
                            p.emit("sync", lambda e, o=bcc[q3], r=row3: e.dma_start(
                                out=o, in_=r.unsqueeze(0).broadcast_to((128, 2, T2))),
                                waits=[(cBW, bcw), (sV, g_bccV[q3]), (sG, g_bccG[q3])],
                                inc16=cBC[q3])
                            bcc_ld[q3] = cBC[q3].n

                    p.emit("scalar", lambda e, i_=zs_h[:, :, sl]: e.dma_start(
                        out=dBuA[0], in_=i_),
                        waits=[(sV, g_scanA[0])] + spill_w, inc16=cZL)
                    ldz = cZL.n

                    def skp(e):
                        for j in range(16):
                            ins = e.scalar_tensor_tensor(dAb[0][:, j, :], xab[:, j, :],
                                                         Dpv[:, j:j + 1], yac[:, j, :],
                                                         OP.mult, OP.add)
                        return ins
                    p.emit("vector", skp, waits=[(sV, g_yaddD), (sG, g_yaddP),
                                                 (sV, g_hV[0]), (sG, g_hG[0])], inc=sV)
                    g_skp = sV.n
                    g_xab = sV.n
                    p.emit("vector", lambda e: e.tensor_tensor(ygbf, dAb[0], dBuA[0],
                                                               OP.mult),
                           waits=[(cZL, ldz), (sV, g_skp), (sV, g_scanA[1])], inc=sV)
                    g_yg = sV.n
                    g_hV[0] = g_yg
                    g_warA[0] = [(sV, g_yg)]
                    for ob in range(8):
                        pp = ob % 2

                        def mwo(e, ob=ob, pp=pp):
                            for dk in range(16):
                                ins = e.matmul(ps_wo[pp][:, 0:T2],
                                               WoT[:, dk, ob * 128:(ob + 1) * 128],
                                               ygbf[:, dk, :], start=(dk == 0),
                                               stop=(dk == 15))
                            return ins
                        p.emit("tensor", mwo, waits=[(sV, g_yg), (sV, g_wo[pp])], inc=sP)
                        won = sP.n
                        p.emit("vector", lambda e, ob=ob, pp=pp: e.tensor_tensor(
                            xres[:, ob, :], xres[:, ob, :], ps_wo[pp][:, 0:T2], OP.add),
                            waits=[(sP, won), (cXR[0], ldxr)], inc=sV)
                        g_wo[pp] = sV.n
                    g_warA[1] = [(sP, sP.n)]
                    p.emit("scalar", lambda e, o=dst[:, :, sl], i_=xres: e.dma_start(
                        out=o, in_=i_), waits=[(sV, sV.n)], inc16=cDS[0])
                    g_xres = cDS[0].n

        mamba_done = (sP.n, sA.n, sV.n)
        mbw = [(sP, mamba_done[0]), (sA, mamba_done[1]), (sV, mamba_done[2])]
        dst_w = [(c, c.n) for c in (cDS if depth > 0 else cLNs)]
        ln_w = [(c, c.n) for c in cLNs]

        def writeback(src_hbm):
            with ExitStack() as lc:
                ps_f = [pst(lc, "wb%d" % i, [128, D]) for i in range(2)]
                ft = [sbt(lc, "wb_f%d" % i, [128, 8, 128], F32) for i in range(2)]
                ot = [sbt(lc, "wb_o%d" % i, [128, D], F32) for i in range(2)]
                g_ft = [0, 0]; g_ps = [0, 0]; g_ot = [0, 0]
                for tt in range(32):
                    pr = tt % 2
                    p.emit("scalar", lambda e, o=ft[pr],
                           i_=src_hbm[:, :, tt * 128:(tt + 1) * 128]:
                           e.dma_start(out=o, in_=i_),
                           waits=mbw + dst_w + [(sP, g_ft[pr])], inc16=cWB[pr])
                    ld = cWB[pr].n

                    def tr(e, pr=pr):
                        for dj in range(8):
                            ins = e.transpose(ps_f[pr][:, dj * 128:(dj + 1) * 128],
                                              ft[pr][:, dj, :], id32)
                        return ins
                    p.emit("tensor", tr, waits=mbw + [(cWB[pr], ld), (sV, g_ps[pr])], inc=sP)
                    g_ft[pr] = sP.n
                    p.emit("vector", lambda e, pr=pr: e.tensor_copy(ot[pr], ps_f[pr]),
                           waits=mbw + [(sP, sP.n), (cOU[pr], g_ot[pr])], inc=sV)
                    g_ps[pr] = sV.n
                    p.emit("sync", lambda e, pr=pr, o=out_h[tt * 128:(tt + 1) * 128, :]:
                           e.dma_start(out=o, in_=ot[pr]), waits=[(sV, sV.n)], inc16=cOU[pr])
                    g_ot[pr] = cOU[pr].n

        if dev_out == "xn":
            writeback(xnT_h)
        elif dev_out == "mamba" or not do_ffn:
            wb_src = dsts[depth - 1] if depth > 0 else xnT_h
            if skip_last_p2 and depth > 1:
                wb_src = dsts[depth - 2]
            writeback(wb_src)
        else:
            with ExitStack() as lc:
                rwsb = sbt(lc, "rwsb", [128, 8, 8], F32R)
                ebb = sbt(lc, "ebb", [128, 8], F32)
                wsc = sbt(lc, "wsc", [128, 8], F32)
                wval = sbt(lc, "wval", [128, 8], F32)
                wmsk = sbt(lc, "wmsk", [128, 8], F32)
                wtmp = sbt(lc, "wtmp", [128, 8], F32)
                m12 = sbt(lc, "m12", [128, 4], F32)
                weT = sbt(lc, "weT", [8, L], F32)
                xnq = [sbt(lc, "xnq%d" % i, [128, 8, TF], F32R) for i in range(2)]
                ps_rt = pst(lc, "rt", [128, 512])
                ps_wt = pst(lc, "wt", [128, 512])
                p.emit("gpsimd", lambda e: e.dma_start(
                    out=rwsb, in_=rw_h.rearrange("(dk p) e -> p dk e", p=128)),
                    waits=mbw + dst_w + [(cBW, cBW.n)], inc16=cRW)
                p.emit("sync", lambda e: e.dma_start(out=ebb,
                                                     in_=eb_h.broadcast_to((128, NE))),
                       waits=mbw + dst_w + [(cBW, cBW.n)], inc16=cRW)
                ld0 = cRW.n
                g_xnq = [0, 0]
                g_rt = 0
                g_wt = 0
                g_weT = 0
                for ch in range(8):
                    pr = ch % 2
                    p.emit("gpsimd", lambda e, o=xnq[pr],
                           i_=xnT_h[:, :, ch * TF:(ch + 1) * TF]: e.dma_start(out=o, in_=i_),
                           waits=mbw + ln_w + dst_w + [(sP, g_xnq[pr]), (cBW, cBW.n)],
                           inc16=cXQ[pr])
                    ldq = cXQ[pr].n
                    for ts_ in range(4):
                        tt = ch * 4 + ts_

                        def mmr(e, pr=pr, ts_=ts_):
                            for dk in range(8):
                                ins = e.matmul(ps_rt[:, 0:8],
                                               xnq[pr][:, dk, ts_ * 128:(ts_ + 1) * 128],
                                               rwsb[:, dk, :], start=(dk == 0),
                                               stop=(dk == 7))
                            return ins
                        p.emit("tensor", mmr,
                               waits=mbw + [(cXQ[pr], ldq), (cRW, ld0), (sV, g_rt)],
                               inc=sP)
                        rtn = sP.n

                        p.emit("vector", lambda e: e.reduce_max(
                            m12[:, 0:1], ps_rt[:, 0:8], AX.X),
                            waits=[(sP, rtn)], inc=sV)
                        p.emit("vector", lambda e: e.tensor_scalar(
                            wsc, ps_rt[:, 0:8], m12[:, 0:1], None, OP.subtract),
                            waits=[(sV, sV.n)], inc=sV)
                        g_rt = sV.n
                        p.emit("scalar", lambda e: e.activation(wsc, wsc, AF.Exp),
                               waits=[(sV, sV.n)], inc=sA)

                        for fn_ in (
                            lambda e: e.reduce_sum(m12[:, 1:2], wsc, AX.X),
                            lambda e: e.reciprocal(m12[:, 2:3], m12[:, 1:2]),
                            lambda e: e.tensor_scalar(wsc, wsc, m12[:, 2:3], None, OP.mult),
                            lambda e: e.tensor_tensor(wval, wsc, ebb, OP.add),
                            lambda e: e.reduce_max(m12[:, 0:1], wval, AX.X),
                            lambda e: e.tensor_scalar(wmsk, wval, m12[:, 0:1], None,
                                                      OP.is_ge),
                            lambda e: e.scalar_tensor_tensor(wtmp, wmsk, -1e30, wval,
                                                             OP.mult, OP.add),
                            lambda e: e.reduce_max(m12[:, 1:2], wtmp, AX.X),
                            lambda e: e.tensor_scalar(wtmp, wtmp, m12[:, 1:2], None,
                                                      OP.is_ge),
                            lambda e: e.tensor_tensor(wmsk, wmsk, wtmp, OP.add),
                            lambda e: e.tensor_tensor(wval, wsc, wmsk, OP.mult),
                        ):
                            p.emit("vector", fn_,
                                   waits=[(sA, sA.n), (sP, g_wt), (sV, sV.n)], inc=sV)
                        wdone = sV.n
                        p.emit("tensor", lambda e: e.transpose(ps_wt[0:8, 0:128], wval, id32),
                               waits=[(sV, wdone), (sV, g_weT)], inc=sP)
                        g_wt = sP.n
                        p.emit("vector", lambda e, o=weT[0:8, tt * 128:(tt + 1) * 128]:
                               e.tensor_copy(o, ps_wt[0:8, 0:128]),
                               waits=[(sP, sP.n)], inc=sV)
                        g_weT = sV.n
                    g_xnq[pr] = sP.n
                p.emit("gpsimd", lambda e: e.dma_start(out=weH_h, in_=weT),
                       waits=[(sV, g_weT)], inc16=cWE)
            we_done = cWE.n
            rbw = [(sP, sP.n), (sA, sA.n), (sV, sV.n)]

            with ExitStack() as fc:
                xnq16 = sbt(fc, "xnq16", [128, 8, LQ], BF16)
                acc = sbt(fc, "acc", [128, 8, LQ], F32)
                s1t = [sbt(fc, "s1t%d" % i, [128, TF], F32) for i in range(2)]
                t1t = [sbt(fc, "t1t%d" % i, [128, TF], F32) for i in range(2)]
                gbf = sbt(fc, "gbf", [128, 8, TF], BF16)
                web = [sbt(fc, "web%d" % i, [128, TF], F32) for i in range(2)]
                for q in range(NQ):
                    qsl = slice(q * LQ, (q + 1) * LQ)
                    with ExitStack() as wc:
                        ps_h1 = [pst(wc, "fh1%d_q%d" % (i, q), [128, 512]) for i in range(2)]
                        ps_h3 = [pst(wc, "fh3%d_q%d" % (i, q), [128, 512]) for i in range(2)]
                        ps_o = [pst(wc, "fo%d_q%d" % (i, q), [128, 512]) for i in range(2)]
                        w1sb = [sbt(wc, "w1sb%d_q%d" % (i, q), [128, 8, D], BF16) for i in range(2)]
                        w3sb = [sbt(wc, "w3sb%d_q%d" % (i, q), [128, 8, D], BF16) for i in range(2)]
                        w2sb = [sbt(wc, "w2sb%d_q%d" % (i, q), [128, 8, D], BF16) for i in range(2)]
                        p.emit("gpsimd", lambda e, o=xnq16, i_=xnT_h[:, :, qsl]:
                               e.dma_start(out=o, in_=i_),
                               waits=rbw + ln_w + [(sP, sP.n), (cWE, we_done)],
                               inc16=cXN)
                        ldxn = cXN.n
                        p.emit("vector", lambda e: e.memset(fl(acc), 0.0),
                               waits=rbw + [(sV, sV.n), (sP, sP.n), (cWE, we_done)],
                               inc=sV)
                        g_w = [0, 0]
                        g_web = [0, 0]
                        g_h1 = [0, 0]
                        g_h3 = [0, 0]
                        g_po = [0, 0]
                        g_s1 = [0, 0]
                        g_gbf = 0
                        ldw = [0, 0]
                        for e_ in range(9):
                            wpr = e_ % 2
                            if e_ < 8:
                                srcs_w = (w1_h[e_], w3_h[e_], w2_h[e_])
                            else:
                                srcs_w = (sw1_h, sw3_h, sw2_h)
                            for wsrc, wdst in zip(srcs_w, (w1sb[wpr], w3sb[wpr], w2sb[wpr])):
                                p.emit("gpsimd", lambda e, o=wdst,
                                       i_=wsrc.rearrange("(dk p) h -> p dk h", p=128):
                                       e.dma_start(out=o, in_=i_),
                                       waits=rbw + [(sP, g_w[wpr]), (cWE, we_done),
                                                    (cOU[0], cOU[0].n),
                                                    (cOU[1], cOU[1].n),
                                                    (cX5, cX5.n)],
                                       inc16=cFW[wpr])
                            ldw[wpr] = cFW[wpr].n
                            for c2 in range(2):
                                csl = slice(c2 * TF, (c2 + 1) * TF)
                                tok0 = q * LQ + c2 * TF
                                if e_ < 8:
                                    p.emit("gpsimd", lambda e, o=web[c2], e_=e_,
                                           r=weH_h[e_:e_ + 1, tok0:tok0 + TF]:
                                           e.dma_start(out=o, in_=r.broadcast_to((128, TF))),
                                           waits=[(cWE, we_done), (sV, g_web[c2])],
                                           inc16=cWB[c2])
                                ldwe = cWB[c2].n
                                for hj in range(8):
                                    hp = hj % 2

                                    def mmh(e, wpr=wpr, hj=hj, hp=hp, cs=csl):
                                        for dk in range(8):
                                            e.matmul(ps_h1[hp][:, 0:TF],
                                                     w1sb[wpr][:, dk, hj * 128:(hj + 1) * 128],
                                                     xnq16[:, dk, cs],
                                                     start=(dk == 0), stop=(dk == 7))
                                        for dk in range(8):
                                            ins = e.matmul(
                                                ps_h3[hp][:, 0:TF],
                                                w3sb[wpr][:, dk, hj * 128:(hj + 1) * 128],
                                                xnq16[:, dk, cs],
                                                start=(dk == 0), stop=(dk == 7))
                                        return ins
                                    p.emit("tensor", mmh,
                                           waits=[(cFW[wpr], ldw[wpr]), (cXN, ldxn),
                                                  (sA, g_h1[hp]), (sV, g_h3[hp])], inc=sP)
                                    hn = sP.n
                                    p.emit("scalar", lambda e, hp=hp, sp=s1t[hj % 2]:
                                           e.activation(sp, ps_h1[hp][:, 0:TF], AF.Silu),
                                           waits=[(sP, hn), (sV, g_s1[hj % 2])], inc=sA)
                                    g_h1[hp] = sA.n

                                    p.emit("vector", lambda e, hj=hj, hp=hp:
                                           e.tensor_tensor(t1t[hj % 2], s1t[hj % 2],
                                                           ps_h3[hp][:, 0:TF], OP.mult),
                                           waits=[(sA, sA.n), (sP, g_gbf)], inc=sV)

                                    def gat(e, hj=hj, e_=e_, c2=c2):
                                        tp = t1t[hj % 2]
                                        if e_ < 8:
                                            ins = e.tensor_tensor(gbf[:, hj, :], tp,
                                                                  web[c2], OP.mult)
                                        else:
                                            ins = e.tensor_copy(gbf[:, hj, :], tp)
                                        return ins
                                    p.emit("vector", gat,
                                           waits=[(cWB[c2], ldwe), (sV, sV.n)], inc=sV)
                                    g_h3[hp] = sV.n
                                    g_s1[hj % 2] = sV.n
                                g_done = sV.n
                                g_web[c2] = sV.n
                                for ob in range(8):
                                    po = ob % 2

                                    def mmo(e, wpr=wpr, ob=ob, po=po):
                                        for hk in range(8):
                                            ins = e.matmul(
                                                ps_o[po][:, 0:TF],
                                                w2sb[wpr][:, hk, ob * 128:(ob + 1) * 128],
                                                gbf[:, hk, :],
                                                start=(hk == 0), stop=(hk == 7))
                                        return ins
                                    p.emit("tensor", mmo,
                                           waits=[(sV, g_done), (sV, g_po[po])], inc=sP)
                                    on = sP.n
                                    p.emit("vector", lambda e, ob=ob, po=po, cs=csl:
                                           e.tensor_tensor(acc[:, ob, cs], acc[:, ob, cs],
                                                           ps_o[po][:, 0:TF], OP.add),
                                           waits=[(sP, on)], inc=sV)
                                    g_po[po] = sV.n
                                g_gbf = sP.n
                            g_w[wpr] = sP.n
                        experts_done = (sP.n, sV.n)
                    with ExitStack() as gc2:
                        gq = sbt(gc2, "gq_q%d" % q, [128, 8, LQ], F32R)
                        x5q = sbt(gc2, "x5q_q%d" % q, [128, 8, LQ], F32)
                        x0t = [sbt(gc2, "x0t%d_q%d" % (i, q), [128, D], F32) for i in range(2)]
                        ott = [sbt(gc2, "ott%d_q%d" % (i, q), [128, D], F32) for i in range(2)]
                        ps_f = [pst(gc2, "pf%d_q%d" % (i, q), [128, D]) for i in range(2)]
                        p.emit("scalar", lambda e, o=x5q, i_=xT0_h[:, :, qsl]: e.dma_start(
                            out=o, in_=i_),
                            waits=dst_w + [(sP, experts_done[0])], inc16=cX5)
                        ldx5 = cX5.n
                        p.emit("scalar", lambda e: e.activation(fl(gq), fl(acc), AF.Silu),
                               waits=[(sV, experts_done[1]), (sP, experts_done[0])], inc=sA)
                        p.emit("vector", lambda e: e.tensor_tensor(fl(acc), fl(x5q), fl(gq),
                                                                   OP.mult),
                               waits=[(sA, sA.n), (cX5, ldx5)], inc=sV)
                        sg_done = sV.n
                        g_ft = [0, 0]; g_psf = [0, 0]; g_ot = [0, 0]
                        for tt in range(8):
                            pr = tt % 2
                            row = q * LQ + tt * 128

                            def trf(e, pr=pr, tt=tt):
                                for dj in range(8):
                                    ins = e.transpose(
                                        ps_f[pr][:, dj * 128:(dj + 1) * 128],
                                        acc[:, dj, tt * 128:(tt + 1) * 128], id32)
                                return ins
                            p.emit("tensor", trf, waits=[(sV, sg_done), (sV, g_psf[pr])],
                                   inc=sP)
                            trn = sP.n
                            p.emit("scalar", lambda e, o=x0t[pr],
                                   i_=x_in[row:row + 128, :]: e.dma_start(out=o, in_=i_),
                                   waits=[(sV, g_ft[pr])], inc16=cX0[pr])
                            ldx0 = cX0[pr].n
                            p.emit("vector", lambda e, pr=pr: e.tensor_tensor(
                                ott[pr], x0t[pr], ps_f[pr], OP.add),
                                waits=[(sP, trn), (cX0[pr], ldx0), (cOU[pr], g_ot[pr])],
                                inc=sV)
                            g_psf[pr] = sV.n
                            g_ft[pr] = sV.n
                            p.emit("sync", lambda e, pr=pr, o=out_h[row:row + 128, :]:
                                   e.dma_start(out=o, in_=ott[pr]), waits=[(sV, sV.n)],
                                   inc16=cOU[pr])
                            g_ot[pr] = cOU[pr].n

        p.emit("sync", lambda e: e.nop(),
               waits=[(c, c.n) for c in all_dma_ctrs])

        block = ctx.enter_context(nc.Block())
        p.build(block)
    return nc


_NC_CACHE = {}


def _get_nc(key):
    if key not in _NC_CACHE:
        _NC_CACHE[key] = build(*key)
    return _NC_CACHE[key]


def kernel(L0_attn_x, L0_attn_W_in, L0_attn_conv_w, L0_attn_conv_b, L0_attn_W_x,
           L0_attn_W_dt, L0_attn_b_dt, L0_attn_A_log, L0_attn_D, L0_attn_W_out,
           L0_ffn_router_w, L0_ffn_expert_bias, L0_ffn_w1, L0_ffn_w2, L0_ffn_w3,
           L0_ffn_sw1, L0_ffn_sw2, L0_ffn_sw3,
           _depth=NL, _do_ffn=True, _dev_out=None, _cores=8, _dbg_spill=False,
           _ret=None):
    nc = _get_nc((_depth, _do_ffn, _dev_out, _dbg_spill))
    common = {
        "W_in": np.ascontiguousarray(L0_attn_W_in),
        "conv_w": np.ascontiguousarray(L0_attn_conv_w),
        "conv_b": np.ascontiguousarray(L0_attn_conv_b),
        "W_x": np.ascontiguousarray(L0_attn_W_x),
        "W_dt": np.ascontiguousarray(L0_attn_W_dt),
        "b_dt": np.ascontiguousarray(L0_attn_b_dt),
        "A_log": np.ascontiguousarray(L0_attn_A_log),
        "Dvec": np.ascontiguousarray(L0_attn_D),
        "W_out": np.ascontiguousarray(L0_attn_W_out),
        "rw": np.ascontiguousarray(L0_ffn_router_w),
        "ebias": np.ascontiguousarray(L0_ffn_expert_bias.reshape(1, NE)),
        "w1": np.ascontiguousarray(L0_ffn_w1),
        "w2": np.ascontiguousarray(L0_ffn_w2),
        "w3": np.ascontiguousarray(L0_ffn_w3),
        "sw1": np.ascontiguousarray(L0_ffn_sw1),
        "sw2": np.ascontiguousarray(L0_ffn_sw2),
        "sw3": np.ascontiguousarray(L0_ffn_sw3),
        "ident": np.eye(128, dtype=np.float32),
    }
    in_maps = []
    for b in range(_cores):
        m = dict(common)
        m["x"] = np.ascontiguousarray(L0_attn_x[b])
        in_maps.append(m)
    res = run_bass_kernel_spmd(nc, in_maps, core_ids=list(range(_cores)))
    if _ret is not None:
        return res
    return np.stack([res.results[b]["out"] for b in range(_cores)], axis=0)


# version 08
# speedup vs baseline: 1.4647x; 1.1959x over previous
import numpy as np
from contextlib import ExitStack

import concourse.bass as bass
import concourse.mybir as mybir
from concourse.bass_utils import run_bass_kernel_spmd

F32 = mybir.dt.float32
F32R = mybir.dt.float32r
BF16 = mybir.dt.bfloat16
AF = mybir.ActivationFunctionType
OP = mybir.AluOpType
AX = mybir.AxisListType

L, D, D2, E4, NST, DTR, NE, NL = 4096, 1024, 2048, 4096, 16, 64, 8, 5
T1, T2, TF, NQ = 512, 256, 512, 4
NC1, NC2, LQ = L // T1, L // T2, L // NQ


class Ctr:
    def __init__(self, sem, name):
        self.sem, self.name, self.n = sem, name, 0


class Prog:
    ENGINES = ("sync", "scalar", "vector", "tensor", "gpsimd")

    def __init__(self):
        self.ops = []

    def emit(self, engine, fn, waits=(), inc=None, inc16=None):
        ws = [(c.sem, v) for (c, v) in waits if v > 0]
        incs = []
        if inc is not None:
            inc.n += 1
            incs.append((inc.sem, 1))
        if inc16 is not None:
            inc16.n += 16
            incs.append((inc16.sem, 16))
        self.ops.append((engine, fn, ws, incs))

    def build(self, block):
        by = {e: [] for e in self.ENGINES}
        for op in self.ops:
            by[op[0]].append(op)

        def mk(name):
            def body(eng):
                waited = {}
                for _, fn, ws, incs in by[name]:
                    for sem, val in ws:
                        k = id(sem)
                        if waited.get(k, -1) < val:
                            eng.wait_ge(sem, val)
                            waited[k] = val
                    instr = fn(eng)
                    for sem, kk in incs:
                        instr.then_inc(sem, kk)
            return body

        for name in self.ENGINES:
            if by[name]:
                getattr(block, name)(mk(name))


def build(depth=NL, do_ffn=True, dev_out=None, dbg_spill=False, skip_last_p2=False):
    nc = bass.Bass("TRN2", target_bir_lowering=False)
    x_in = nc.dram_tensor("x", [L, D], F32, kind="ExternalInput").ap()
    Wi_h = nc.dram_tensor("W_in", [NL, E4, D], F32, kind="ExternalInput").ap()
    cw_h = nc.dram_tensor("conv_w", [NL, D2, 4], F32, kind="ExternalInput").ap()
    cb_h = nc.dram_tensor("conv_b", [NL, D2], F32, kind="ExternalInput").ap()
    Wx_h = nc.dram_tensor("W_x", [NL, 96, D2], F32, kind="ExternalInput").ap()
    Wdt_h = nc.dram_tensor("W_dt", [NL, D2, DTR], F32, kind="ExternalInput").ap()
    bdt_h = nc.dram_tensor("b_dt", [NL, D2], F32, kind="ExternalInput").ap()
    Al_h = nc.dram_tensor("A_log", [NL, D2, NST], F32, kind="ExternalInput").ap()
    Dp_h = nc.dram_tensor("Dvec", [NL, D2], F32, kind="ExternalInput").ap()
    Wo_h = nc.dram_tensor("W_out", [NL, D, D2], F32, kind="ExternalInput").ap()
    rw_h = nc.dram_tensor("rw", [D, NE], F32, kind="ExternalInput").ap()
    eb_h = nc.dram_tensor("ebias", [1, NE], F32, kind="ExternalInput").ap()
    w1_h = nc.dram_tensor("w1", [NE, D, D], F32, kind="ExternalInput").ap()
    w2_h = nc.dram_tensor("w2", [NE, D, D], F32, kind="ExternalInput").ap()
    w3_h = nc.dram_tensor("w3", [NE, D, D], F32, kind="ExternalInput").ap()
    sw1_h = nc.dram_tensor("sw1", [D, D], F32, kind="ExternalInput").ap()
    sw2_h = nc.dram_tensor("sw2", [D, D], F32, kind="ExternalInput").ap()
    sw3_h = nc.dram_tensor("sw3", [D, D], F32, kind="ExternalInput").ap()
    id_h = nc.dram_tensor("ident", [128, 128], F32, kind="ExternalInput").ap()
    out_h = nc.dram_tensor("out", [L, D], F32, kind="ExternalOutput").ap()

    xk = "ExternalOutput" if dbg_spill else "Internal"
    xnT_h = nc.dram_tensor("xnT", [128, 8, L], F32, kind=xk).ap()
    xT0_h = nc.dram_tensor("xT0", [128, 8, L], F32, kind=xk).ap()
    xT1_h = nc.dram_tensor("xT1", [128, 8, L], F32, kind=xk).ap()
    spill_kind = "ExternalOutput" if dbg_spill else "Internal"
    xa_h = nc.dram_tensor("xa_s", [128, 16, L], BF16, kind=spill_kind).ap()
    zs_h = nc.dram_tensor("zs_s", [128, 16, L], F32, kind=spill_kind).ap()
    dbgd_h = (nc.dram_tensor("dbg_del", [128, 16, L], F32, kind="ExternalOutput").ap()
              if dbg_spill else None)
    dbgy_h = (nc.dram_tensor("dbg_yac", [128, 16, L], F32, kind="ExternalOutput").ap()
              if dbg_spill else None)
    dbgg_h = (nc.dram_tensor("dbg_yg", [128, 16, L], mybir.dt.bfloat16,
                             kind="ExternalOutput").ap()
              if dbg_spill else None)
    bcs_h = nc.dram_tensor("bcs", [32, L], F32).ap()
    weH_h = nc.dram_tensor("weH", [NE, L], F32, kind=spill_kind).ap()

    p = Prog()
    ctx = ExitStack()
    with ctx:
        def mkc(name):
            return Ctr(ctx.enter_context(nc.semaphore(name)), name)

        sP, sA, sV = mkc("sP"), mkc("sA"), mkc("sV")
        sG = mkc("sG")
        cId = mkc("cId")
        cLNx = [mkc("cLNx%d" % i) for i in range(2)]
        cLNs = [mkc("cLNs%d" % i) for i in range(2)]
        cW1 = [mkc("cW1_%d" % i) for i in range(2)]
        cC1 = mkc("cC1")
        cXC = [mkc("cXC%d" % i) for i in range(2)]
        cZS = [mkc("cZS%d" % i) for i in range(2)]
        cXA = [mkc("cXA%d" % i) for i in range(2)]
        cW2 = mkc("cW2")
        cC2 = mkc("cC2")
        cXAB = mkc("cXAB")
        cXR = [mkc("cXR%d" % i) for i in range(2)]
        cZL = mkc("cZL")
        cBW = mkc("cBW")
        cBC = [mkc("cBC%d" % i) for i in range(4)]
        cDS = [mkc("cDS%d" % i) for i in range(2)]
        cRW = mkc("cRW")
        cXQ = [mkc("cXQ%d" % i) for i in range(2)]
        cWE = mkc("cWE")
        cXN = mkc("cXN")
        cFW = [mkc("cFW%d" % i) for i in range(2)]
        cWB = [mkc("cWB%d" % i) for i in range(2)]
        cX5 = mkc("cX5")
        cX0 = [mkc("cX0_%d" % i) for i in range(2)]
        cOU = [mkc("cOU%d" % i) for i in range(2)]
        all_dma_ctrs = ([cId, cC1, cW2, cC2, cXAB, cZL, cBW, cRW, cWE, cXN, cX5]
                        + cLNx + cLNs + cW1 + cXC + cZS + cXA + cXR + cBC + cDS
                        + cXQ + cFW + cWB + cX0 + cOU)

        def sbt(c, name, shape, dt):
            return c.enter_context(nc.sbuf_tensor(name, shape, dt)).ap()

        def pst(c, name, shape, dt=F32):
            return c.enter_context(nc.psum_tensor(name, shape, dt)).ap()

        id32 = sbt(ctx, "id32", [128, 128], F32)
        id16 = sbt(ctx, "id16", [128, 128], BF16)
        idr = sbt(ctx, "idr", [128, 128], F32R)

        p.emit("sync", lambda e: e.dma_start(out=id32, in_=id_h), inc16=cId)
        p.emit("vector", lambda e: e.tensor_copy(id16, id32), waits=[(cId, 16)])
        p.emit("vector", lambda e: e.tensor_copy(idr, id32), inc=sV)

        fl = lambda ap: ap.rearrange("p j t -> p (j t)")

        with ExitStack() as lc:
            xt = [sbt(lc, "ln_x%d" % i, [128, D], F32) for i in range(2)]
            x2t = [sbt(lc, "ln_x2%d" % i, [128, D], F32) for i in range(2)]
            st = [sbt(lc, "ln_st%d" % i, [128, 8], F32) for i in range(2)]
            xnt = [sbt(lc, "ln_xn%d" % i, [128, D], F32R) for i in range(2)]
            stg = [sbt(lc, "ln_stg%d" % i, [128, D], F32) for i in range(2)]
            ps_ln = [pst(lc, "ln_ps%d" % i, [128, D], F32R) for i in range(2)]
            g_x = [0, 0]; g_x2 = [0, 0]; g_st = [0, 0]; g_xn = [0, 0]
            g_ps = [0, 0]; g_stg = [0, 0]
            for tt in range(32):
                pr = tt % 2
                p.emit("sync", lambda e, b=xt[pr], r=x_in[tt * 128:(tt + 1) * 128, :]:
                       e.dma_start(out=b, in_=r), waits=[(sV, g_x[pr])], inc16=cLNx[pr])
                ld = cLNx[pr].n
                p.emit("scalar", lambda e, b=xt[pr], o=x2t[pr]: e.activation(o, b, AF.Square),
                       waits=[(cLNx[pr], ld), (sV, g_x2[pr])], inc=sA)
                sq = sA.n
                s_ = st[pr]
                p.emit("vector", lambda e, b=xt[pr], s=s_: e.reduce_sum(s[:, 0:1], b, AX.X),
                       waits=[(cLNx[pr], ld), (sA, g_st[pr])], inc=sV)
                r1n = sV.n
                p.emit("vector", lambda e, o=x2t[pr], s=s_: e.reduce_sum(s[:, 1:2], o, AX.X),
                       waits=[(sA, sq)], inc=sV)
                g_x2[pr] = sV.n
                p.emit("vector", lambda e, s=s_: e.tensor_scalar(
                    s[:, 2:3], s[:, 0:1], 1.0 / D, None, OP.mult),
                    waits=[(sV, r1n)], inc=sV)
                p.emit("vector", lambda e, s=s_: e.tensor_tensor(
                    s[:, 3:4], s[:, 2:3], s[:, 2:3], OP.mult),
                    waits=[(sV, sV.n)], inc=sV)
                p.emit("vector", lambda e, s=s_: e.scalar_tensor_tensor(
                    s[:, 4:5], s[:, 1:2], 1.0 / D, s[:, 3:4], OP.mult, OP.subtract),
                    waits=[(sV, sV.n)], inc=sV)
                p.emit("vector", lambda e, s=s_: e.tensor_scalar(
                    s[:, 4:5], s[:, 4:5], 1e-5, None, OP.add),
                    waits=[(sV, sV.n)], inc=sV)
                var = sV.n
                p.emit("scalar", lambda e, s=s_: e.activation(
                    s[:, 5:6], s[:, 4:5], AF.Ln), waits=[(sV, var)], inc=sA)
                p.emit("vector", lambda e, s=s_: e.tensor_scalar(
                    s[:, 5:6], s[:, 5:6], -0.5, None, OP.mult), waits=[(sA, sA.n)], inc=sV)
                p.emit("scalar", lambda e, s=s_: e.activation(
                    s[:, 6:7], s[:, 5:6], AF.Exp), waits=[(sV, sV.n)], inc=sA)
                rst = sA.n
                p.emit("vector", lambda e, b=xt[pr], s=s_, o=xnt[pr]: e.tensor_scalar(
                    o, b, s[:, 2:3], s[:, 6:7], OP.subtract, OP.mult),
                    waits=[(sA, rst), (sP, g_xn[pr])], inc=sV)
                xn_done = sV.n
                g_x[pr] = sV.n
                g_st[pr] = sA.n

                def tr(e, o=ps_ln[pr], i_=xnt[pr]):
                    for dj in range(8):
                        ins = e.transpose(o[:, dj * 128:(dj + 1) * 128],
                                          i_[:, dj * 128:(dj + 1) * 128], idr)
                    return ins
                p.emit("tensor", tr, waits=[(sV, xn_done), (sV, g_ps[pr])], inc=sP)
                g_xn[pr] = sP.n
                p.emit("vector", lambda e, ps=ps_ln[pr], o=stg[pr]: e.tensor_copy(o, ps),
                       waits=[(sP, sP.n), (cLNs[pr], g_stg[pr])], inc=sV)
                g_ps[pr] = sV.n
                p.emit("sync", lambda e, o=xnT_h[:, :, tt * 128:(tt + 1) * 128], i_=stg[pr]:
                       e.dma_start(out=o, in_=i_.rearrange("p (j t) -> p j t", j=8)),
                       waits=[(sV, sV.n)], inc16=cLNs[pr])
                g_stg[pr] = cLNs[pr].n

        srcs = [xnT_h, xT0_h, xT1_h, xT0_h, xT1_h]
        dsts = [xT0_h, xT1_h, xT0_h, xT1_h, xT0_h]

        for li in range(depth):
            src, dst = srcs[li], dsts[li]
            barrier = (sP.n, sA.n, sV.n)
            bw = [(sP, barrier[0]), (sA, barrier[1]), (sV, barrier[2])]
            src_w = ([(c, c.n) for c in cLNs] if li == 0
                     else [(c, c.n) for c in cDS])
            spill_w = [(c, c.n) for c in cZS + cXA]

            with ExitStack() as lc:
                WiT = sbt(lc, "WiT_L%d" % li, [128, 8, E4], BF16)
                stg16 = [sbt(lc, "stg16_%d_L%d" % (i, li), [128, 2048], BF16) for i in range(2)]
                xcbf = [sbt(lc, "xcbf%d_L%d" % (i, li), [128, 8, T1], BF16) for i in range(2)]
                xim = sbt(lc, "xim_L%d" % li, [128, 16, T1 + 3], F32R)
                convb = sbt(lc, "convb_L%d" % li, [128, 16, T1], F32)
                xat = [sbt(lc, "xat%d_L%d" % (i, li), [128, T1], BF16) for i in range(2)]
                zst = [sbt(lc, "zst%d_L%d" % (i, li), [128, T1], F32R) for i in range(2)]
                cwsb = sbt(lc, "cwsb_L%d" % li, [128, 16, 4], F32)
                cbv = sbt(lc, "cbv_L%d" % li, [128, 16], F32)
                v16 = sbt(lc, "v16_L%d" % li, [16, 128], F32)
                ps_pr = [pst(lc, "p1pr%d_L%d" % (i, li), [128, 1024], BF16) for i in range(2)]
                ps_mm = [pst(lc, "p1mm%d_L%d" % (i, li), [128, T1]) for i in range(2)]
                ps_v = pst(lc, "p1v_L%d" % li, [128, 512])

                g_stg = [0, 0]; g_ps = [0, 0]
                for ebk in range(32):
                    pr = ebk % 2
                    p.emit("gpsimd", lambda e, o=stg16[pr],
                           i_=Wi_h[li, ebk * 128:(ebk + 1) * 128, :]:
                           e.dma_start(out=o[:, 0:1024], in_=i_),
                           waits=bw + src_w + [(sP, g_stg[pr]), (cBW, cBW.n)],
                           inc16=cW1[pr])
                    ld = cW1[pr].n

                    def trw(e, ps=ps_pr[pr], s=stg16[pr]):
                        for dj in range(8):
                            ins = e.transpose(ps[:, dj * 128:(dj + 1) * 128],
                                              s[:, dj * 128:(dj + 1) * 128], id16)
                        return ins
                    p.emit("tensor", trw, waits=bw + [(cW1[pr], ld), (sV, g_ps[pr])], inc=sP)
                    g_stg[pr] = sP.n
                    p.emit("vector", lambda e, o=WiT[:, :, ebk * 128:(ebk + 1) * 128],
                           ps=ps_pr[pr]:
                           e.tensor_copy(o, ps.rearrange("p (j t) -> p j t", j=8)),
                           waits=bw + [(sP, sP.n)], inc=sV)
                    g_ps[pr] = sV.n
                p.emit("sync", lambda e, o=cwsb,
                       i_=cw_h[li].rearrange("(j p) k -> p j k", p=128):
                       e.dma_start(out=o, in_=i_),
                       waits=bw + src_w + [(cBW, cBW.n)], inc16=cC1)
                p.emit("sync", lambda e, o=v16,
                       i_=cb_h[li].rearrange("(j p) -> j p", p=128):
                       e.dma_start(out=o, in_=i_),
                       waits=bw + src_w + [(cBW, cBW.n)], inc16=cC1)
                ldcb = cC1.n
                p.emit("tensor", lambda e: e.transpose(ps_v[:, 0:16], v16, id32[0:16, 0:16]),
                       waits=bw + [(cC1, ldcb)], inc=sP)
                p.emit("vector", lambda e: e.tensor_copy(cbv, ps_v[:, 0:16]),
                       waits=[(sP, sP.n)], inc=sV)
                for j in range(16):
                    p.emit("vector", lambda e, j=j: e.memset(xim[:, j, 0:3].bitcast(F32), 0.0),
                           inc=(sV if j == 15 else None))
                wit_done = sV.n
                ldcw = cC1.n

                g_xc = [0, 0]; g_ps1 = [0, 0]
                g_xat = [0, 0]; g_zst = [0, 0]
                g_cv = 0
                evac_last = 0
                for c in range(NC1):
                    pr = c % 2
                    p.emit("gpsimd", lambda e, o=xcbf[pr],
                           i_=src[:, :, c * T1:(c + 1) * T1]:
                           e.dma_start(out=o, in_=i_),
                           waits=bw + src_w + [(sP, g_xc[pr])], inc16=cXC[pr])
                    ldx = cXC[pr].n
                    if c > 0:
                        p.emit("vector", lambda e: e.tensor_copy(
                            xim[:, :, 0:3], xim[:, :, T1:T1 + 3]),
                            waits=[(sA, evac_last)], inc=sV)
                    ms = sV.n
                    evac0 = sA.n
                    for ebk in range(32):
                        pp = ebk % 2

                        def mm(e, pp=pp, ebk=ebk, pr=pr):
                            for dk in range(8):
                                ins = e.matmul(ps_mm[pp][:, 0:T1],
                                               WiT[:, dk, ebk * 128:(ebk + 1) * 128],
                                               xcbf[pr][:, dk, :],
                                               start=(dk == 0), stop=(dk == 7))
                            return ins
                        p.emit("tensor", mm,
                               waits=bw + [(cXC[pr], ldx), (sA, g_ps1[pp]),
                                           (sV, wit_done)], inc=sP)
                        mmn = sP.n
                        if ebk < 16:
                            p.emit("scalar", lambda e, o=xim[:, ebk, 3:3 + T1], ps=ps_mm[pp]:
                                   e.activation(o, ps[:, 0:T1], AF.Copy),
                                   waits=[(sP, mmn), (sV, ms)], inc=sA)
                        else:
                            zpr = ebk % 2
                            p.emit("scalar", lambda e, ps=ps_mm[pp], o=zst[zpr]:
                                   e.activation(o, ps[:, 0:T1], AF.Silu),
                                   waits=[(sP, mmn), (cZS[zpr], g_zst[zpr])], inc=sA)
                            p.emit("scalar", lambda e, o=zs_h[:, ebk - 16, c * T1:(c + 1) * T1],
                                   i_=zst[zpr].bitcast(F32): e.dma_start(out=o, in_=i_),
                                   waits=[(sA, sA.n)], inc16=cZS[zpr])
                            g_zst[zpr] = cZS[zpr].n
                        g_ps1[pp] = sA.n
                    evac_last = sA.n
                    g_xc[pr] = sP.n
                    for k in range(4):
                        for j in range(16):
                            evj = evac0 + j + 1
                            if k == 0:
                                p.emit("vector", lambda e, j=j: e.tensor_scalar(
                                    convb[:, j, :], xim[:, j, 0:T1],
                                    cwsb[:, j:j + 1, 0:1].squeeze(1),
                                    cbv[:, j:j + 1], OP.mult, OP.add),
                                    waits=[(sA, evj), (cC1, ldcw), (sV, wit_done),
                                           (sA, g_cv)])
                            else:
                                p.emit("vector", lambda e, j=j, k=k:
                                       e.scalar_tensor_tensor(
                                           convb[:, j, :], xim[:, j, k:k + T1],
                                           cwsb[:, j:j + 1, k:k + 1].squeeze(1),
                                           convb[:, j, :], OP.mult, OP.add),
                                       waits=[(sA, evj)],
                                       inc=(sV if (k == 3 and j == 15) else None))
                    cv_done = sV.n
                    for j in range(16):
                        apr = j % 2
                        p.emit("scalar", lambda e, j=j, o=xat[apr]:
                               e.activation(o, convb[:, j, :], AF.Silu),
                               waits=[(sV, cv_done), (cXA[apr], g_xat[apr])], inc=sA)
                        p.emit("scalar", lambda e, o=xa_h[:, j, c * T1:(c + 1) * T1],
                               i_=xat[apr]: e.dma_start(out=o, in_=i_),
                               waits=[(sA, sA.n)], inc16=cXA[apr])
                        g_xat[apr] = cXA[apr].n
                    g_cv = sA.n

            if skip_last_p2 and li == depth - 1:
                continue
            barrier = (sP.n, sA.n, sV.n)
            bw = [(sP, barrier[0]), (sA, barrier[1]), (sV, barrier[2])]
            spill_w = [(c, c.n) for c in cZS + cXA]
            with ExitStack() as lc:
                WoT = sbt(lc, "WoT_L%d" % li, [128, 16, D], BF16)
                WxT = sbt(lc, "WxT_L%d" % li, [128, 16, 96], BF16)
                WdtT = sbt(lc, "WdtT_L%d" % li, [64, D2], F32R)
                Asb = sbt(lc, "Asb_L%d" % li, [128, 16, NST], F32)
                Dpv = sbt(lc, "Dpv_L%d" % li, [128, 16], F32)
                bdtv = sbt(lc, "bdtv_L%d" % li, [128, 16], F32)
                v16b = sbt(lc, "v16b_L%d" % li, [16, 128], F32)
                state = sbt(lc, "state_L%d" % li, [128, 256], F32)

                with ExitStack() as pc:
                    stg16b = sbt(pc, "stg16b_L%d" % li, [128, 2048], BF16)
                    stgr = sbt(pc, "stgr_L%d" % li, [128, 2048], F32R)
                    ps_prb = pst(pc, "p2prb_L%d" % li, [128, 2048], BF16)
                    ps_prf = pst(pc, "p2prf_L%d" % li, [128, 512])
                    ps_prr = pst(pc, "p2prr_L%d" % li, [128, 512], F32R)
                    g_stg = 0
                    g_ps = 0
                    for ob in range(8):
                        p.emit("gpsimd", lambda e, o=stg16b,
                               i_=Wo_h[li, ob * 128:(ob + 1) * 128, :]:
                               e.dma_start(out=o, in_=i_),
                               waits=bw + spill_w + [(sP, g_stg)], inc16=cW2)
                        ld = cW2.n

                        def trw(e):
                            for dk in range(16):
                                ins = e.transpose(ps_prb[:, dk * 128:(dk + 1) * 128],
                                                  stg16b[:, dk * 128:(dk + 1) * 128], id16)
                            return ins
                        p.emit("tensor", trw, waits=bw + [(cW2, ld), (sV, g_ps)], inc=sP)
                        g_stg = sP.n
                        p.emit("vector", lambda e, o=WoT[:, :, ob * 128:(ob + 1) * 128]:
                               e.tensor_copy(o, ps_prb.rearrange("p (j t) -> p j t", j=16)),
                               waits=bw + [(sP, sP.n)], inc=sV)
                        g_ps = sV.n
                    p.emit("gpsimd", lambda e, o=stg16b[0:96, :], i_=Wx_h[li]:
                           e.dma_start(out=o, in_=i_), waits=[(sV, g_ps)], inc16=cW2)
                    ldwx = cW2.n
                    for dk in range(16):
                        p.emit("tensor", lambda e, dk=dk: e.transpose(
                            ps_prb[:, 0:96], stg16b[0:96, dk * 128:(dk + 1) * 128],
                            id16[0:96, 0:96]), waits=[(cW2, ldwx), (sV, sV.n)], inc=sP)
                        p.emit("vector", lambda e, dk=dk: e.tensor_copy(
                            WxT[:, dk, :], ps_prb[:, 0:96]), waits=[(sP, sP.n)], inc=sV)
                    p.emit("gpsimd", lambda e,
                           o=stgr[:, 0:1024].rearrange("p (dk r) -> p dk r", dk=16),
                           i_=Wdt_h[li].rearrange("(dk p) r -> p dk r", p=128):
                           e.dma_start(out=o, in_=i_), waits=[(sV, sV.n)], inc16=cW2)
                    ldwd = cW2.n
                    for dk in range(16):
                        p.emit("tensor", lambda e, dk=dk: e.transpose(
                            ps_prr[0:64, 0:128], stgr[:, dk * 64:(dk + 1) * 64], idr),
                            waits=[(cW2, ldwd), (sV, sV.n)], inc=sP)
                        p.emit("vector", lambda e, dk=dk: e.tensor_copy(
                            WdtT[:, dk * 128:(dk + 1) * 128], ps_prr[0:64, 0:128]),
                            waits=[(sP, sP.n)], inc=sV)
                    p.emit("sync", lambda e, o=Asb,
                           i_=Al_h[li].rearrange("(j p) n -> p j n", p=128):
                           e.dma_start(out=o, in_=i_),
                           waits=bw + spill_w, inc16=cC2)
                    ldA = cC2.n
                    p.emit("scalar", lambda e: e.activation(fl(Asb), fl(Asb), AF.Exp),
                           waits=bw + [(cC2, ldA)], inc=sA)
                    p.emit("vector", lambda e: e.tensor_scalar(
                        fl(Asb), fl(Asb), -1.0, None, OP.mult), waits=[(sA, sA.n)], inc=sV)
                    for vec_h, vdst in ((Dp_h[li], Dpv), (bdt_h[li], bdtv)):
                        p.emit("sync", lambda e, i_=vec_h.rearrange("(j p) -> j p", p=128):
                               e.dma_start(out=v16b, in_=i_),
                               waits=spill_w + [(sP, sP.n)], inc16=cC2)
                        ldv = cC2.n
                        p.emit("tensor", lambda e: e.transpose(
                            ps_prf[:, 0:16], v16b, id32[0:16, 0:16]),
                            waits=[(cC2, ldv), (sV, sV.n)], inc=sP)
                        p.emit("vector", lambda e, o=vdst: e.tensor_copy(o, ps_prf[:, 0:16]),
                               waits=[(sP, sP.n)], inc=sV)
                    p.emit("vector", lambda e: e.memset(state, 0.0), inc=sV)
                prep_done = sV.n

                xab = sbt(lc, "xab_L%d" % li, [128, 16, T2], BF16)
                delb = sbt(lc, "delb_L%d" % li, [128, 16, T2], F32)
                dub = sbt(lc, "dub_L%d" % li, [128, 16, T2], F32)
                dAb = [sbt(lc, "dAb%d_L%d" % (i, li), [128, 16, T2], F32) for i in range(2)]
                dBuA = [sbt(lc, "dBuA%d_L%d" % (i, li), [128, 16, T2], F32) for i in range(2)]
                dBuB = sbt(lc, "dBuB_L%d" % li, [128, 16, T2], F32)
                yac = sbt(lc, "yac_L%d" % li, [128, 16, T2], F32)
                ygbf = dBuA[1].bitcast(BF16)[:, :, 0:T2]
                xres = sbt(lc, "xres_L%d" % li, [128, 8, T2], F32)
                dtb = sbt(lc, "dtb_L%d" % li, [64, T2], F32R)
                bcb = sbt(lc, "bcb_L%d" % li, [32, T2], F32)
                bcc = [sbt(lc, "bcc%d_L%d" % (i, li), [128, 2, T2], F32) for i in range(4)]
                ps_xd = pst(lc, "p2xd_L%d" % li, [128, 512])
                ps_del = [pst(lc, "p2dl%d_L%d" % (i, li), [128, 512]) for i in range(2)]
                ps_wo = [pst(lc, "p2wo%d_L%d" % (i, li), [128, 512]) for i in range(2)]

                JD = 3
                g_xab = 0
                g_xres = 0
                g_xd = 0
                g_del = [0, 0]
                g_wo = [0, 0]
                g_bcb = 0
                g_scanA = [0, 0]
                g_m1 = [0, 0]
                g_exp = [0, 0]
                g_hV = [0, 0]
                g_hG = [0, 0]
                g_warA = [[], []]
                g_bccV = [0, 0, 0, 0]
                g_bccG = [0, 0, 0, 0]
                bcc_ld = [0, 0, 0, 0]
                g_yaddD = 0
                g_yaddP = 0
                g_memset = 0
                g_skp = 0
                g_yg = 0
                g_dub = 0
                g_m1last = 0
                for c in range(NC2):
                    sl = slice(c * T2, (c + 1) * T2)
                    p.emit("sync", lambda e, i_=xa_h[:, :, sl]: e.dma_start(out=xab, in_=i_),
                           waits=bw + spill_w + [(sV, g_xab)], inc16=cXAB)
                    ldxa = cXAB.n
                    p.emit("scalar", lambda e, o=xres, i_=src[:, :, sl]: e.dma_start(
                        out=o, in_=i_),
                        waits=bw + src_w + [(cDS[0], g_xres)], inc16=cXR[0])
                    ldxr = cXR[0].n

                    def mmx(e):
                        for dk in range(16):
                            ins = e.matmul(ps_xd[0:96, 0:T2], WxT[:, dk, :], xab[:, dk, :],
                                           start=(dk == 0), stop=(dk == 15))
                        return ins
                    p.emit("tensor", mmx, waits=[(cXAB, ldxa), (sV, g_xd), (sV, prep_done)],
                           inc=sP)
                    mmxn = sP.n
                    p.emit("vector", lambda e: e.tensor_copy(dtb, ps_xd[0:64, 0:T2]),
                           waits=[(sP, mmxn), (cBW, g_bcb)])
                    p.emit("vector", lambda e: e.tensor_copy(bcb, ps_xd[64:96, 0:T2]), inc=sV)
                    g_xd = sV.n
                    bc_ev = sV.n
                    p.emit("sync", lambda e, o=bcs_h[:, sl]: e.dma_start(out=o, in_=bcb),
                           waits=[(sV, bc_ev)], inc16=cBW)
                    bcw = cBW.n
                    g_bcb = cBW.n
                    for dk in range(16):
                        pp = dk % 2
                        p.emit("tensor", lambda e, dk=dk, pp=pp: e.matmul(
                            ps_del[pp][:, 0:T2], WdtT[:, dk * 128:(dk + 1) * 128], dtb,
                            start=True, stop=True),
                            waits=[(sV, bc_ev), (sV, g_del[pp])], inc=sP)
                        p.emit("vector", lambda e, dk=dk, pp=pp: e.tensor_scalar(
                            delb[:, dk, :], ps_del[pp][:, 0:T2], bdtv[:, dk:dk + 1],
                            None, OP.add),
                            waits=[(sP, sP.n), (sV, prep_done),
                                   (sA, g_exp[0]), (sA, g_exp[1])], inc=sV)
                        g_del[pp] = sV.n
                    del_done = sV.n
                    p.emit("vector", lambda e: e.tensor_scalar(
                        fl(dBuA[0]), fl(delb), -1.0, None, OP.mult),
                        waits=[(sV, del_done), (sV, g_yg), (cZL, cZL.n)] + g_warA[0],
                        inc=sV)
                    p.emit("vector", lambda e: e.tensor_tensor(
                        fl(dBuA[0]), fl(delb), fl(dBuA[0]), OP.min),
                        waits=[(sV, sV.n)], inc=sV)
                    min_n = sV.n
                    p.emit("scalar", lambda e: e.activation(fl(dAb[0]), fl(dBuA[0]), AF.Exp),
                           waits=[(sV, min_n), (sV, g_yg)])
                    p.emit("scalar", lambda e: e.activation(fl(dBuA[0]), fl(dAb[0]), AF.Ln,
                                                            bias=1.0), inc=sA)
                    p.emit("vector", lambda e: e.scalar_tensor_tensor(
                        fl(delb), fl(delb), 0.0, fl(dBuA[0]), OP.max, OP.add),
                        waits=[(sA, sA.n)], inc=sV)
                    stt_n = sV.n
                    p.emit("vector", lambda e: e.tensor_tensor(fl(dub), fl(delb), fl(xab),
                                                               OP.mult),
                           waits=[(sV, stt_n), (sV, g_m1last)], inc=sV)
                    g_dub = sV.n
                    sp_done = sV.n
                    p.emit("vector", lambda e: e.memset(fl(yac), 0.0),
                           waits=[(sV, g_skp)], inc=sV)
                    g_memset = sV.n
                    for q in range(3):
                        row0 = bcs_h[q:q + 17:16, sl]
                        p.emit("sync", lambda e, o=bcc[q], r=row0: e.dma_start(
                            out=o, in_=r.unsqueeze(0).broadcast_to((128, 2, T2))),
                            waits=[(cBW, bcw), (sV, g_bccV[q])],
                            inc16=cBC[q])
                        bcc_ld[q] = cBC[q].n
                    for nn in range(2):
                        p.emit("vector", lambda e, nn=nn: e.tensor_tensor(
                            dBuA[nn], dub,
                            bcc[nn][:, 0:1, :].broadcast_to((128, 16, T2)), OP.mult),
                            waits=[(sV, g_dub), (cBC[nn], bcc_ld[nn]),
                                   (sV, g_scanA[nn])] + g_warA[nn], inc=sV)
                        g_m1[nn] = sV.n
                        g_warA[nn] = []
                        g_bccV[nn] = sV.n

                    for n in range(NST):
                        bp = n % 2
                        q = n % 4

                        def dAe(e, n=n, bp=bp):
                            for j in range(16):
                                ins = e.activation(dAb[bp][:, j, :], delb[:, j, :], AF.Exp,
                                                   scale=Asb[:, j:j + 1, n:n + 1].squeeze(1))
                            return ins
                        p.emit("scalar", dAe, waits=[(sV, sp_done), (sV, prep_done),
                                                     (sV, g_hV[bp])],
                               inc=sA)
                        g_exp[bp] = sA.n

                        def scn(e, n=n, bp=bp):
                            for j in range(16):
                                ins = e.tensor_tensor_scan(
                                    dAb[bp][:, j, :], dAb[bp][:, j, :],
                                    dBuA[bp][:, j, :],
                                    state[:, n * 16 + j:n * 16 + j + 1],
                                    OP.mult, OP.add)
                            return ins
                        p.emit("vector", scn, waits=[(sA, g_exp[bp]), (sV, g_m1[bp])],
                               inc=sV)
                        scn_n = sV.n
                        g_scanA[bp] = sV.n
                        p.emit("scalar", lambda e, n=n, bp=bp: e.activation(
                            state[:, n * 16:(n + 1) * 16],
                            dAb[bp][:, :, T2 - 1:T2].squeeze(2), AF.Copy),
                            waits=[(sV, scn_n)], inc=sA)
                        st_n = sA.n
                        if n + 2 < NST:
                            bp2 = n % 2
                            q2 = (n + 2) % 4
                            p.emit("vector", lambda e, bp2=bp2, q2=q2: e.tensor_tensor(
                                dBuA[bp2], dub,
                                bcc[q2][:, 0:1, :].broadcast_to((128, 16, T2)), OP.mult),
                                waits=[(sV, g_dub), (cBC[q2], bcc_ld[q2]),
                                       (sV, scn_n)], inc=sV)
                            g_m1[bp2] = sV.n
                            g_bccV[q2] = sV.n
                            if n + 2 == NST - 1:
                                g_m1last = sV.n
                        p.emit("vector", lambda e, bp=bp, q=q: e.tensor_tensor(
                            dAb[bp], dAb[bp],
                            bcc[q][:, 1:2, :].broadcast_to((128, 16, T2)), OP.mult),
                            waits=[(sA, st_n), (cBC[q], bcc_ld[q])], inc=sV)
                        mD = sV.n
                        p.emit("vector", lambda e, bp=bp: e.tensor_tensor(
                            fl(yac), fl(yac), fl(dAb[bp]), OP.add),
                            waits=[(sV, mD), (sV, g_memset)], inc=sV)
                        g_yaddD = sV.n
                        g_hV[bp] = sV.n
                        g_bccV[q] = max(g_bccV[q], mD)
                        if n + 3 < NST:
                            q3 = (n + 3) % 4
                            row3 = bcs_h[n + 3:n + 3 + 17:16, sl]
                            p.emit("sync", lambda e, o=bcc[q3], r=row3: e.dma_start(
                                out=o, in_=r.unsqueeze(0).broadcast_to((128, 2, T2))),
                                waits=[(cBW, bcw), (sV, g_bccV[q3])],
                                inc16=cBC[q3])
                            bcc_ld[q3] = cBC[q3].n

                    p.emit("scalar", lambda e, i_=zs_h[:, :, sl]: e.dma_start(
                        out=dBuA[0], in_=i_),
                        waits=[(sV, g_scanA[0])] + spill_w, inc16=cZL)
                    ldz = cZL.n

                    def skp(e):
                        for j in range(16):
                            ins = e.scalar_tensor_tensor(dAb[0][:, j, :], xab[:, j, :],
                                                         Dpv[:, j:j + 1], yac[:, j, :],
                                                         OP.mult, OP.add)
                        return ins
                    p.emit("vector", skp, waits=[(sV, g_yaddD), (sV, g_hV[0])], inc=sV)
                    g_skp = sV.n
                    g_xab = sV.n
                    p.emit("vector", lambda e: e.tensor_tensor(ygbf, dAb[0], dBuA[0],
                                                               OP.mult),
                           waits=[(cZL, ldz), (sV, g_skp), (sV, g_scanA[1])], inc=sV)
                    g_yg = sV.n
                    g_hV[0] = g_yg
                    g_warA[0] = [(sV, g_yg)]
                    for ob in range(8):
                        pp = ob % 2

                        def mwo(e, ob=ob, pp=pp):
                            for dk in range(16):
                                ins = e.matmul(ps_wo[pp][:, 0:T2],
                                               WoT[:, dk, ob * 128:(ob + 1) * 128],
                                               ygbf[:, dk, :], start=(dk == 0),
                                               stop=(dk == 15))
                            return ins
                        p.emit("tensor", mwo, waits=[(sV, g_yg), (sV, g_wo[pp])], inc=sP)
                        won = sP.n
                        p.emit("vector", lambda e, ob=ob, pp=pp: e.tensor_tensor(
                            xres[:, ob, :], xres[:, ob, :], ps_wo[pp][:, 0:T2], OP.add),
                            waits=[(sP, won), (cXR[0], ldxr)], inc=sV)
                        g_wo[pp] = sV.n
                    g_warA[1] = [(sP, sP.n)]
                    p.emit("scalar", lambda e, o=dst[:, :, sl], i_=xres: e.dma_start(
                        out=o, in_=i_), waits=[(sV, sV.n)], inc16=cDS[0])
                    g_xres = cDS[0].n

        mamba_done = (sP.n, sA.n, sV.n)
        mbw = [(sP, mamba_done[0]), (sA, mamba_done[1]), (sV, mamba_done[2])]
        dst_w = [(c, c.n) for c in (cDS if depth > 0 else cLNs)]
        ln_w = [(c, c.n) for c in cLNs]

        def writeback(src_hbm):
            with ExitStack() as lc:
                ps_f = [pst(lc, "wb%d" % i, [128, D]) for i in range(2)]
                ft = [sbt(lc, "wb_f%d" % i, [128, 8, 128], F32) for i in range(2)]
                ot = [sbt(lc, "wb_o%d" % i, [128, D], F32) for i in range(2)]
                g_ft = [0, 0]; g_ps = [0, 0]; g_ot = [0, 0]
                for tt in range(32):
                    pr = tt % 2
                    p.emit("scalar", lambda e, o=ft[pr],
                           i_=src_hbm[:, :, tt * 128:(tt + 1) * 128]:
                           e.dma_start(out=o, in_=i_),
                           waits=mbw + dst_w + [(sP, g_ft[pr])], inc16=cWB[pr])
                    ld = cWB[pr].n

                    def tr(e, pr=pr):
                        for dj in range(8):
                            ins = e.transpose(ps_f[pr][:, dj * 128:(dj + 1) * 128],
                                              ft[pr][:, dj, :], id32)
                        return ins
                    p.emit("tensor", tr, waits=mbw + [(cWB[pr], ld), (sV, g_ps[pr])], inc=sP)
                    g_ft[pr] = sP.n
                    p.emit("vector", lambda e, pr=pr: e.tensor_copy(ot[pr], ps_f[pr]),
                           waits=mbw + [(sP, sP.n), (cOU[pr], g_ot[pr])], inc=sV)
                    g_ps[pr] = sV.n
                    p.emit("sync", lambda e, pr=pr, o=out_h[tt * 128:(tt + 1) * 128, :]:
                           e.dma_start(out=o, in_=ot[pr]), waits=[(sV, sV.n)], inc16=cOU[pr])
                    g_ot[pr] = cOU[pr].n

        if dev_out == "xn":
            writeback(xnT_h)
        elif dev_out == "mamba" or not do_ffn:
            wb_src = dsts[depth - 1] if depth > 0 else xnT_h
            if skip_last_p2 and depth > 1:
                wb_src = dsts[depth - 2]
            writeback(wb_src)
        else:
            with ExitStack() as lc:
                rwsb = sbt(lc, "rwsb", [128, 8, 8], F32R)
                ebb = sbt(lc, "ebb", [128, 8], F32)
                wsc = sbt(lc, "wsc", [128, 8], F32)
                wval = sbt(lc, "wval", [128, 8], F32)
                wmsk = sbt(lc, "wmsk", [128, 8], F32)
                wtmp = sbt(lc, "wtmp", [128, 8], F32)
                m12 = sbt(lc, "m12", [128, 4], F32)
                weT = sbt(lc, "weT", [8, L], F32)
                xnq = [sbt(lc, "xnq%d" % i, [128, 8, TF], F32R) for i in range(2)]
                ps_rt = pst(lc, "rt", [128, 512])
                ps_wt = pst(lc, "wt", [128, 512])
                p.emit("gpsimd", lambda e: e.dma_start(
                    out=rwsb, in_=rw_h.rearrange("(dk p) e -> p dk e", p=128)),
                    waits=mbw + dst_w + [(cBW, cBW.n)], inc16=cRW)
                p.emit("sync", lambda e: e.dma_start(out=ebb,
                                                     in_=eb_h.broadcast_to((128, NE))),
                       waits=mbw + dst_w + [(cBW, cBW.n)], inc16=cRW)
                ld0 = cRW.n
                g_xnq = [0, 0]
                g_rt = 0
                g_wt = 0
                g_weT = 0
                for ch in range(8):
                    pr = ch % 2
                    p.emit("gpsimd", lambda e, o=xnq[pr],
                           i_=xnT_h[:, :, ch * TF:(ch + 1) * TF]: e.dma_start(out=o, in_=i_),
                           waits=mbw + ln_w + dst_w + [(sP, g_xnq[pr]), (cBW, cBW.n)],
                           inc16=cXQ[pr])
                    ldq = cXQ[pr].n
                    for ts_ in range(4):
                        tt = ch * 4 + ts_

                        def mmr(e, pr=pr, ts_=ts_):
                            for dk in range(8):
                                ins = e.matmul(ps_rt[:, 0:8],
                                               xnq[pr][:, dk, ts_ * 128:(ts_ + 1) * 128],
                                               rwsb[:, dk, :], start=(dk == 0),
                                               stop=(dk == 7))
                            return ins
                        p.emit("tensor", mmr,
                               waits=mbw + [(cXQ[pr], ldq), (cRW, ld0), (sV, g_rt)],
                               inc=sP)
                        rtn = sP.n

                        p.emit("vector", lambda e: e.reduce_max(
                            m12[:, 0:1], ps_rt[:, 0:8], AX.X),
                            waits=[(sP, rtn)], inc=sV)
                        p.emit("vector", lambda e: e.tensor_scalar(
                            wsc, ps_rt[:, 0:8], m12[:, 0:1], None, OP.subtract),
                            waits=[(sV, sV.n)], inc=sV)
                        g_rt = sV.n
                        p.emit("scalar", lambda e: e.activation(wsc, wsc, AF.Exp),
                               waits=[(sV, sV.n)], inc=sA)

                        for fn_ in (
                            lambda e: e.reduce_sum(m12[:, 1:2], wsc, AX.X),
                            lambda e: e.reciprocal(m12[:, 2:3], m12[:, 1:2]),
                            lambda e: e.tensor_scalar(wsc, wsc, m12[:, 2:3], None, OP.mult),
                            lambda e: e.tensor_tensor(wval, wsc, ebb, OP.add),
                            lambda e: e.reduce_max(m12[:, 0:1], wval, AX.X),
                            lambda e: e.tensor_scalar(wmsk, wval, m12[:, 0:1], None,
                                                      OP.is_ge),
                            lambda e: e.scalar_tensor_tensor(wtmp, wmsk, -1e30, wval,
                                                             OP.mult, OP.add),
                            lambda e: e.reduce_max(m12[:, 1:2], wtmp, AX.X),
                            lambda e: e.tensor_scalar(wtmp, wtmp, m12[:, 1:2], None,
                                                      OP.is_ge),
                            lambda e: e.tensor_tensor(wmsk, wmsk, wtmp, OP.add),
                            lambda e: e.tensor_tensor(wval, wsc, wmsk, OP.mult),
                        ):
                            p.emit("vector", fn_,
                                   waits=[(sA, sA.n), (sP, g_wt), (sV, sV.n)], inc=sV)
                        wdone = sV.n
                        p.emit("tensor", lambda e: e.transpose(ps_wt[0:8, 0:128], wval, id32),
                               waits=[(sV, wdone), (sV, g_weT)], inc=sP)
                        g_wt = sP.n
                        p.emit("vector", lambda e, o=weT[0:8, tt * 128:(tt + 1) * 128]:
                               e.tensor_copy(o, ps_wt[0:8, 0:128]),
                               waits=[(sP, sP.n)], inc=sV)
                        g_weT = sV.n
                    g_xnq[pr] = sP.n
                p.emit("gpsimd", lambda e: e.dma_start(out=weH_h, in_=weT),
                       waits=[(sV, g_weT)], inc16=cWE)
            we_done = cWE.n
            rbw = [(sP, sP.n), (sA, sA.n), (sV, sV.n)]

            with ExitStack() as fc:
                xnq16 = sbt(fc, "xnq16", [128, 8, LQ], BF16)
                acc = sbt(fc, "acc", [128, 8, LQ], F32)
                s1t = [sbt(fc, "s1t%d" % i, [128, TF], F32) for i in range(2)]
                t1t = [sbt(fc, "t1t%d" % i, [128, TF], F32) for i in range(2)]
                gbf = sbt(fc, "gbf", [128, 8, TF], BF16)
                web = [sbt(fc, "web%d" % i, [128, TF], F32) for i in range(2)]
                for q in range(NQ):
                    qsl = slice(q * LQ, (q + 1) * LQ)
                    with ExitStack() as wc:
                        ps_h1 = [pst(wc, "fh1%d_q%d" % (i, q), [128, 512]) for i in range(2)]
                        ps_h3 = [pst(wc, "fh3%d_q%d" % (i, q), [128, 512]) for i in range(2)]
                        ps_o = [pst(wc, "fo%d_q%d" % (i, q), [128, 512]) for i in range(2)]
                        w1sb = [sbt(wc, "w1sb%d_q%d" % (i, q), [128, 8, D], BF16) for i in range(2)]
                        w3sb = [sbt(wc, "w3sb%d_q%d" % (i, q), [128, 8, D], BF16) for i in range(2)]
                        w2sb = [sbt(wc, "w2sb%d_q%d" % (i, q), [128, 8, D], BF16) for i in range(2)]
                        p.emit("gpsimd", lambda e, o=xnq16, i_=xnT_h[:, :, qsl]:
                               e.dma_start(out=o, in_=i_),
                               waits=rbw + ln_w + [(sP, sP.n), (cWE, we_done)],
                               inc16=cXN)
                        ldxn = cXN.n
                        p.emit("vector", lambda e: e.memset(fl(acc), 0.0),
                               waits=rbw + [(sV, sV.n), (sP, sP.n), (cWE, we_done)],
                               inc=sV)
                        g_w = [0, 0]
                        g_web = [0, 0]
                        g_h1 = [0, 0]
                        g_h3 = [0, 0]
                        g_po = [0, 0]
                        g_s1 = [0, 0]
                        g_gbf = 0
                        ldw = [0, 0]
                        for e_ in range(9):
                            wpr = e_ % 2
                            if e_ < 8:
                                srcs_w = (w1_h[e_], w3_h[e_], w2_h[e_])
                            else:
                                srcs_w = (sw1_h, sw3_h, sw2_h)
                            for wsrc, wdst in zip(srcs_w, (w1sb[wpr], w3sb[wpr], w2sb[wpr])):
                                p.emit("gpsimd", lambda e, o=wdst,
                                       i_=wsrc.rearrange("(dk p) h -> p dk h", p=128):
                                       e.dma_start(out=o, in_=i_),
                                       waits=rbw + [(sP, g_w[wpr]), (cWE, we_done),
                                                    (cOU[0], cOU[0].n),
                                                    (cOU[1], cOU[1].n),
                                                    (cX5, cX5.n)],
                                       inc16=cFW[wpr])
                            ldw[wpr] = cFW[wpr].n
                            for c2 in range(2):
                                csl = slice(c2 * TF, (c2 + 1) * TF)
                                tok0 = q * LQ + c2 * TF
                                if e_ < 8:
                                    p.emit("gpsimd", lambda e, o=web[c2], e_=e_,
                                           r=weH_h[e_:e_ + 1, tok0:tok0 + TF]:
                                           e.dma_start(out=o, in_=r.broadcast_to((128, TF))),
                                           waits=[(cWE, we_done), (sV, g_web[c2])],
                                           inc16=cWB[c2])
                                ldwe = cWB[c2].n
                                for hj in range(8):
                                    hp = hj % 2

                                    def mmh(e, wpr=wpr, hj=hj, hp=hp, cs=csl):
                                        for dk in range(8):
                                            e.matmul(ps_h1[hp][:, 0:TF],
                                                     w1sb[wpr][:, dk, hj * 128:(hj + 1) * 128],
                                                     xnq16[:, dk, cs],
                                                     start=(dk == 0), stop=(dk == 7))
                                        for dk in range(8):
                                            ins = e.matmul(
                                                ps_h3[hp][:, 0:TF],
                                                w3sb[wpr][:, dk, hj * 128:(hj + 1) * 128],
                                                xnq16[:, dk, cs],
                                                start=(dk == 0), stop=(dk == 7))
                                        return ins
                                    p.emit("tensor", mmh,
                                           waits=[(cFW[wpr], ldw[wpr]), (cXN, ldxn),
                                                  (sA, g_h1[hp]), (sV, g_h3[hp])], inc=sP)
                                    hn = sP.n
                                    p.emit("scalar", lambda e, hp=hp, sp=s1t[hj % 2]:
                                           e.activation(sp, ps_h1[hp][:, 0:TF], AF.Silu),
                                           waits=[(sP, hn), (sV, g_s1[hj % 2])], inc=sA)
                                    g_h1[hp] = sA.n

                                    p.emit("vector", lambda e, hj=hj, hp=hp:
                                           e.tensor_tensor(t1t[hj % 2], s1t[hj % 2],
                                                           ps_h3[hp][:, 0:TF], OP.mult),
                                           waits=[(sA, sA.n), (sP, g_gbf)], inc=sV)

                                    def gat(e, hj=hj, e_=e_, c2=c2):
                                        tp = t1t[hj % 2]
                                        if e_ < 8:
                                            ins = e.tensor_tensor(gbf[:, hj, :], tp,
                                                                  web[c2], OP.mult)
                                        else:
                                            ins = e.tensor_copy(gbf[:, hj, :], tp)
                                        return ins
                                    p.emit("vector", gat,
                                           waits=[(cWB[c2], ldwe), (sV, sV.n)], inc=sV)
                                    g_h3[hp] = sV.n
                                    g_s1[hj % 2] = sV.n
                                g_done = sV.n
                                g_web[c2] = sV.n
                                for ob in range(8):
                                    po = ob % 2

                                    def mmo(e, wpr=wpr, ob=ob, po=po):
                                        for hk in range(8):
                                            ins = e.matmul(
                                                ps_o[po][:, 0:TF],
                                                w2sb[wpr][:, hk, ob * 128:(ob + 1) * 128],
                                                gbf[:, hk, :],
                                                start=(hk == 0), stop=(hk == 7))
                                        return ins
                                    p.emit("tensor", mmo,
                                           waits=[(sV, g_done), (sV, g_po[po])], inc=sP)
                                    on = sP.n
                                    p.emit("vector", lambda e, ob=ob, po=po, cs=csl:
                                           e.tensor_tensor(acc[:, ob, cs], acc[:, ob, cs],
                                                           ps_o[po][:, 0:TF], OP.add),
                                           waits=[(sP, on)], inc=sV)
                                    g_po[po] = sV.n
                                g_gbf = sP.n
                            g_w[wpr] = sP.n
                        experts_done = (sP.n, sV.n)
                    with ExitStack() as gc2:
                        gq = sbt(gc2, "gq_q%d" % q, [128, 8, LQ], F32R)
                        x5q = sbt(gc2, "x5q_q%d" % q, [128, 8, LQ], F32)
                        x0t = [sbt(gc2, "x0t%d_q%d" % (i, q), [128, D], F32) for i in range(2)]
                        ott = [sbt(gc2, "ott%d_q%d" % (i, q), [128, D], F32) for i in range(2)]
                        ps_f = [pst(gc2, "pf%d_q%d" % (i, q), [128, D]) for i in range(2)]
                        p.emit("scalar", lambda e, o=x5q, i_=xT0_h[:, :, qsl]: e.dma_start(
                            out=o, in_=i_),
                            waits=dst_w + [(sP, experts_done[0])], inc16=cX5)
                        ldx5 = cX5.n
                        p.emit("scalar", lambda e: e.activation(fl(gq), fl(acc), AF.Silu),
                               waits=[(sV, experts_done[1]), (sP, experts_done[0])], inc=sA)
                        p.emit("vector", lambda e: e.tensor_tensor(fl(acc), fl(x5q), fl(gq),
                                                                   OP.mult),
                               waits=[(sA, sA.n), (cX5, ldx5)], inc=sV)
                        sg_done = sV.n
                        g_ft = [0, 0]; g_psf = [0, 0]; g_ot = [0, 0]
                        for tt in range(8):
                            pr = tt % 2
                            row = q * LQ + tt * 128

                            def trf(e, pr=pr, tt=tt):
                                for dj in range(8):
                                    ins = e.transpose(
                                        ps_f[pr][:, dj * 128:(dj + 1) * 128],
                                        acc[:, dj, tt * 128:(tt + 1) * 128], id32)
                                return ins
                            p.emit("tensor", trf, waits=[(sV, sg_done), (sV, g_psf[pr])],
                                   inc=sP)
                            trn = sP.n
                            p.emit("scalar", lambda e, o=x0t[pr],
                                   i_=x_in[row:row + 128, :]: e.dma_start(out=o, in_=i_),
                                   waits=[(sV, g_ft[pr])], inc16=cX0[pr])
                            ldx0 = cX0[pr].n
                            p.emit("vector", lambda e, pr=pr: e.tensor_tensor(
                                ott[pr], x0t[pr], ps_f[pr], OP.add),
                                waits=[(sP, trn), (cX0[pr], ldx0), (cOU[pr], g_ot[pr])],
                                inc=sV)
                            g_psf[pr] = sV.n
                            g_ft[pr] = sV.n
                            p.emit("sync", lambda e, pr=pr, o=out_h[row:row + 128, :]:
                                   e.dma_start(out=o, in_=ott[pr]), waits=[(sV, sV.n)],
                                   inc16=cOU[pr])
                            g_ot[pr] = cOU[pr].n

        p.emit("sync", lambda e: e.nop(),
               waits=[(c, c.n) for c in all_dma_ctrs])

        block = ctx.enter_context(nc.Block())
        p.build(block)
    return nc


_NC_CACHE = {}


def _get_nc(key):
    if key not in _NC_CACHE:
        _NC_CACHE[key] = build(*key)
    return _NC_CACHE[key]


def kernel(L0_attn_x, L0_attn_W_in, L0_attn_conv_w, L0_attn_conv_b, L0_attn_W_x,
           L0_attn_W_dt, L0_attn_b_dt, L0_attn_A_log, L0_attn_D, L0_attn_W_out,
           L0_ffn_router_w, L0_ffn_expert_bias, L0_ffn_w1, L0_ffn_w2, L0_ffn_w3,
           L0_ffn_sw1, L0_ffn_sw2, L0_ffn_sw3,
           _depth=NL, _do_ffn=True, _dev_out=None, _cores=8, _dbg_spill=False,
           _ret=None):
    nc = _get_nc((_depth, _do_ffn, _dev_out, _dbg_spill))
    common = {
        "W_in": np.ascontiguousarray(L0_attn_W_in),
        "conv_w": np.ascontiguousarray(L0_attn_conv_w),
        "conv_b": np.ascontiguousarray(L0_attn_conv_b),
        "W_x": np.ascontiguousarray(L0_attn_W_x),
        "W_dt": np.ascontiguousarray(L0_attn_W_dt),
        "b_dt": np.ascontiguousarray(L0_attn_b_dt),
        "A_log": np.ascontiguousarray(L0_attn_A_log),
        "Dvec": np.ascontiguousarray(L0_attn_D),
        "W_out": np.ascontiguousarray(L0_attn_W_out),
        "rw": np.ascontiguousarray(L0_ffn_router_w),
        "ebias": np.ascontiguousarray(L0_ffn_expert_bias.reshape(1, NE)),
        "w1": np.ascontiguousarray(L0_ffn_w1),
        "w2": np.ascontiguousarray(L0_ffn_w2),
        "w3": np.ascontiguousarray(L0_ffn_w3),
        "sw1": np.ascontiguousarray(L0_ffn_sw1),
        "sw2": np.ascontiguousarray(L0_ffn_sw2),
        "sw3": np.ascontiguousarray(L0_ffn_sw3),
        "ident": np.eye(128, dtype=np.float32),
    }
    in_maps = []
    for b in range(_cores):
        m = dict(common)
        m["x"] = np.ascontiguousarray(L0_attn_x[b])
        in_maps.append(m)
    res = run_bass_kernel_spmd(nc, in_maps, core_ids=list(range(_cores)))
    if _ret is not None:
        return res
    return np.stack([res.results[b]["out"] for b in range(_cores)], axis=0)
